# version 06
# baseline (speedup 1.0000x reference)
import math
import numpy as np
import ml_dtypes
import concourse.bass as bass
import concourse.mybir as mybir

F32 = mybir.dt.float32
BF16 = mybir.dt.bfloat16
I32 = mybir.dt.int32
A = mybir.AluOpType
AX = mybir.AxisListType.X
ACT = mybir.ActivationFunctionType
BF = ml_dtypes.bfloat16

L = 8192
DM = 4096
FS = 512
HD = 256
NT = 64
NMAC = 32
NLT = 16
MCN = 64
KCH = 32
EPS = 1e-5
CLIP1 = 1.4999
SHIFT = 256.5
ROLES = ("sync", "vector", "scalar", "tensor", "gpsimd")
KB = 1024


def _build_tables():
    g = [1.0, 1.0, 1.0]
    n = 8
    masks = sorted(range(n), key=lambda m: (bin(m).count("1"), m))
    inv = {m: i for i, m in enumerate(masks)}
    C = np.zeros((n, n, n), dtype=np.float32)
    for i, a in enumerate(masks):
        for j, b in enumerate(masks):
            sw, at_ = (0, a >> 1)
            while at_:
                sw += bin(at_ & b).count("1")
                at_ >>= 1
            s = -1.0 if sw & 1 else 1.0
            for t in range(3):
                if a >> t & 1 and b >> t & 1:
                    s *= float(g[t])
            C[i, j, inv[a ^ b]] = s
    grades = np.array([bin(m).count("1") for m in masks])
    rev = np.where(grades * (grades - 1) // 2 % 2 == 1, -1.0, 1.0).astype(np.float32)
    met = np.array([float(np.prod([g[t] for t in range(3) if m >> t & 1] + [1.0]))
                    for m in masks], dtype=np.float32)
    return C, rev, met


def host_constants():
    C, rev, met = _build_tables()
    biv = C[:, :, [4, 5, 6]]
    scale = 1.0 / math.sqrt(HD)
    bivf = np.ascontiguousarray(biv.transpose(2, 0, 1).reshape(3, 64)).astype(np.float32)
    rconst = np.tile(np.repeat(rev, 8)[None, :], (2, 1)).astype(np.float32)
    dconst = np.tile((scale * np.diag(rev * met)).reshape(1, 64), (2, 1)).astype(np.float32)
    return dict(
        bivf=bivf, rconst=rconst, dconst=dconst,
        iden=np.eye(128).astype(BF),
        onesf=np.ones((1, 128), np.float32),
        onesb=np.ones((128, 1)).astype(BF),
        onesrb=np.ones((1, 128)).astype(BF),
        onescf=np.ones((128, 1), np.float32),
    )


def build(dev=False):
    nc = bass.Bass("TRN2", target_bir_lowering=False, debug=False)

    d_x = nc.dram_tensor("x", [L, DM], F32, kind="ExternalInput").ap()
    d_xsl = nc.dram_tensor("xsl", [L, FS], F32, kind="ExternalInput").ap()
    d_wq = nc.dram_tensor("wq", [FS, DM], F32, kind="ExternalInput").ap()
    d_wk = nc.dram_tensor("wk", [FS, DM], F32, kind="ExternalInput").ap()
    d_wv = nc.dram_tensor("wv", [FS, DM], F32, kind="ExternalInput").ap()
    d_wf = nc.dram_tensor("wf", [DM, DM], F32, kind="ExternalInput").ap()
    d_wfs = nc.dram_tensor("wfs", [FS, DM], F32, kind="ExternalInput").ap()
    d_bq = nc.dram_tensor("bq", [1, FS], F32, kind="ExternalInput").ap()
    d_bk = nc.dram_tensor("bk", [1, FS], F32, kind="ExternalInput").ap()
    d_bv = nc.dram_tensor("bv", [1, FS], F32, kind="ExternalInput").ap()
    d_bwT = nc.dram_tensor("bwT", [3, 2], F32, kind="ExternalInput").ap()
    d_bivf = nc.dram_tensor("bivf", [3, 64], F32, kind="ExternalInput").ap()
    d_rc = nc.dram_tensor("rconst", [2, 64], F32, kind="ExternalInput").ap()
    d_dc = nc.dram_tensor("dconst", [2, 64], F32, kind="ExternalInput").ap()
    d_id = nc.dram_tensor("iden", [128, 128], BF16, kind="ExternalInput").ap()
    d_onesf = nc.dram_tensor("onesf", [1, 128], F32, kind="ExternalInput").ap()
    d_onesb = nc.dram_tensor("onesb", [128, 1], BF16, kind="ExternalInput").ap()
    d_onesrb = nc.dram_tensor("onesrb", [1, 128], BF16, kind="ExternalInput").ap()
    d_onescf = nc.dram_tensor("onescf", [128, 1], F32, kind="ExternalInput").ap()
    d_out = nc.dram_tensor("out", [L, FS], F32, kind="ExternalOutput").ap()

    skind = "ExternalOutput" if dev else "Internal"
    d_kscr = nc.dram_tensor("k_scr", [FS, L], BF16, kind=skind).ap()
    d_vscr = nc.dram_tensor("v_scr", [L, FS], BF16, kind=skind).ap()
    d_qscr = nc.dram_tensor("q_scr", [FS, L], BF16, kind=skind).ap()
    d_xiscr = nc.dram_tensor("xi_scr", [L, DM], BF16, kind=skind).ap()
    d_wiscr = nc.dram_tensor("wi_scr", [128, KCH * FS], BF16, kind=skind).ap()
    d_ascr = nc.dram_tensor("a_scr", [L, HD], BF16, kind=skind).ap()
    if dev:
        d_dbg = nc.dram_tensor("dbg", [128, 80], F32, kind="ExternalOutput").ap()

    ARENA_PP = 203 * KB
    arena = nc.sbuf_tensor("arena", [128, ARENA_PP], mybir.dt.uint8).__enter__()
    base = nc.lookup_mloc(arena).addr

    def at(name, shape, dtype, off):
        end = off + int(np.prod(shape[1:])) * mybir.dt.size(dtype)
        assert end <= ARENA_PP, (name, end)
        return nc.alloc_sbuf_tensor_at(name, shape, dtype, offset=base + off)

    wqT = at("wqT", [128, KCH, FS], BF16, 0)
    wkT = at("wkT", [128, KCH, FS], BF16, 32 * KB)
    wvT = at("wvT", [128, KCH, FS], BF16, 64 * KB)
    xst = [at(f"xst{i}", [128, DM], F32, 96 * KB + 16 * KB * i) for i in range(2)]
    xn_tok = [at("xntok", [128, DM], BF16, 128 * KB)] * 2
    xn_fm2 = [at(f"xnfm{i}", [128, KCH, 256], BF16, 136 * KB + 16 * KB * i) for i in range(2)]
    xi_tok = [at("xitok", [128, DM], BF16, 168 * KB)] * 2
    RS = 176 * KB
    r_w = [at(f"rw{i}", [128, 512], F32, RS + 2 * KB * i) for i in range(2)]
    r_wm = [at(f"rwm{i}", [128, 512], F32, RS + 2 * KB * (2 + i)) for i in range(2)]
    r_iw = [at(f"riw{i}", [128, 512], I32, RS + 2 * KB * (4 + i)) for i in range(2)]
    r_r1 = [at(f"rr1{i}", [128, 512], F32, RS + 2 * KB * (6 + i)) for i in range(2)]
    r_cu = [at(f"rcu{i}", [128, 512], F32, RS + 2 * KB * (8 + i)) for i in range(2)]
    k_sb = at("ksb", [128, 4, 256], BF16, 196 * KB)
    v_sb = at("vsb", [128, 2, FS], BF16, 198 * KB)
    q_sb = at("qsb", [128, 4, 256], BF16, 200 * KB)
    sq_sb = at("sqsb", [128, 2, 256], BF16, 202 * KB)
    wstage = [xst[0], xst[1]]
    wcast = at("wcast", [128, DM], BF16, 128 * KB)
    wcast2 = at("wcast2", [128, DM], BF16, 136 * KB)
    wpre = [at(f"wpre{i}", [128, 512], F32, 144 * KB + 2 * KB * i) for i in range(2)]
    wiT_b = at("wiTb", [128, KCH, FS], BF16, 0)
    wiT = at("wiT", [128, KCH, FS], BF16, 0)
    K_res = at("Kres", [128, 2, L], BF16, 32 * KB)
    V_res = at("Vres", [128, MCN, HD + 1], BF16, 64 * KB)
    xi2_tok = [at(f"xi2t{i}", [128, DM], BF16, 98 * KB + 8 * KB * i) for i in range(2)]
    xi2_fm = at("xi2fm", [128, KCH, FS], BF16, 114 * KB)
    q2_sb = [at(f"q2sb{i}", [128, 2, FS], BF16, 146 * KB + 2 * KB * i) for i in range(2)]
    expS = [at(f"expS{i}", [128, 512], BF16, 150 * KB + KB * i) for i in range(3)]
    attn_f = [at(f"attnf{i}", [128, FS], BF16, 166 * KB + KB * i) for i in range(4)]
    a0_sb = [at(f"a0sb{i}", [128, HD], BF16, 170 * KB + 512 * i) for i in range(4)]
    xsl_sb = [at(f"xslb{i}", [128, FS], F32, 172 * KB + 2 * KB * i) for i in range(4)]
    ffn_sb = [at(f"ffnsb{i}", [128, FS], BF16, 180 * KB + KB * i) for i in range(4)]
    tadd_sb = [at(f"tadd{i}", [128, FS], F32, 184 * KB + 2 * KB * i) for i in range(4)]
    out_sb = [at(f"outsb{i}", [128, FS], F32, 192 * KB + 2 * KB * i) for i in range(4)]

    import contextlib
    es = contextlib.ExitStack()
    sb = lambda n, s, d: es.enter_context(nc.sbuf_tensor(n, s, d))
    c_id = sb("c_id", [128, 128], BF16)
    c_onesf = sb("c_onesf", [1, 128], F32)
    c_onesb = sb("c_onesb", [128, 1], BF16)
    c_onesrb = sb("c_onesrb", [1, 128], BF16)
    b_v_bf = sb("b_v_bf", [1, FS], BF16)
    c_onescf = sb("c_onescf", [128, 1], F32)
    c_bivf = at("c_bivf", [3, 64], F32, RS + 6 * KB)
    c_rc = at("c_rc", [2, 64], F32, RS + 6 * KB + 256)
    c_dc = at("c_dc", [2, 64], F32, RS + 6 * KB + 512)
    c_bwT = at("c_bwT", [3, 2], F32, RS + 6 * KB + 768)
    c_T = sb("c_T", [128, 256], BF16)
    c_A = at("c_A", [2, 64], F32, RS + 7 * KB)
    c_Abf = at("c_Abf", [2, 64], BF16, RS + 7 * KB + 256)
    b_q = at("b_q", [1, FS], F32, RS)
    b_k = at("b_k", [1, FS], F32, RS + 2 * KB)
    b_v = at("b_v", [1, FS], F32, RS + 4 * KB)
    bkT = sb("bkT", [128, 4], F32)
    bqT = sb("bqT", [128, 4], F32)
    bqTb = sb("bqTb", [128, 4], BF16)
    bqe = sb("bqe", [128, 4], F32)
    st_all = sb("st_all", [128, 84], F32)
    _st = [0]

    def _stc(n):
        a, _st[0] = _st[0], _st[0] + n
        return st_all[:, a:a + n]

    st_bn = _stc(48)
    st_mv = _stc(4)
    st_varc = _stc(2)
    st_rstd = _stc(2)
    st_nrstd = _stc(2)
    st_nmrs = _stc(2)
    st_mrs = _stc(2)
    st_xmax = _stc(2)
    st_xmin = _stc(2)
    st_a = _stc(2)
    st_b = _stc(2)
    st_amax = _stc(2)
    st_sx = _stc(2)
    st_rx = _stc(2)
    st_rr = _stc(2)
    st_bb = _stc(2)
    st_bb1 = _stc(2)
    c_2565 = sb("c_2565", [128, 1], F32)
    c_eps = sb("c_eps", [128, 1], F32)
    sxw = sb("sxw", [128, NT], F32)
    s_wcol = sb("s_wcol", [128, 1], F32)
    r_wcol = sb("r_wcol", [128, 1], F32)
    wfsum = sb("wfsum", [128, 1], F32)
    wfpart = sb("wfpart", [128, 1], F32)
    sm_all = sb("sm_all", [1, 16], F32)
    s_sw = sm_all[0:1, 0:1]
    s_rw = sm_all[0:1, 1:2]
    qnmax = sm_all[0:1, 2:3]
    knmax = sm_all[0:1, 3:4]
    nrm_red = sm_all[0:1, 4:6]
    cc_q = sm_all[0:1, 6:7]
    cc_k = sm_all[0:1, 7:8]
    cc_c = sm_all[0:1, 8:9]
    cc_c2 = sm_all[0:1, 9:10]
    eb_exp = sb("eb_exp", [128, 1], F32)
    rec_sb = sb("rec_sb", [128, 1], F32)

    pb_f = [es.enter_context(nc.psum_tensor(f"pbf{i}", [128, 512], F32)) for i in range(6)]
    pb_t = [es.enter_context(nc.psum_tensor(f"pbt{i}", [128, 1024], BF16)) for i in range(2)]
    ps_pj = pb_f[0:2]
    ps_o = pb_f[2:6]
    ps_nm = pb_f[2]
    ps_m0 = pb_f[3]

    sem_names = [
        "BAR", "C0", "T0", "V0", "S0", "G0", "W0", "W1",
        "LDX", "LDXB", "W0B", "W1B", "DMOB", "XI2B", "A2B", "XSLB", "QLDB", "XFR", "XFRB", "VS", "GS", "SSM", "SXN",
        "VW", "SC", "GU", "VD", "GT", "VX", "DMO",
        "TPS", "TEV", "PRJ", "PBK", "PEV", "SQN", "TNM", "NRD", "KVQ",
        "KLD", "QLD", "QF", "SR", "ER", "EF", "OF",
        "A2", "XSL", "XI2", "TXF", "FPB", "FFV", "VOUT", "OUT", "ASD", "ASDB",
        "CB1", "CB2", "CB3", "VQS", "SSQ", "VRS", "GE1", "VCC", "VSF", "SSF", "GSF", "VC1", "VT",
    ]
    S = {n: es.enter_context(nc.semaphore(n)) for n in sem_names}

    plan = {r: [] for r in ROLES}
    cnt = {n: 0 for n in sem_names}

    def op(role, fn, sem=None, n=1):
        if sem is not None:
            cnt[sem] += n
            plan[role].append(lambda eng, fn=fn, s=S[sem], m=n: fn(eng).then_inc(s, m))
        else:
            plan[role].append(lambda eng, fn=fn: fn(eng))

    def dma(role, out, in_, sem=None, slow=False):
        op(role, lambda e, o=out, i=in_, sl=slow: e.dma_start(
            out=o, in_=i, allow_slow_non_contiguous=sl), sem=sem, n=16)

    deferred_waits = []

    def wait(role, sem, val):
        deferred_waits.append((role, sem, val))
        if val > 0:
            plan[role].append(lambda eng, s=S[sem], v=val: eng.wait_ge(s, v))

    def wait_cur(role, sem):
        wait(role, sem, cnt[sem])

    def drain(role, sem=None, n=1):
        op(role, lambda e: e.drain(), sem=sem, n=n)

    SELF = {"vector": "VSF", "scalar": "SSF", "gpsimd": "GSF"}

    def selfwait(role):
        wait(role, SELF[role], cnt[SELF[role]])

    def chop(role, fn, sem=None):
        selfwait(role)
        op(role, fn, sem=(sem if sem is not None else SELF[role]))

    nbar = [0]

    def flush(*sems, role="sync"):
        for s_ in sems:
            wait_cur(role, s_)

    def barrier():
        nbar[0] += 1
        k = nbar[0] * len(ROLES)
        cnt["BAR"] += len(ROLES)
        for r in ROLES:
            def f(eng, v=k):
                eng.drain().then_inc(S["BAR"], 1)
                eng.wait_ge(S["BAR"], v)
            plan[r].append(f)

    for dst, src in [(c_id, d_id), (c_onesf, d_onesf), (c_onesb, d_onesb),
                     (c_onescf, d_onescf), (c_bivf, d_bivf), (c_rc, d_rc),
                     (c_dc, d_dc), (c_bwT, d_bwT), (b_q, d_bq), (b_k, d_bk),
                     (b_v, d_bv), (c_onesrb, d_onesrb)]:
        dma("sync", dst[:], src, sem="C0")
    op("vector", lambda e: e.memset(c_2565[:], SHIFT))
    op("vector", lambda e: e.memset(c_eps[:], EPS))
    op("vector", lambda e: e.memset(qnmax, 0.0))
    op("vector", lambda e: e.memset(knmax, 0.0))
    op("vector", lambda e: e.memset(wfsum[:], 0.0))
    op("vector", lambda e: e.memset(c_T[:], 0.0))
    flush("C0")
    barrier()

    for f in range(4):
        dma("sync", bkT[:, f:f + 1], b_k[0:1, f * 128:(f + 1) * 128], sem="C0")
        dma("sync", bqT[:, f:f + 1], b_q[0:1, f * 128:(f + 1) * 128], sem="C0")
    op("tensor", lambda e: e.matmul(ps_m0[0:2, 0:64], c_bwT[:], c_bivf[:],
                                    start=True, stop=True), sem="T0")
    wait_cur("vector", "T0")
    op("vector", lambda e: e.tensor_mul(out=c_A[:], in0=ps_m0[0:2, 0:64], in1=c_rc[:]))
    drain("vector")
    op("vector", lambda e: e.tensor_add(out=c_A[:], in0=c_A[:], in1=c_dc[:]))
    drain("vector")
    op("vector", lambda e: e.tensor_copy(out=c_Abf[:], in_=c_A[:]))
    wait_cur("vector", "C0")
    op("vector", lambda e: e.tensor_copy(out=bqTb[:], in_=bqT[:]))
    drain("vector", sem="V0")
    wait_cur("sync", "V0")
    for hh in range(2):
        for dd in range(16):
            dma("sync", c_T[8 * dd:8 * dd + 8, hh * 128 + 8 * dd:hh * 128 + 8 * dd + 8],
                c_Abf[hh:hh + 1, 0:64], sem="C0")
    op("vector", lambda e: e.tensor_copy(out=b_v_bf[:], in_=b_v[:]))
    drain("vector", sem="V0")
    wait_cur("tensor", "C0")
    wait_cur("tensor", "V0")
    for j in range(4):
        if j >= 2:
            wait("tensor", "S0", cnt["S0"] - 1)
        op("tensor", lambda e, j=j: e.matmul(
            ps_pj[j % 2][:, 0:1], c_T[:, (j // 2) * 128:(j // 2) * 128 + 128],
            bqTb[:, j:j + 1], start=True, stop=True), sem="T0")
        wait_cur("scalar", "T0")
        op("scalar", lambda e, j=j: e.copy(out=bqe[:, j:j + 1], in_=ps_pj[j % 2][:, 0:1]))
        drain("scalar", sem="S0")
    flush("C0")
    barrier()

    def transp32(src_bf, dst, rcslice, tag_rc):
        for g in range(4):
            bnk = g % 2
            wait("tensor", "TEV", cnt["TEV"] - 1)
            for j in range(8):
                f = g * 8 + j
                op("tensor", lambda e, f=f, bnk=bnk, j=j, s=src_bf: e.matmul(
                    pb_t[bnk][:, j * 128:(j + 1) * 128], s[:, f * 128:(f + 1) * 128],
                    c_id[:], start=(j == 0), stop=(j == 7), is_transpose=True),
                    sem=("TPS" if j == 7 else None))
            wait_cur("vector", "TPS")
            op("vector", lambda e, g=g, bnk=bnk, d=dst, rcs=rcslice: e.tensor_copy(
                out=d[:, g * 8:(g + 1) * 8, rcs], in_=pb_t[bnk][:]), sem="TEV")

    def w_prep_plain(dsrc, wT):
        for rc in range(4):
            par = rc % 2
            sw = "W0" if par == 0 else "W0B"
            wait("sync", "V0", cnt["V0"] - 1)
            dma("sync", wstage[par][:], dsrc[rc * 128:(rc + 1) * 128, :], sem=sw)
            wait_cur("vector", sw)
            wait("vector", "TEV", cnt["TEV"])
            op("vector", lambda e, par=par: e.tensor_copy(out=wcast[:], in_=wstage[par][:]))
            drain("vector", sem="V0")
            wait_cur("tensor", "V0")
            transp32(wcast, wT, slice(rc * 128, (rc + 1) * 128), rc)

    w_prep_plain(d_wk, wkT)
    w_prep_plain(d_wv, wvT)
    flush("W0", "W0B")
    barrier()

    for ch in range(32):
        par = ch % 2
        sw = "W0" if par == 0 else "W0B"
        wait("sync", "V0", cnt["V0"] - 1)
        dma("sync", wstage[par][:], d_wf[ch * 128:(ch + 1) * 128, :], sem=sw)
        wait("vector", sw, cnt[sw])
        wait("vector", "V0", cnt["V0"])
        op("vector", lambda e, par=par: e.tensor_reduce(
            out=wfpart[:], in_=wstage[par][:], axis=AX, op=A.add,
            apply_absolute_value=True), sem="VSF")
        selfwait("vector")
        op("vector", lambda e: e.tensor_add(out=wfsum[:], in0=wfsum[:], in1=wfpart[:]),
            sem="V0")
    wait_cur("tensor", "V0")
    op("tensor", lambda e: e.matmul(ps_m0[0:1, 0:1], c_onescf[:], wfsum[:],
                                    start=True, stop=True), sem="T0")
    wait_cur("vector", "T0")
    op("vector", lambda e: e.tensor_scalar_mul(out=s_sw, in0=ps_m0[0:1, 0:1],
                                               scalar1=1.0 / (DM * DM)))
    drain("vector")
    op("vector", lambda e: e.tensor_scalar_add(out=s_rw, in0=s_sw, scalar1=1e-8))
    drain("vector")
    op("vector", lambda e: e.reciprocal(out=s_rw, in_=s_rw))
    drain("vector", sem="V0")
    wait_cur("tensor", "V0")
    op("tensor", lambda e: e.matmul(ps_m0[:, 0:1], c_onesf[:], s_sw,
                                    start=True, stop=True))
    op("tensor", lambda e: e.matmul(pb_f[4][:, 0:1], c_onesf[:], s_rw,
                                    start=True, stop=True), sem="T0")
    wait_cur("scalar", "T0")
    op("scalar", lambda e: e.copy(out=s_wcol[:], in_=ps_m0[:, 0:1]))
    op("scalar", lambda e: e.copy(out=r_wcol[:], in_=pb_f[4][:, 0:1]))
    drain("scalar", sem="S0")
    flush("W0", "W0B")
    barrier()

    ridx = [0]

    def round_chunk(make_w, dst_ap):
        i = ridx[0]
        ridx[0] += 1
        s_ = i % 2
        wait("vector", "SC", i - 1)
        wait("vector", "GE1", i - 1)
        op("vector", make_w, sem="VW")
        wait("scalar", "VW", i + 1)
        wait("scalar", "GE1", i - 1)
        wait("scalar", "VX", i - 1)
        op("scalar", lambda e, s_=s_: e.copy(out=r_iw[s_][:], in_=r_w[s_][:]), sem="SSF")
        selfwait("scalar")
        op("scalar", lambda e, s_=s_: e.copy(out=r_r1[s_][:], in_=r_iw[s_][:]), sem="SC")
        wait("gpsimd", "SC", i + 1)
        wait("gpsimd", "VX", i - 1)
        op("gpsimd", lambda e, s_=s_: e.tensor_tensor(
            out=r_wm[s_][:], in0=r_w[s_][:], in1=r_r1[s_][:], op=A.subtract), sem="GE1")
        wait("vector", "GE1", i + 1)
        op("vector", lambda e, s_=s_: e.tensor_single_scalar(
            out=r_w[s_][:], in_=r_wm[s_][:], scalar=0.0, op=A.is_lt), sem="VC1")
        op("vector", lambda e, s_=s_: e.tensor_single_scalar(
            out=r_cu[s_][:], in_=r_wm[s_][:], scalar=1.0, op=A.is_ge), sem="VC1")
        wait("vector", "VC1", 2 * (i + 1))
        op("vector", lambda e, s_=s_: e.tensor_tensor(
            out=r_wm[s_][:], in0=r_cu[s_][:], in1=r_w[s_][:], op=A.subtract), sem="VT")
        wait("vector", "VT", i + 1)
        op("vector", lambda e, s_=s_, d=dst_ap: e.scalar_tensor_tensor(
            out=d, in0=r_r1[s_][:], scalar=-256.0, in1=r_wm[s_][:],
            op0=A.add, op1=A.add), sem="VX")

    for rc in range(4):
        par = rc % 2
        sw = "W1" if par == 0 else "W1B"
        if rc >= 2:
            wait("sync", "VX", (rc - 1) * 8)
        dma("sync", wstage[par][:], d_wfs[rc * 128:(rc + 1) * 128, :], sem=sw)
        wait("vector", sw, cnt[sw])
        wait("vector", "TEV", cnt["TEV"])
        for c in range(8):
            s_ = ridx[0] % 2
            cs = slice(c * 512, (c + 1) * 512)
            op("vector", lambda e, par=par, cs=cs, s_=s_: e.tensor_scalar(
                out=wpre[s_][:], in0=wstage[par][:, cs], scalar1=r_wcol[:],
                scalar2=CLIP1, op0=A.mult, op1=A.min), sem="VSF")
            selfwait("vector")
            round_chunk(
                lambda e, s_=s_: e.tensor_scalar(
                    out=r_w[s_][:], in0=wpre[s_][:], scalar1=-CLIP1, scalar2=SHIFT,
                    op0=A.max, op1=A.add),
                wcast[:, cs],
            )
        wait_cur("tensor", "VX")
        transp32(wcast, wiT_b, slice(rc * 128, (rc + 1) * 128), rc)
    wait_cur("sync", "TEV")
    dma("sync", d_wiscr.rearrange("p (a b) -> p a b", a=KCH), wiT_b[:], sem="W1")
    flush("W1", "W1B")
    barrier()

    for rc in range(4):
        hh = rc // 2
        par = rc % 2
        sw = "W0" if par == 0 else "W0B"
        wait("sync", "V0", cnt["V0"] - 1)
        dma("sync", wstage[par][:], d_wq[rc * 128:(rc + 1) * 128, :], sem=sw)
        wait_cur("vector", sw)
        wait("vector", "PEV", cnt["PEV"])
        op("vector", lambda e, par=par: e.tensor_copy(out=wcast[:], in_=wstage[par][:]))
        drain("vector", sem="V0")
        wait_cur("tensor", "V0")
        wait_cur("scalar", "TEV")
        for g in range(8):
            bnk = g % 2
            wait("tensor", "PEV", cnt["PBK"] - 1)
            op("tensor", lambda e, g=g, bnk=bnk, hh=hh: e.matmul(
                ps_pj[bnk][:], c_T[:, hh * 128:(hh + 1) * 128],
                wcast[:, g * 512:(g + 1) * 512], start=True, stop=True), sem="PBK")
            wait_cur("scalar", "PBK")
            op("scalar", lambda e, g=g, bnk=bnk: e.copy(
                out=wcast2[:, g * 512:(g + 1) * 512], in_=ps_pj[bnk][:]))
            drain("scalar", sem="PEV")
        wait_cur("tensor", "PEV")
        transp32(wcast2, wqT, slice(rc * 128, (rc + 1) * 128), rc)
    flush("W0", "W0B")
    barrier()

    TEV0 = cnt["TEV"]
    TPS0 = cnt["TPS"]
    P1VX0 = cnt["VX"]
    cnt["XFR"] += NT
    cnt["XFRB"] += NT
    SSM0 = cnt["SSM"]
    LDX0 = cnt["LDX"]
    LDXB0 = cnt["LDXB"]

    def p1():
        pev = [0]
        pbk = [cnt["PBK"]]
        def emit_proj(j):
            jp = j % 2
            wait("tensor", "TEV", TEV0 + 8 * (j + 1))
            wait("scalar", "KVQ", 160 * j)
            wait("vector", "KVQ", 160 * j)
            for ch in range(4):
                b = pbk[0] % 2
                wait("tensor", "PEV", pbk[0] - 1)
                for k in range(KCH):
                    op("tensor", lambda e, ch=ch, k=k, b=b, jp=jp: e.matmul(
                        ps_pj[b][:, 0:256], wqT[:, k, ch * 128:(ch + 1) * 128],
                        xn_fm2[jp][:, k, :], start=(k == 0), stop=(k == KCH - 1)),
                        sem=("PBK" if k == KCH - 1 else None))
                pbk[0] += 1
                wait("vector", "PBK", pbk[0])
                op("vector", lambda e, ch=ch, b=b: e.tensor_scalar(
                    out=q_sb[:, ch, :], in0=ps_pj[b][:, 0:256],
                    scalar1=bqe[:, ch:ch + 1], scalar2=None, op0=A.add), sem="PEV")
                wait("vector", "TNM", max(0, cnt["TNM"] - 1))
                wait("vector", "PEV", cnt["PEV"])
                op("vector", lambda e, ch=ch: e.tensor_mul(
                    out=sq_sb[:, ch % 2, :], in0=q_sb[:, ch, :], in1=q_sb[:, ch, :]),
                    sem="SQN")
                wait("tensor", "SQN", cnt["SQN"])
                if ch == 0:
                    wait("tensor", "NRD", j)
                op("tensor", lambda e, ch=ch: e.matmul(
                    pb_f[2 + ch // 2][0:1, 0:256], c_onesb[:], sq_sb[:, ch % 2, :],
                    start=(ch % 2 == 0), stop=(ch % 2 == 1)), sem="TNM")
            for ch in range(4):
                b = pbk[0] % 2
                wait("tensor", "PEV", pbk[0] - 1)
                for k in range(KCH):
                    op("tensor", lambda e, ch=ch, k=k, b=b, jp=jp: e.matmul(
                        ps_pj[b][:, 0:256], wkT[:, k, ch * 128:(ch + 1) * 128],
                        xn_fm2[jp][:, k, :], start=(k == 0), stop=(k == KCH - 1)),
                        sem=("PBK" if k == KCH - 1 else None))
                pbk[0] += 1
                wait("scalar", "PBK", pbk[0])
                op("scalar", lambda e, ch=ch, b=b: e.activation(
                    out=k_sb[:, ch, :], in_=ps_pj[b][:, 0:256], func=ACT.Identity,
                    bias=bkT[:, ch:ch + 1]), sem="PEV")
                wait("scalar", "TNM", cnt["TNM"] - 1)
                wait("scalar", "PEV", cnt["PEV"])
                op("scalar", lambda e, ch=ch: e.activation(
                    out=sq_sb[:, ch % 2, :], in_=k_sb[:, ch, :], func=ACT.Square),
                    sem="SQN")
                wait("tensor", "SQN", cnt["SQN"])
                op("tensor", lambda e, ch=ch: e.matmul(
                    pb_f[4 + ch // 2][0:1, 0:256], c_onesb[:], sq_sb[:, ch % 2, :],
                    start=(ch % 2 == 0), stop=(ch % 2 == 1)), sem="TNM")
            for tc in range(2):
                b = pbk[0] % 2
                wait("tensor", "PEV", pbk[0] - 1)
                for k in range(KCH):
                    op("tensor", lambda e, tc=tc, k=k, b=b, jp=jp: e.matmul(
                        ps_pj[b][:], xn_fm2[jp][:, k, tc * 128:(tc + 1) * 128],
                        wvT[:, k, :], start=(k == 0), stop=False),
                        sem=None)
                op("tensor", lambda e, tc=tc, b=b: e.matmul(
                    ps_pj[b][:], c_onesrb[:], b_v_bf[:],
                    start=False, stop=True), sem="PBK")
                pbk[0] += 1
                wait("vector", "PBK", pbk[0])
                wait("vector", "KVQ", 160 * j)
                op("vector", lambda e, tc=tc, b=b: e.tensor_copy(
                    out=v_sb[:, tc, :], in_=ps_pj[b][:]), sem="PEV")
            PEV_AT[j] = cnt["PEV"]
            drain("tensor", sem="PRJ")
            wait("vector", "TNM", cnt["TNM"])
            for bi, dst in ((2, 4), (3, 4), (4, 5), (5, 5)):
                op("vector", lambda e, bi=bi, dst=dst: e.reduce_max(
                    out=sm_all[0:1, dst:dst + 1], in_=pb_f[bi][0:1, 0:256], axis=AX),
                    sem="VSF")
                selfwait("vector")
                op("vector", lambda e, bi=bi, dst=dst: e.tensor_max(
                    out=(qnmax if bi < 4 else knmax),
                    in0=(qnmax if bi < 4 else knmax), in1=sm_all[0:1, dst:dst + 1]),
                    sem="VSF")
                selfwait("vector")
            op("vector", lambda e: e.nop(), sem="NRD")

        for t in range(NT):
            p = t % 2
            m = t // 2
            half = t % 2
            if t % 2 == 0 and m >= 1:
                emit_proj(m - 1)
            sl = "LDX" if p == 0 else "LDXB"
            sd = "DMO" if p == 0 else "DMOB"
            sf = "XFR" if p == 0 else "XFRB"
            wait("sync", sf, 2 * (t // 2))
            dma("sync", xst[p][:], d_x[t * 128:(t + 1) * 128, :], sem=sl)
            if t >= 1:
                wait("sync", "VX", 8 * t + P1VX0)
                dma("sync", d_xiscr[(t - 1) * 128:t * 128, :],
                    xi_tok[0][:], sem="DMO")
            if t % 2 == 0 and m >= 2:
                mm = m - 2
                wait("sync", "PEV", PEV_AT[mm])
                for ch in range(4):
                    dma("sync", d_kscr[ch * 128:(ch + 1) * 128, mm * 256:(mm + 1) * 256],
                        k_sb[:, ch, :], sem="KVQ")
                    dma("sync", d_qscr[ch * 128:(ch + 1) * 128, mm * 256:(mm + 1) * 256],
                        q_sb[:, ch, :], sem="KVQ")
                for tc in range(2):
                    dma("sync", d_vscr[mm * 256 + tc * 128:mm * 256 + (tc + 1) * 128, :],
                        v_sb[:, tc, :], sem="KVQ")

            wait("vector", sl, 16 * (t // 2 + 1) + (LDX0 if p == 0 else LDXB0))
            wait("vector", "SSM", max(0, t - 1))
            selfwait("vector")
            for i in range(8):
                op("vector", lambda e, p=p, i=i: e.bn_stats(
                    out=st_bn[:, i * 6:(i + 1) * 6], in_=xst[p][:, i * 512:(i + 1) * 512]),
                    sem="VSF")
            op("vector", lambda e, p=p: e.reduce_max(
                out=st_xmax[:, p:p + 1], in_=xst[p][:], axis=AX), sem="VS")
            op("vector", lambda e, p=p: e.tensor_reduce(
                out=st_xmin[:, p:p + 1], in_=xst[p][:], axis=AX, op=A.min), sem="VS")
            selfwait("vector")
            op("vector", lambda e, p=p: e.bn_aggr(out=st_mv[:, 2 * p:2 * p + 2], in_=st_bn[:]),
                sem="VS")
            wait("scalar", "VS", 3 * (t + 1))
            wait("scalar", "VQS", max(0, t - 1))
            selfwait("scalar")
            op("scalar", lambda e, p=p: e.activation(
                out=st_varc[:, p:p + 1], in_=st_mv[:, 2 * p + 1:2 * p + 2], func=ACT.Relu),
                sem="SSF")
            selfwait("scalar")
            op("scalar", lambda e, p=p: e.activation(
                out=st_varc[:, p:p + 1], in_=st_varc[:, p:p + 1], func=ACT.Sqrt,
                bias=c_eps[:]), sem="SSQ")
            wait("vector", "SSQ", t + 1)
            op("vector", lambda e, p=p: e.reciprocal(
                out=st_rstd[:, p:p + 1], in_=st_varc[:, p:p + 1]), sem="VRS")
            wait("scalar", "VRS", t + 1)
            op("scalar", lambda e, p=p: e.mul(
                out=st_nrstd[:, p:p + 1], in_=st_rstd[:, p:p + 1], mul=-1.0), sem="SSF")
            selfwait("scalar")
            op("scalar", lambda e, p=p: e.activation(
                out=st_nmrs[:, p:p + 1], in_=st_mv[:, 2 * p:2 * p + 1], func=ACT.Identity,
                scale=st_nrstd[:, p:p + 1]), sem="SSF")
            selfwait("scalar")
            op("scalar", lambda e, p=p: e.mul(
                out=st_mrs[:, p:p + 1], in_=st_nmrs[:, p:p + 1], mul=-1.0), sem="SSF")
            op("scalar", lambda e, p=p: e.activation(
                out=st_a[:, p:p + 1], in_=st_xmax[:, p:p + 1], func=ACT.Identity,
                scale=st_rstd[:, p:p + 1], bias=st_nmrs[:, p:p + 1]), sem="SSF")
            selfwait("scalar")
            op("scalar", lambda e, p=p: e.activation(
                out=st_b[:, p:p + 1], in_=st_xmin[:, p:p + 1], func=ACT.Identity,
                scale=st_nrstd[:, p:p + 1], bias=st_mrs[:, p:p + 1]), sem="SSM")
            wait("scalar", "TPS", TPS0 + 4 * t)
            op("scalar", lambda e, p=p: e.activation(
                out=xn_tok[p][:], in_=xst[p][:], func=ACT.Identity,
                scale=st_rstd[:, p:p + 1], bias=st_nmrs[:, p:p + 1]), sem="SXN")
            wait("scalar", "SXN", cnt["SXN"])
            op("scalar", lambda e, sf=sf: e.nop().then_inc(S[sf], 1))
            wait("vector", "SSM", t + 1)
            op("vector", lambda e, p=p: e.tensor_max(
                out=st_amax[:, p:p + 1], in0=st_a[:, p:p + 1], in1=st_b[:, p:p + 1]),
                sem="VSF")
            selfwait("vector")
            op("vector", lambda e, p=p: e.tensor_scalar(
                out=st_sx[:, p:p + 1], in0=st_amax[:, p:p + 1],
                scalar1=1.0 / 127.0, scalar2=1e-8, op0=A.mult, op1=A.max), sem="VSF")
            selfwait("vector")
            op("vector", lambda e, p=p: e.reciprocal(
                out=st_rx[:, p:p + 1], in_=st_sx[:, p:p + 1]), sem="VSF")
            op("vector", lambda e, p=p, t=t: e.tensor_mul(
                out=sxw[:, t:t + 1], in0=st_sx[:, p:p + 1], in1=s_wcol[:]), sem="VSF")
            selfwait("vector")
            op("vector", lambda e, p=p: e.tensor_mul(
                out=st_rr[:, p:p + 1], in0=st_rstd[:, p:p + 1], in1=st_rx[:, p:p + 1]),
                sem="VSF")
            op("vector", lambda e, p=p: e.scalar_tensor_tensor(
                out=st_bb[:, p:p + 1], in0=st_nmrs[:, p:p + 1], scalar=st_rx[:, p:p + 1],
                in1=c_2565[:], op0=A.mult, op1=A.add), sem="VSF")
            selfwait("vector")
            op("vector", lambda e: e.nop(), sem="VQS")
            wait("vector", "DMO", 16 * t)
            for c in range(4):
                s_ = ridx[0] % 2
                cs = slice(c * 512, (c + 1) * 512)
                round_chunk(
                    lambda e, p=p, cs=cs, s_=s_: e.tensor_scalar(
                        out=r_w[s_][:], in0=xst[p][:, cs], scalar1=st_rr[:, p:p + 1],
                        scalar2=st_bb[:, p:p + 1], op0=A.mult, op1=A.add),
                    xi_tok[p][:, cs])
            wait("tensor", "SXN", t + 1)
            for g in range(4):
                gt = 4 * t + g
                bnk = gt % 2
                wait("tensor", "TEV", TEV0 + gt - 1)
                for j in range(8):
                    f = g * 8 + j
                    op("tensor", lambda e, p=p, f=f, bnk=bnk, j=j: e.matmul(
                        pb_t[bnk][:, j * 128:(j + 1) * 128],
                        xn_tok[p][:, f * 128:(f + 1) * 128], c_id[:],
                        start=(j == 0), stop=(j == 7), is_transpose=True),
                        sem=("TPS" if j == 7 else None))
            for g in range(2):
                gt = 4 * t + g
                wait("scalar", "TPS", TPS0 + gt + 1)
                wait("scalar", "PRJ", max(0, m - 1))
                op("scalar", lambda e, g=g, bnk=gt % 2, half=half, mp=m % 2: e.copy(
                    out=xn_fm2[mp][:, g * 8:(g + 1) * 8, half * 128:half * 128 + 128],
                    in_=pb_t[bnk][:]), sem="TEV")
            for c in range(4, 8):
                s_ = ridx[0] % 2
                cs = slice(c * 512, (c + 1) * 512)
                round_chunk(
                    lambda e, p=p, cs=cs, s_=s_: e.tensor_scalar(
                        out=r_w[s_][:], in0=xst[p][:, cs], scalar1=st_rr[:, p:p + 1],
                        scalar2=st_bb[:, p:p + 1], op0=A.mult, op1=A.add),
                    xi_tok[p][:, cs])
            wait("vector", "VW", cnt["VW"])
            op("vector", lambda e, sf=sf: e.nop().then_inc(S[sf], 1))
            for g in range(2, 4):
                gt = 4 * t + g
                wait("scalar", "TPS", TPS0 + gt + 1)
                wait("scalar", "PRJ", max(0, m - 1))
                op("scalar", lambda e, g=g, bnk=gt % 2, half=half, mp=m % 2: e.copy(
                    out=xn_fm2[mp][:, g * 8:(g + 1) * 8, half * 128:half * 128 + 128],
                    in_=pb_t[bnk][:]), sem="TEV")

        wait("sync", "VX", 8 * NT + P1VX0)
        dma("sync", d_xiscr[(NT - 1) * 128:NT * 128, :], xi_tok[0][:], sem="DMO")
        emit_proj(NMAC - 1)
        for mm in (NMAC - 2, NMAC - 1):
            wait("sync", "PEV", PEV_AT[mm])
            for ch in range(4):
                dma("sync", d_kscr[ch * 128:(ch + 1) * 128, mm * 256:(mm + 1) * 256],
                    k_sb[:, ch, :], sem="KVQ")
                dma("sync", d_qscr[ch * 128:(ch + 1) * 128, mm * 256:(mm + 1) * 256],
                    q_sb[:, ch, :], sem="KVQ")
            for tc in range(2):
                dma("sync", d_vscr[mm * 256 + tc * 128:mm * 256 + (tc + 1) * 128, :],
                    v_sb[:, tc, :], sem="KVQ")

    PEV_AT = {}
    p1()
    flush("LDX", "LDXB", "DMO", "DMOB", "KVQ")
    flush("XFR", "XFRB")
    barrier()

    op("scalar", lambda e: e.activation(out=cc_q, in_=qnmax, func=ACT.Sqrt))
    op("scalar", lambda e: e.activation(out=cc_k, in_=knmax, func=ACT.Sqrt))
    drain("scalar", sem="CB1")
    wait_cur("vector", "CB1")
    op("vector", lambda e: e.tensor_mul(out=cc_c, in0=cc_q, in1=cc_k))
    drain("vector")
    op("vector", lambda e: e.tensor_scalar_mul(out=cc_c2, in0=cc_c, scalar1=-0.5))
    drain("vector", sem="CB2")
    wait_cur("tensor", "CB2")
    op("tensor", lambda e: e.matmul(ps_m0[:, 0:1], c_onesf[:], cc_c2,
                                    start=True, stop=True), sem="CB3")
    wait_cur("scalar", "CB3")
    op("scalar", lambda e: e.copy(out=eb_exp[:], in_=ps_m0[:, 0:1]))
    drain("scalar", sem="CB1")
    dma("sync", wiT[:], d_wiscr.rearrange("p (a b) -> p a b", a=KCH), sem="KLD")
    flush("KLD")
    barrier()

    v_re = d_vscr.rearrange("(mc p) f -> p mc f", p=128)

    glt = [0]
    ridx2 = [0]
    qk_i = [0]
    ffb = [0]

    def p2_head(hh):
        for ch in range(2):
            dma("sync", K_res[:, ch, :],
                d_kscr[hh * 256 + ch * 128:hh * 256 + (ch + 1) * 128, :], sem="KLD")
        dma("sync", V_res[:, :, 0:256], v_re[:, :, hh * 256:(hh + 1) * 256], sem="KLD")
        wait_cur("vector", "KLD")
        op("vector", lambda e: e.memset(V_res[:, :, 256:257], 1.0))
        drain("vector", sem="NRD")
        wait_cur("tensor", "NRD")
        wait_cur("scalar", "CB1")
        for lt in range(NLT):
            g = glt[0]
            wait("sync", "QF", g - 1)
            sq_ = "QLD" if lt % 2 == 0 else "QLDB"
            for ch in range(2):
                dma("sync", q2_sb[lt % 2][:, ch, :],
                    d_qscr[hh * 256 + ch * 128:hh * 256 + (ch + 1) * 128,
                           lt * 512:(lt + 1) * 512], sem=sq_)
            if hh == 1:
                for lc in range(4):
                    row = lt * 512 + lc * 128
                    wait("sync", "VOUT", max(0, cnt["VOUT"] - 4 + lc + 1))
                    wait("sync", "TXF", max(0, ridx2[0] // 4 - 1))
                    sx2 = "XI2" if lc % 2 == 0 else "XI2B"
                    sa2 = "A2" if lc % 2 == 0 else "A2B"
                    ss2 = "XSL" if lc % 2 == 0 else "XSLB"
                    dma("sync", xi2_tok[lc % 2][:], d_xiscr[row:row + 128, :], sem=sx2)
                    dma("sync", a0_sb[lc][:], d_ascr[row:row + 128, :], sem=sa2)
                    dma("sync", xsl_sb[lc][:], d_xsl[row:row + 128, :], sem=ss2)
                    wait("tensor", sx2, cnt[sx2])
                    for gg in range(4):
                        bnk = ridx2[0] % 2
                        wait("tensor", "FFV", max(0, ffb[0] - 1))
                        wait("tensor", "TEV", cnt["TEV"] - 1)
                        for j in range(8):
                            f = gg * 8 + j
                            op("tensor", lambda e, lc=lc, f=f, bnk=bnk, j=j: e.matmul(
                                pb_t[bnk][:, j * 128:(j + 1) * 128],
                                xi2_tok[lc % 2][:, f * 128:(f + 1) * 128], c_id[:],
                                start=(j == 0), stop=(j == 7), is_transpose=True),
                                sem=("TPS" if j == 7 else None))
                        wait("vector", "TPS", cnt["TPS"])
                        op("vector", lambda e, gg=gg, bnk=bnk, lc=lc: e.tensor_copy(
                            out=xi2_fm[:, gg * 8:(gg + 1) * 8, lc * 128:(lc + 1) * 128],
                            in_=pb_t[bnk][:]), sem="TEV")
                        ridx2[0] += 1
                    drain("tensor", sem="TXF")
                    b = lc % 2
                    wait("tensor", "TEV", cnt["TEV"])
                    wait("tensor", "ER", cnt["ER"])
                    wait("tensor", "FFV", max(0, ffb[0] - 1))
                    for k in range(KCH):
                        op("tensor", lambda e, lc=lc, k=k, b=b: e.matmul(
                            ps_pj[b][:], xi2_fm[:, k, lc * 128:(lc + 1) * 128],
                            wiT[:, k, :], start=(k == 0), stop=(k == KCH - 1)),
                            sem=("FPB" if k == KCH - 1 else None))
                    wait("vector", "FPB", cnt["FPB"])
                    t_idx = lt * 4 + lc
                    op("vector", lambda e, lc=lc, b=b, t_idx=t_idx: e.tensor_scalar_mul(
                        out=ffn_sb[lc][:], in0=ps_pj[b][:],
                        scalar1=sxw[:, t_idx:t_idx + 1]), sem="FFV")
                    ffb[0] += 1
            wait("tensor", sq_, cnt[sq_])
            def emit_qk(i, mc):
                b = i % 2
                wait("tensor", "ER", i - 1)
                if hh == 1 and mc < 2:
                    wait("tensor", "FFV", ffb[0])
                for d in range(2):
                    op("tensor", lambda e, d=d, mc=mc, b=b, lt=lt: e.matmul(
                        ps_pj[b][:], K_res[:, d, mc * 128:(mc + 1) * 128],
                        q2_sb[lt % 2][:, d, :], start=(d == 0), stop=(d == 1)),
                        sem=("SR" if d == 1 else None))
                wait("scalar", "SR", i + 1)
                wait("scalar", "EF", i - 2)
                op("scalar", lambda e, b=b, e3=i % 3: e.activation(
                    out=expS[e3][:], in_=ps_pj[b][:], func=ACT.Exp,
                    bias=eb_exp[:]), sem="ER")

            def emit_pv(i, mc):
                e3 = i % 3
                wait("tensor", "ER", i + 1)
                for lc in range(4):
                    op("tensor", lambda e, lc=lc, e3=e3, mc=mc: e.matmul(
                        ps_o[lc][:, 0:257], expS[e3][:, lc * 128:(lc + 1) * 128],
                        V_res[:, mc, :], start=(mc == 0), stop=(mc == MCN - 1)),
                        sem=("EF" if lc == 3 else None))

            i0 = qk_i[0]
            qk_i[0] += MCN
            wait("tensor", "OF", 4 * g)
            emit_qk(i0, 0)
            for mc in range(1, MCN):
                emit_qk(i0 + mc, mc)
                emit_pv(i0 + mc - 1, mc - 1)
            emit_pv(i0 + MCN - 1, MCN - 1)
            drain("tensor", sem="QF")
            for lc in range(4):
                wait("vector", "EF", 64 * (g + 1))
                wait("vector", "OF", cnt["OF"])
                op("vector", lambda e, lc=lc: e.reciprocal(
                    out=rec_sb[:], in_=ps_o[lc][:, 256:257]), sem="VSF")
                selfwait("vector")
                if hh == 0:
                    sad = "ASD" if lc % 2 == 0 else "ASDB"
                    wait("vector", sad, cnt[sad])
                    op("vector", lambda e, lc=lc: e.tensor_scalar_mul(
                        out=a0_sb[lc][:], in0=ps_o[lc][:, 0:256],
                        scalar1=rec_sb[:]), sem="OF")
                    wait("sync", "OF", cnt["OF"])
                    row = lt * 512 + lc * 128
                    dma("sync", d_ascr[row:row + 128, :], a0_sb[lc][:], sem=sad)
                else:
                    op("vector", lambda e, lc=lc: e.tensor_scalar_mul(
                        out=attn_f[lc][:, 256:512], in0=ps_o[lc][:, 0:256],
                        scalar1=rec_sb[:]), sem="OF")
                    wait("vector", "A2" if lc % 2 == 0 else "A2B",
                         cnt["A2" if lc % 2 == 0 else "A2B"])
                    wait("vector", "XSL" if lc % 2 == 0 else "XSLB",
                         cnt["XSL" if lc % 2 == 0 else "XSLB"])
                    op("vector", lambda e, lc=lc: e.tensor_copy(
                        out=attn_f[lc][:, 0:256], in_=a0_sb[lc][:]), sem="VSF")
                    selfwait("vector")
                    op("vector", lambda e, lc=lc: e.tensor_add(
                        out=tadd_sb[lc][:], in0=attn_f[lc][:], in1=ffn_sb[lc][:]),
                        sem="VSF")
                    selfwait("vector")
                    wait("vector", "OUT", 16 * max(0, cnt["VOUT"] - lc))
                    op("vector", lambda e, lc=lc: e.tensor_add(
                        out=out_sb[lc][:], in0=tadd_sb[lc][:], in1=xsl_sb[lc][:]),
                        sem="VOUT")
                    wait("sync", "VOUT", cnt["VOUT"])
                    row = lt * 512 + lc * 128
                    dma("sync", d_out[row:row + 128, :], out_sb[lc][:], sem="OUT")
            glt[0] += 1

    p2_head(0)
    flush("ASD", "ASDB", "QLD", "QLDB", "KLD")
    barrier()
    p2_head(1)
    wait_cur("sync", "OUT")
    wait_cur("sync", "ASD")
    if dev:
        op("vector", lambda e: e.memset(st_bn[:, 0:1], 0.0))
        drain("vector", sem="V0")
        wait_cur("sync", "V0")
        dma("sync", d_dbg[:, 0:64], sxw[:], sem="OUT", slow=True)
        dma("sync", d_dbg[:, 64:65], s_wcol[:], sem="OUT", slow=True)
        dma("sync", d_dbg[:, 65:66], r_wcol[:], sem="OUT", slow=True)
        dma("sync", d_dbg[:, 66:67], eb_exp[:], sem="OUT", slow=True)
        dma("sync", d_dbg[0:1, 67:68], qnmax, sem="OUT", slow=True)
        dma("sync", d_dbg[0:1, 68:69], knmax, sem="OUT", slow=True)
        wait_cur("sync", "OUT")

    for role, semn, val in deferred_waits:
        assert val <= cnt[semn], ("unsatisfiable wait", role, semn, val, cnt[semn])
    with nc.Block() as block:
        for role in ROLES:
            getattr(block, role)(lambda eng, role=role: [f(eng) for f in plan[role]])
    return nc


def make_in_maps(inputs, ncore=8):
    cst = host_constants()
    x = np.ascontiguousarray(np.asarray(inputs["L0_attn_x"], np.float32).reshape(L, DM))
    wq = np.asarray(inputs["L0_attn_Wq"], np.float32)
    wk = np.asarray(inputs["L0_attn_Wk"], np.float32)
    wv = np.asarray(inputs["L0_attn_Wv"], np.float32)
    bq = np.asarray(inputs["L0_attn_bq"], np.float32)
    bk = np.asarray(inputs["L0_attn_bk"], np.float32)
    bv = np.asarray(inputs["L0_attn_bv"], np.float32)
    bw = np.asarray(inputs["L0_attn_bias_weight"], np.float32)
    wf = np.ascontiguousarray(np.asarray(inputs["L0_ffn_weight"], np.float32))
    maps = []
    for c in range(ncore):
        sl = slice(FS * c, FS * (c + 1))
        m = dict(
            x=x,
            xsl=np.ascontiguousarray(x[:, sl]),
            wq=np.ascontiguousarray(wq[sl]),
            wk=np.ascontiguousarray(wk[sl]),
            wv=np.ascontiguousarray(wv[sl]),
            wf=wf,
            wfs=np.ascontiguousarray(wf[sl]),
            bq=np.ascontiguousarray(bq[sl]).reshape(1, FS),
            bk=np.ascontiguousarray(bk[sl]).reshape(1, FS),
            bv=np.ascontiguousarray(bv[sl]).reshape(1, FS),
            bwT=np.ascontiguousarray(bw[2 * c:2 * c + 2].T),
            **cst,
        )
        maps.append(m)
    return maps


def assemble(results):
    return np.concatenate([np.asarray(r["out"]) for r in results],
                          axis=1).reshape(1, L, DM)


from concourse.bass_utils import run_bass_kernel_spmd

_NC_CACHE = {}


def kernel(**inputs):
    maps = make_in_maps(inputs, ncore=8)
    if "nc" not in _NC_CACHE:
        _NC_CACHE["nc"] = build(dev=False)
    nc = _NC_CACHE["nc"]
    res = run_bass_kernel_spmd(nc, maps, core_ids=list(range(8)))
    return assemble(res.results)


# version 07
# speedup vs baseline: 1.0561x; 1.0561x over previous
import math
import numpy as np
import ml_dtypes
import concourse.bass as bass
import concourse.mybir as mybir

F32 = mybir.dt.float32
BF16 = mybir.dt.bfloat16
I32 = mybir.dt.int32
A = mybir.AluOpType
AX = mybir.AxisListType.X
ACT = mybir.ActivationFunctionType
BF = ml_dtypes.bfloat16

L = 8192
DM = 4096
FS = 512
HD = 256
NT = 64
NMAC = 32
NLT = 16
MCN = 64
KCH = 32
EPS = 1e-5
CLIP1 = 1.4999
SHIFT = 256.5
ROLES = ("sync", "vector", "scalar", "tensor", "gpsimd")
KB = 1024


def _build_tables():
    g = [1.0, 1.0, 1.0]
    n = 8
    masks = sorted(range(n), key=lambda m: (bin(m).count("1"), m))
    inv = {m: i for i, m in enumerate(masks)}
    C = np.zeros((n, n, n), dtype=np.float32)
    for i, a in enumerate(masks):
        for j, b in enumerate(masks):
            sw, at_ = (0, a >> 1)
            while at_:
                sw += bin(at_ & b).count("1")
                at_ >>= 1
            s = -1.0 if sw & 1 else 1.0
            for t in range(3):
                if a >> t & 1 and b >> t & 1:
                    s *= float(g[t])
            C[i, j, inv[a ^ b]] = s
    grades = np.array([bin(m).count("1") for m in masks])
    rev = np.where(grades * (grades - 1) // 2 % 2 == 1, -1.0, 1.0).astype(np.float32)
    met = np.array([float(np.prod([g[t] for t in range(3) if m >> t & 1] + [1.0]))
                    for m in masks], dtype=np.float32)
    return C, rev, met


def host_constants():
    C, rev, met = _build_tables()
    biv = C[:, :, [4, 5, 6]]
    scale = 1.0 / math.sqrt(HD)
    bivf = np.ascontiguousarray(biv.transpose(2, 0, 1).reshape(3, 64)).astype(np.float32)
    rconst = np.tile(np.repeat(rev, 8)[None, :], (2, 1)).astype(np.float32)
    dconst = np.tile((scale * np.diag(rev * met)).reshape(1, 64), (2, 1)).astype(np.float32)
    return dict(
        bivf=bivf, rconst=rconst, dconst=dconst,
        iden=np.eye(128).astype(BF),
        onesf=np.ones((1, 128), np.float32),
        onesb=np.ones((128, 1)).astype(BF),
        onesrb=np.ones((1, 128)).astype(BF),
        onescf=np.ones((128, 1), np.float32),
    )


def build(dev=False):
    nc = bass.Bass("TRN2", target_bir_lowering=False, debug=False)

    d_x = nc.dram_tensor("x", [L, DM], F32, kind="ExternalInput").ap()
    d_xsl = nc.dram_tensor("xsl", [L, FS], F32, kind="ExternalInput").ap()
    d_wq = nc.dram_tensor("wq", [FS, DM], F32, kind="ExternalInput").ap()
    d_wk = nc.dram_tensor("wk", [FS, DM], F32, kind="ExternalInput").ap()
    d_wv = nc.dram_tensor("wv", [FS, DM], F32, kind="ExternalInput").ap()
    d_wf = nc.dram_tensor("wf", [DM, DM], F32, kind="ExternalInput").ap()
    d_wfs = nc.dram_tensor("wfs", [FS, DM], F32, kind="ExternalInput").ap()
    d_bq = nc.dram_tensor("bq", [1, FS], F32, kind="ExternalInput").ap()
    d_bk = nc.dram_tensor("bk", [1, FS], F32, kind="ExternalInput").ap()
    d_bv = nc.dram_tensor("bv", [1, FS], F32, kind="ExternalInput").ap()
    d_bwT = nc.dram_tensor("bwT", [3, 2], F32, kind="ExternalInput").ap()
    d_bivf = nc.dram_tensor("bivf", [3, 64], F32, kind="ExternalInput").ap()
    d_rc = nc.dram_tensor("rconst", [2, 64], F32, kind="ExternalInput").ap()
    d_dc = nc.dram_tensor("dconst", [2, 64], F32, kind="ExternalInput").ap()
    d_id = nc.dram_tensor("iden", [128, 128], BF16, kind="ExternalInput").ap()
    d_onesf = nc.dram_tensor("onesf", [1, 128], F32, kind="ExternalInput").ap()
    d_onesb = nc.dram_tensor("onesb", [128, 1], BF16, kind="ExternalInput").ap()
    d_onesrb = nc.dram_tensor("onesrb", [1, 128], BF16, kind="ExternalInput").ap()
    d_onescf = nc.dram_tensor("onescf", [128, 1], F32, kind="ExternalInput").ap()
    d_out = nc.dram_tensor("out", [L, FS], F32, kind="ExternalOutput").ap()

    skind = "ExternalOutput" if dev else "Internal"
    d_kscr = nc.dram_tensor("k_scr", [FS, L], BF16, kind=skind).ap()
    d_vscr = nc.dram_tensor("v_scr", [L, FS], BF16, kind=skind).ap()
    d_qscr = nc.dram_tensor("q_scr", [FS, L], BF16, kind=skind).ap()
    d_xiscr = nc.dram_tensor("xi_scr", [L, DM], BF16, kind=skind).ap()
    d_wiscr = nc.dram_tensor("wi_scr", [128, KCH * FS], BF16, kind=skind).ap()
    d_ascr = nc.dram_tensor("a_scr", [L, HD], BF16, kind=skind).ap()
    if dev:
        d_dbg = nc.dram_tensor("dbg", [128, 80], F32, kind="ExternalOutput").ap()

    ARENA_PP = 203 * KB
    arena = nc.sbuf_tensor("arena", [128, ARENA_PP], mybir.dt.uint8).__enter__()
    base = nc.lookup_mloc(arena).addr

    def at(name, shape, dtype, off):
        end = off + int(np.prod(shape[1:])) * mybir.dt.size(dtype)
        assert end <= ARENA_PP, (name, end)
        return nc.alloc_sbuf_tensor_at(name, shape, dtype, offset=base + off)

    wqT = at("wqT", [128, KCH, FS], BF16, 0)
    wkT = at("wkT", [128, KCH, FS], BF16, 32 * KB)
    wvT = at("wvT", [128, KCH, FS], BF16, 64 * KB)
    xst = [at(f"xst{i}", [128, DM], F32, 96 * KB + 16 * KB * i) for i in range(2)]
    xn_tok = [at("xntok", [128, DM], BF16, 128 * KB)] * 2
    xn_fm2 = [at(f"xnfm{i}", [128, KCH, 256], BF16, 136 * KB + 16 * KB * i) for i in range(2)]
    xi_tok = [at("xitok", [128, DM], BF16, 168 * KB)] * 2
    RS = 176 * KB
    r_w = [at(f"rw{i}", [128, 512], F32, RS + 2 * KB * i) for i in range(2)]
    r_wm = [at(f"rwm{i}", [128, 512], F32, RS + 2 * KB * (2 + i)) for i in range(2)]
    r_iw = [at(f"riw{i}", [128, 512], mybir.dt.int16, RS + 4 * 2 * KB + KB * i)
            for i in range(2)]
    r_r1 = [at(f"rr1{i}", [128, 512], F32, RS + 10 * KB + 2 * KB * i) for i in range(2)]
    r_cu = [at(f"rcu{i}", [128, 512], F32, RS + 14 * KB + 2 * KB * i) for i in range(2)]
    nrm_sb = at("nrm_sb", [1, 512], F32, RS + 18 * KB)
    k_sb = at("ksb", [128, 4, 256], BF16, 196 * KB)
    v_sb = at("vsb", [128, 2, FS], BF16, 198 * KB)
    q_sb = at("qsb", [128, 4, 256], BF16, 200 * KB)
    sq_sb = at("sqsb", [128, 2, 256], BF16, 202 * KB)
    wstage = [xst[0], xst[1]]
    wcast = at("wcast", [128, DM], BF16, 128 * KB)
    wcast2 = at("wcast2", [128, DM], BF16, 136 * KB)
    wpre = [at(f"wpre{i}", [128, 512], F32, 144 * KB + 2 * KB * i) for i in range(2)]
    wiT_b = at("wiTb", [128, KCH, FS], BF16, 0)
    wiT = at("wiT", [128, KCH, FS], BF16, 0)
    K_res = at("Kres", [128, 2, L], BF16, 32 * KB)
    V_res = at("Vres", [128, MCN, HD + 1], BF16, 64 * KB)
    xi2_tok = [at(f"xi2t{i}", [128, DM], BF16, 98 * KB + 8 * KB * i) for i in range(2)]
    xi2_fm = at("xi2fm", [128, KCH, FS], BF16, 114 * KB)
    q2_sb = [at(f"q2sb{i}", [128, 2, FS], BF16, 146 * KB + 2 * KB * i) for i in range(2)]
    expS = [at(f"expS{i}", [128, 512], BF16, 150 * KB + KB * i) for i in range(3)]
    attn_f = [at(f"attnf{i}", [128, FS], BF16, 166 * KB + KB * i) for i in range(4)]
    a0_sb = [at(f"a0sb{i}", [128, HD], BF16, 170 * KB + 512 * i) for i in range(4)]
    xsl_sb = [at(f"xslb{i}", [128, FS], F32, 172 * KB + 2 * KB * i) for i in range(4)]
    ffn_sb = [at(f"ffnsb{i}", [128, FS], BF16, 180 * KB + KB * i) for i in range(4)]
    tadd_sb = [at(f"tadd{i}", [128, FS], F32, 184 * KB + 2 * KB * i) for i in range(4)]
    out_sb = [at(f"outsb{i}", [128, FS], F32, 192 * KB + 2 * KB * i) for i in range(4)]

    import contextlib
    es = contextlib.ExitStack()
    sb = lambda n, s, d: es.enter_context(nc.sbuf_tensor(n, s, d))
    c_id = sb("c_id", [128, 128], BF16)
    c_onesf = sb("c_onesf", [1, 128], F32)
    c_onesb = sb("c_onesb", [128, 1], BF16)
    c_onesrb = sb("c_onesrb", [1, 128], BF16)
    b_v_bf = sb("b_v_bf", [1, FS], BF16)
    c_onescf = sb("c_onescf", [128, 1], F32)
    c_bivf = at("c_bivf", [3, 64], F32, RS + 6 * KB)
    c_rc = at("c_rc", [2, 64], F32, RS + 6 * KB + 256)
    c_dc = at("c_dc", [2, 64], F32, RS + 6 * KB + 512)
    c_bwT = at("c_bwT", [3, 2], F32, RS + 6 * KB + 768)
    c_T = sb("c_T", [128, 256], BF16)
    c_A = at("c_A", [2, 64], F32, RS + 7 * KB)
    c_Abf = at("c_Abf", [2, 64], BF16, RS + 7 * KB + 256)
    b_q = at("b_q", [1, FS], F32, RS)
    b_k = at("b_k", [1, FS], F32, RS + 2 * KB)
    b_v = at("b_v", [1, FS], F32, RS + 4 * KB)
    bkT = sb("bkT", [128, 4], F32)
    bqT = sb("bqT", [128, 4], F32)
    bqTb = sb("bqTb", [128, 4], BF16)
    bqe = sb("bqe", [128, 4], F32)
    st_all = sb("st_all", [128, 84], F32)
    _st = [0]

    def _stc(n):
        a, _st[0] = _st[0], _st[0] + n
        return st_all[:, a:a + n]

    st_bn = _stc(48)
    st_mv = _stc(4)
    st_varc = _stc(2)
    st_rstd = _stc(2)
    st_nrstd = _stc(2)
    st_nmrs = _stc(2)
    st_mrs = _stc(2)
    st_xmax = _stc(2)
    st_xmin = _stc(2)
    st_a = _stc(2)
    st_b = _stc(2)
    st_amax = _stc(2)
    st_sx = _stc(2)
    st_rx = _stc(2)
    st_rr = _stc(2)
    st_bb = _stc(2)
    st_bb1 = _stc(2)
    c_2565 = sb("c_2565", [128, 1], F32)
    c_eps = sb("c_eps", [128, 1], F32)
    sxw = sb("sxw", [128, NT], F32)
    s_wcol = sb("s_wcol", [128, 1], F32)
    r_wcol = sb("r_wcol", [128, 1], F32)
    wfsum = sb("wfsum", [128, 1], F32)
    wfpart = sb("wfpart", [128, 1], F32)
    sm_all = sb("sm_all", [1, 16], F32)
    s_sw = sm_all[0:1, 0:1]
    s_rw = sm_all[0:1, 1:2]
    qnmax = sm_all[0:1, 2:3]
    knmax = sm_all[0:1, 3:4]
    nrm_red = sm_all[0:1, 4:6]
    cc_q = sm_all[0:1, 6:7]
    cc_k = sm_all[0:1, 7:8]
    cc_c = sm_all[0:1, 8:9]
    cc_c2 = sm_all[0:1, 9:10]
    eb_exp = sb("eb_exp", [128, 1], F32)
    rec_sb = sb("rec_sb", [128, 1], F32)

    pb_f = [es.enter_context(nc.psum_tensor(f"pbf{i}", [128, 512], F32)) for i in range(6)]
    pb_t = [es.enter_context(nc.psum_tensor(f"pbt{i}", [128, 1024], BF16)) for i in range(2)]
    ps_pj = pb_f[0:2]
    ps_o = pb_f[2:6]
    ps_nm = pb_f[2]
    ps_m0 = pb_f[3]

    sem_names = [
        "BAR", "C0", "T0", "V0", "S0", "G0", "W0", "W1",
        "LDX", "LDXB", "W0B", "W1B", "DMOB", "XI2B", "A2B", "XSLB", "QLDB", "XFR", "XFRB", "VS", "GS", "SSM", "SXN",
        "VW", "SC", "GU", "VD", "GT", "VX", "DMO",
        "TPS", "TEV", "PRJ", "PBK", "PEV", "PEVV", "PEVK", "SQN", "TNM", "NRD", "KVQ",
        "KLD", "QLD", "QF", "SR", "ER", "EF", "OF",
        "A2", "XSL", "XI2", "TXF", "FPB", "FFV", "VOUT", "OUT", "ASD", "ASDB",
        "CB1", "CB2", "CB3", "VQS", "SSQ", "VRS", "GE1", "VCC", "VSF", "SSF", "GSF", "VC1", "VT", "NRM",
    ]
    S = {n: es.enter_context(nc.semaphore(n)) for n in sem_names}

    plan = {r: [] for r in ROLES}
    cnt = {n: 0 for n in sem_names}

    def op(role, fn, sem=None, n=1):
        if sem is not None:
            cnt[sem] += n
            plan[role].append(lambda eng, fn=fn, s=S[sem], m=n: fn(eng).then_inc(s, m))
        else:
            plan[role].append(lambda eng, fn=fn: fn(eng))

    def dma(role, out, in_, sem=None, slow=False):
        op(role, lambda e, o=out, i=in_, sl=slow: e.dma_start(
            out=o, in_=i, allow_slow_non_contiguous=sl), sem=sem, n=16)

    deferred_waits = []

    def wait(role, sem, val):
        deferred_waits.append((role, sem, val))
        if val > 0:
            plan[role].append(lambda eng, s=S[sem], v=val: eng.wait_ge(s, v))

    def wait_cur(role, sem):
        wait(role, sem, cnt[sem])

    def drain(role, sem=None, n=1):
        op(role, lambda e: e.drain(), sem=sem, n=n)

    SELF = {"vector": "VSF", "scalar": "SSF", "gpsimd": "GSF"}

    def selfwait(role):
        wait(role, SELF[role], cnt[SELF[role]])

    def chop(role, fn, sem=None):
        selfwait(role)
        op(role, fn, sem=(sem if sem is not None else SELF[role]))

    nbar = [0]

    def flush(*sems, role="sync"):
        for s_ in sems:
            wait_cur(role, s_)

    def barrier():
        nbar[0] += 1
        k = nbar[0] * len(ROLES)
        cnt["BAR"] += len(ROLES)
        for r in ROLES:
            def f(eng, v=k):
                eng.drain().then_inc(S["BAR"], 1)
                eng.wait_ge(S["BAR"], v)
            plan[r].append(f)

    for dst, src in [(c_id, d_id), (c_onesf, d_onesf), (c_onesb, d_onesb),
                     (c_onescf, d_onescf), (c_bivf, d_bivf), (c_rc, d_rc),
                     (c_dc, d_dc), (c_bwT, d_bwT), (b_q, d_bq), (b_k, d_bk),
                     (b_v, d_bv), (c_onesrb, d_onesrb)]:
        dma("sync", dst[:], src, sem="C0")
    op("vector", lambda e: e.memset(c_2565[:], SHIFT))
    op("vector", lambda e: e.memset(c_eps[:], EPS))
    op("vector", lambda e: e.memset(qnmax, 0.0))
    op("vector", lambda e: e.memset(knmax, 0.0))
    op("vector", lambda e: e.memset(wfsum[:], 0.0))
    op("vector", lambda e: e.memset(c_T[:], 0.0))
    flush("C0")
    barrier()

    for f in range(4):
        dma("sync", bkT[:, f:f + 1], b_k[0:1, f * 128:(f + 1) * 128], sem="C0")
        dma("sync", bqT[:, f:f + 1], b_q[0:1, f * 128:(f + 1) * 128], sem="C0")
    op("tensor", lambda e: e.matmul(ps_m0[0:2, 0:64], c_bwT[:], c_bivf[:],
                                    start=True, stop=True), sem="T0")
    wait_cur("vector", "T0")
    op("vector", lambda e: e.tensor_mul(out=c_A[:], in0=ps_m0[0:2, 0:64], in1=c_rc[:]))
    drain("vector")
    op("vector", lambda e: e.tensor_add(out=c_A[:], in0=c_A[:], in1=c_dc[:]))
    drain("vector")
    op("vector", lambda e: e.tensor_copy(out=c_Abf[:], in_=c_A[:]))
    wait_cur("vector", "C0")
    op("vector", lambda e: e.tensor_copy(out=bqTb[:], in_=bqT[:]))
    drain("vector", sem="V0")
    wait_cur("sync", "V0")
    for hh in range(2):
        for dd in range(16):
            dma("sync", c_T[8 * dd:8 * dd + 8, hh * 128 + 8 * dd:hh * 128 + 8 * dd + 8],
                c_Abf[hh:hh + 1, 0:64], sem="C0")
    op("vector", lambda e: e.tensor_copy(out=b_v_bf[:], in_=b_v[:]))
    drain("vector", sem="V0")
    wait_cur("tensor", "C0")
    wait_cur("tensor", "V0")
    for j in range(4):
        if j >= 2:
            wait("tensor", "S0", cnt["S0"] - 1)
        op("tensor", lambda e, j=j: e.matmul(
            ps_pj[j % 2][:, 0:1], c_T[:, (j // 2) * 128:(j // 2) * 128 + 128],
            bqTb[:, j:j + 1], start=True, stop=True), sem="T0")
        wait_cur("scalar", "T0")
        op("scalar", lambda e, j=j: e.copy(out=bqe[:, j:j + 1], in_=ps_pj[j % 2][:, 0:1]))
        drain("scalar", sem="S0")
    flush("C0")
    barrier()

    def transp32(src_bf, dst, rcslice, tag_rc):
        for g in range(4):
            bnk = g % 2
            wait("tensor", "TEV", cnt["TEV"] - 1)
            for j in range(8):
                f = g * 8 + j
                op("tensor", lambda e, f=f, bnk=bnk, j=j, s=src_bf: e.matmul(
                    pb_t[bnk][:, j * 128:(j + 1) * 128], s[:, f * 128:(f + 1) * 128],
                    c_id[:], start=(j == 0), stop=(j == 7), is_transpose=True),
                    sem=("TPS" if j == 7 else None))
            wait_cur("vector", "TPS")
            op("vector", lambda e, g=g, bnk=bnk, d=dst, rcs=rcslice: e.tensor_copy(
                out=d[:, g * 8:(g + 1) * 8, rcs], in_=pb_t[bnk][:]), sem="TEV")

    def w_prep_plain(dsrc, wT):
        for rc in range(4):
            par = rc % 2
            sw = "W0" if par == 0 else "W0B"
            wait("sync", "V0", cnt["V0"] - 1)
            dma("sync", wstage[par][:], dsrc[rc * 128:(rc + 1) * 128, :], sem=sw)
            wait_cur("vector", sw)
            wait("vector", "TEV", cnt["TEV"])
            op("vector", lambda e, par=par: e.tensor_copy(out=wcast[:], in_=wstage[par][:]))
            drain("vector", sem="V0")
            wait_cur("tensor", "V0")
            transp32(wcast, wT, slice(rc * 128, (rc + 1) * 128), rc)

    w_prep_plain(d_wk, wkT)
    w_prep_plain(d_wv, wvT)
    flush("W0", "W0B")
    barrier()

    for ch in range(32):
        par = ch % 2
        sw = "W0" if par == 0 else "W0B"
        wait("sync", "V0", cnt["V0"] - 1)
        dma("sync", wstage[par][:], d_wf[ch * 128:(ch + 1) * 128, :], sem=sw)
        wait("vector", sw, cnt[sw])
        wait("vector", "V0", cnt["V0"])
        op("vector", lambda e, par=par: e.tensor_reduce(
            out=wfpart[:], in_=wstage[par][:], axis=AX, op=A.add,
            apply_absolute_value=True), sem="VSF")
        selfwait("vector")
        op("vector", lambda e: e.tensor_add(out=wfsum[:], in0=wfsum[:], in1=wfpart[:]),
            sem="V0")
    wait_cur("tensor", "V0")
    op("tensor", lambda e: e.matmul(ps_m0[0:1, 0:1], c_onescf[:], wfsum[:],
                                    start=True, stop=True), sem="T0")
    wait_cur("vector", "T0")
    op("vector", lambda e: e.tensor_scalar_mul(out=s_sw, in0=ps_m0[0:1, 0:1],
                                               scalar1=1.0 / (DM * DM)))
    drain("vector")
    op("vector", lambda e: e.tensor_scalar_add(out=s_rw, in0=s_sw, scalar1=1e-8))
    drain("vector")
    op("vector", lambda e: e.reciprocal(out=s_rw, in_=s_rw))
    drain("vector", sem="V0")
    wait_cur("tensor", "V0")
    op("tensor", lambda e: e.matmul(ps_m0[:, 0:1], c_onesf[:], s_sw,
                                    start=True, stop=True))
    op("tensor", lambda e: e.matmul(pb_f[4][:, 0:1], c_onesf[:], s_rw,
                                    start=True, stop=True), sem="T0")
    wait_cur("scalar", "T0")
    op("scalar", lambda e: e.copy(out=s_wcol[:], in_=ps_m0[:, 0:1]))
    op("scalar", lambda e: e.copy(out=r_wcol[:], in_=pb_f[4][:, 0:1]))
    drain("scalar", sem="S0")
    flush("W0", "W0B")
    barrier()

    ridx = [0]

    def round_chunk(make_w, dst_ap):
        i = ridx[0]
        ridx[0] += 1
        s_ = i % 2
        wait("vector", "SC", i - 1)
        wait("vector", "GE1", i - 1)
        op("vector", make_w, sem="VW")
        wait("scalar", "VW", i + 1)
        wait("scalar", "GE1", i - 1)
        wait("scalar", "VX", i - 1)
        op("scalar", lambda e, s_=s_: e.copy(out=r_iw[s_][:], in_=r_w[s_][:]), sem="SSF")
        selfwait("scalar")
        op("scalar", lambda e, s_=s_: e.copy(out=r_r1[s_][:], in_=r_iw[s_][:]), sem="SC")
        wait("gpsimd", "SC", i + 1)
        wait("gpsimd", "VX", i - 1)
        op("gpsimd", lambda e, s_=s_: e.tensor_tensor(
            out=r_wm[s_][:], in0=r_w[s_][:], in1=r_r1[s_][:], op=A.subtract), sem="GE1")
        wait("vector", "GE1", i + 1)
        op("vector", lambda e, s_=s_: e.tensor_single_scalar(
            out=r_w[s_][:], in_=r_wm[s_][:], scalar=0.0, op=A.is_lt), sem="VC1")
        op("vector", lambda e, s_=s_: e.tensor_single_scalar(
            out=r_cu[s_][:], in_=r_wm[s_][:], scalar=1.0, op=A.is_ge), sem="VC1")
        wait("vector", "VC1", 2 * (i + 1))
        op("vector", lambda e, s_=s_: e.tensor_tensor(
            out=r_wm[s_][:], in0=r_cu[s_][:], in1=r_w[s_][:], op=A.subtract), sem="VT")
        wait("vector", "VT", i + 1)
        op("vector", lambda e, s_=s_, d=dst_ap: e.scalar_tensor_tensor(
            out=d, in0=r_r1[s_][:], scalar=-256.0, in1=r_wm[s_][:],
            op0=A.add, op1=A.add), sem="VX")

    for rc in range(4):
        par = rc % 2
        sw = "W1" if par == 0 else "W1B"
        if rc >= 2:
            wait("sync", "VX", (rc - 1) * 8)
        dma("sync", wstage[par][:], d_wfs[rc * 128:(rc + 1) * 128, :], sem=sw)
        wait("vector", sw, cnt[sw])
        wait("vector", "TEV", cnt["TEV"])
        for c in range(8):
            s_ = ridx[0] % 2
            cs = slice(c * 512, (c + 1) * 512)
            op("vector", lambda e, par=par, cs=cs, s_=s_: e.tensor_scalar(
                out=wpre[s_][:], in0=wstage[par][:, cs], scalar1=r_wcol[:],
                scalar2=CLIP1, op0=A.mult, op1=A.min), sem="VSF")
            selfwait("vector")
            round_chunk(
                lambda e, s_=s_: e.tensor_scalar(
                    out=r_w[s_][:], in0=wpre[s_][:], scalar1=-CLIP1, scalar2=SHIFT,
                    op0=A.max, op1=A.add),
                wcast[:, cs],
            )
        wait_cur("tensor", "VX")
        transp32(wcast, wiT_b, slice(rc * 128, (rc + 1) * 128), rc)
    wait_cur("sync", "TEV")
    dma("sync", d_wiscr.rearrange("p (a b) -> p a b", a=KCH), wiT_b[:], sem="W1")
    flush("W1", "W1B")
    barrier()

    for rc in range(4):
        hh = rc // 2
        par = rc % 2
        sw = "W0" if par == 0 else "W0B"
        wait("sync", "V0", cnt["V0"] - 1)
        dma("sync", wstage[par][:], d_wq[rc * 128:(rc + 1) * 128, :], sem=sw)
        wait_cur("vector", sw)
        wait("vector", "PEV", cnt["PEV"])
        op("vector", lambda e, par=par: e.tensor_copy(out=wcast[:], in_=wstage[par][:]))
        drain("vector", sem="V0")
        wait_cur("tensor", "V0")
        wait_cur("scalar", "TEV")
        for g in range(8):
            bnk = g % 2
            wait("tensor", "PEV", cnt["PBK"] - 1)
            op("tensor", lambda e, g=g, bnk=bnk, hh=hh: e.matmul(
                ps_pj[bnk][:], c_T[:, hh * 128:(hh + 1) * 128],
                wcast[:, g * 512:(g + 1) * 512], start=True, stop=True), sem="PBK")
            wait_cur("scalar", "PBK")
            op("scalar", lambda e, g=g, bnk=bnk: e.copy(
                out=wcast2[:, g * 512:(g + 1) * 512], in_=ps_pj[bnk][:]))
            drain("scalar", sem="PEV")
        wait_cur("tensor", "PEV")
        transp32(wcast2, wqT, slice(rc * 128, (rc + 1) * 128), rc)
    flush("W0", "W0B")
    barrier()

    TEV0 = cnt["TEV"]
    TPS0 = cnt["TPS"]
    P1VX0 = cnt["VX"]
    cnt["XFR"] += NT
    cnt["XFRB"] += NT
    SSM0 = cnt["SSM"]
    LDX0 = cnt["LDX"]
    LDXB0 = cnt["LDXB"]

    def p1():
        evhist = []
        pbk = [cnt["PBK"]]
        def emit_proj(j):
            jp = j % 2
            wait("tensor", "TEV", TEV0 + 8 * (j + 1))
            wait("scalar", "KVQ", 160 * j)
            wait("vector", "KVQ", 160 * j)
            for ch in range(4):
                b = pbk[0] % 6
                if len(evhist) >= 6:
                    wait("tensor", *evhist[-6])
                for k in range(KCH):
                    op("tensor", lambda e, ch=ch, k=k, b=b, jp=jp: e.matmul(
                        pb_f[b][:, 0:256], wqT[:, k, ch * 128:(ch + 1) * 128],
                        xn_fm2[jp][:, k, :], start=(k == 0), stop=(k == KCH - 1)),
                        sem=("PBK" if k == KCH - 1 else None))
                pbk[0] += 1
                wait("vector", "PBK", pbk[0])
                op("vector", lambda e, ch=ch, b=b: e.tensor_scalar(
                    out=q_sb[:, ch, :], in0=pb_f[b][:, 0:256],
                    scalar1=bqe[:, ch:ch + 1], scalar2=None, op0=A.add), sem="PEVV")
                evhist.append(("PEVV", cnt["PEVV"]))
            for ch in range(4):
                b = pbk[0] % 6
                if len(evhist) >= 6:
                    wait("tensor", *evhist[-6])
                for k in range(KCH):
                    op("tensor", lambda e, ch=ch, k=k, b=b, jp=jp: e.matmul(
                        pb_f[b][:, 0:256], wkT[:, k, ch * 128:(ch + 1) * 128],
                        xn_fm2[jp][:, k, :], start=(k == 0), stop=(k == KCH - 1)),
                        sem=("PBK" if k == KCH - 1 else None))
                pbk[0] += 1
                wait("scalar", "PBK", pbk[0])
                op("scalar", lambda e, ch=ch, b=b: e.activation(
                    out=k_sb[:, ch, :], in_=pb_f[b][:, 0:256], func=ACT.Identity,
                    bias=bkT[:, ch:ch + 1]), sem="PEVK")
                evhist.append(("PEVK", cnt["PEVK"]))
            for tc in range(2):
                b = pbk[0] % 6
                if len(evhist) >= 6:
                    wait("tensor", *evhist[-6])
                for k in range(KCH):
                    op("tensor", lambda e, tc=tc, k=k, b=b, jp=jp: e.matmul(
                        pb_f[b][:], xn_fm2[jp][:, k, tc * 128:(tc + 1) * 128],
                        wvT[:, k, :], start=(k == 0), stop=False),
                        sem=None)
                op("tensor", lambda e, tc=tc, b=b: e.matmul(
                    pb_f[b][:], c_onesrb[:], b_v_bf[:],
                    start=False, stop=True), sem="PBK")
                pbk[0] += 1
                wait("vector", "PBK", pbk[0])
                op("vector", lambda e, tc=tc, b=b: e.tensor_copy(
                    out=v_sb[:, tc, :], in_=pb_f[b][:]), sem="PEVV")
                evhist.append(("PEVV", cnt["PEVV"]))
            PEV_AT[j] = (cnt["PEVV"], cnt["PEVK"])
            drain("tensor", sem="PRJ")

        for t in range(NT):
            p = t % 2
            m = t // 2
            half = t % 2
            if t % 2 == 0 and m >= 1:
                emit_proj(m - 1)
            sl = "LDX" if p == 0 else "LDXB"
            sd = "DMO" if p == 0 else "DMOB"
            sf = "XFR" if p == 0 else "XFRB"
            wait("sync", sf, 2 * (t // 2))
            dma("sync", xst[p][:], d_x[t * 128:(t + 1) * 128, :], sem=sl)
            if t >= 1:
                wait("sync", "VX", 8 * t + P1VX0)
                dma("sync", d_xiscr[(t - 1) * 128:t * 128, :],
                    xi_tok[0][:], sem="DMO")
            if t % 2 == 0 and m >= 2:
                mm = m - 2
                wait("sync", "PEVV", PEV_AT[mm][0])
                wait("sync", "PEVK", PEV_AT[mm][1])
                for ch in range(4):
                    dma("sync", d_kscr[ch * 128:(ch + 1) * 128, mm * 256:(mm + 1) * 256],
                        k_sb[:, ch, :], sem="KVQ")
                    dma("sync", d_qscr[ch * 128:(ch + 1) * 128, mm * 256:(mm + 1) * 256],
                        q_sb[:, ch, :], sem="KVQ")
                for tc in range(2):
                    dma("sync", d_vscr[mm * 256 + tc * 128:mm * 256 + (tc + 1) * 128, :],
                        v_sb[:, tc, :], sem="KVQ")

            wait("vector", sl, 16 * (t // 2 + 1) + (LDX0 if p == 0 else LDXB0))
            wait("vector", "SSM", max(0, t - 1))
            selfwait("vector")
            for i in range(8):
                op("vector", lambda e, p=p, i=i: e.bn_stats(
                    out=st_bn[:, i * 6:(i + 1) * 6], in_=xst[p][:, i * 512:(i + 1) * 512]),
                    sem="VSF")
            op("vector", lambda e, p=p: e.reduce_max(
                out=st_xmax[:, p:p + 1], in_=xst[p][:], axis=AX), sem="VS")
            op("vector", lambda e, p=p: e.tensor_reduce(
                out=st_xmin[:, p:p + 1], in_=xst[p][:], axis=AX, op=A.min), sem="VS")
            selfwait("vector")
            op("vector", lambda e, p=p: e.bn_aggr(out=st_mv[:, 2 * p:2 * p + 2], in_=st_bn[:]),
                sem="VS")
            wait("scalar", "VS", 3 * (t + 1))
            wait("scalar", "VQS", max(0, t - 1))
            selfwait("scalar")
            op("scalar", lambda e, p=p: e.activation(
                out=st_varc[:, p:p + 1], in_=st_mv[:, 2 * p + 1:2 * p + 2], func=ACT.Relu),
                sem="SSF")
            selfwait("scalar")
            op("scalar", lambda e, p=p: e.activation(
                out=st_varc[:, p:p + 1], in_=st_varc[:, p:p + 1], func=ACT.Sqrt,
                bias=c_eps[:]), sem="SSQ")
            wait("vector", "SSQ", t + 1)
            op("vector", lambda e, p=p: e.reciprocal(
                out=st_rstd[:, p:p + 1], in_=st_varc[:, p:p + 1]), sem="VRS")
            wait("scalar", "VRS", t + 1)
            op("scalar", lambda e, p=p: e.mul(
                out=st_nrstd[:, p:p + 1], in_=st_rstd[:, p:p + 1], mul=-1.0), sem="SSF")
            selfwait("scalar")
            op("scalar", lambda e, p=p: e.activation(
                out=st_nmrs[:, p:p + 1], in_=st_mv[:, 2 * p:2 * p + 1], func=ACT.Identity,
                scale=st_nrstd[:, p:p + 1]), sem="SSF")
            selfwait("scalar")
            op("scalar", lambda e, p=p: e.mul(
                out=st_mrs[:, p:p + 1], in_=st_nmrs[:, p:p + 1], mul=-1.0), sem="SSF")
            op("scalar", lambda e, p=p: e.activation(
                out=st_a[:, p:p + 1], in_=st_xmax[:, p:p + 1], func=ACT.Identity,
                scale=st_rstd[:, p:p + 1], bias=st_nmrs[:, p:p + 1]), sem="SSF")
            selfwait("scalar")
            op("scalar", lambda e, p=p: e.activation(
                out=st_b[:, p:p + 1], in_=st_xmin[:, p:p + 1], func=ACT.Identity,
                scale=st_nrstd[:, p:p + 1], bias=st_mrs[:, p:p + 1]), sem="SSM")
            wait("scalar", "TPS", TPS0 + 4 * t)
            op("scalar", lambda e, p=p: e.activation(
                out=xn_tok[p][:], in_=xst[p][:], func=ACT.Identity,
                scale=st_rstd[:, p:p + 1], bias=st_nmrs[:, p:p + 1]), sem="SXN")
            wait("scalar", "SXN", cnt["SXN"])
            op("scalar", lambda e, sf=sf: e.nop().then_inc(S[sf], 1))
            wait("vector", "SSM", t + 1)
            op("vector", lambda e, p=p: e.tensor_max(
                out=st_amax[:, p:p + 1], in0=st_a[:, p:p + 1], in1=st_b[:, p:p + 1]),
                sem="VSF")
            selfwait("vector")
            op("vector", lambda e, p=p: e.tensor_scalar(
                out=st_sx[:, p:p + 1], in0=st_amax[:, p:p + 1],
                scalar1=1.0 / 127.0, scalar2=1e-8, op0=A.mult, op1=A.max), sem="VSF")
            selfwait("vector")
            op("vector", lambda e, p=p: e.reciprocal(
                out=st_rx[:, p:p + 1], in_=st_sx[:, p:p + 1]), sem="VSF")
            op("vector", lambda e, p=p, t=t: e.tensor_mul(
                out=sxw[:, t:t + 1], in0=st_sx[:, p:p + 1], in1=s_wcol[:]), sem="VSF")
            selfwait("vector")
            op("vector", lambda e, p=p: e.tensor_mul(
                out=st_rr[:, p:p + 1], in0=st_rstd[:, p:p + 1], in1=st_rx[:, p:p + 1]),
                sem="VSF")
            op("vector", lambda e, p=p: e.scalar_tensor_tensor(
                out=st_bb[:, p:p + 1], in0=st_nmrs[:, p:p + 1], scalar=st_rx[:, p:p + 1],
                in1=c_2565[:], op0=A.mult, op1=A.add), sem="VSF")
            selfwait("vector")
            op("vector", lambda e: e.nop(), sem="VQS")
            wait("vector", "DMO", 16 * t)
            for c in range(4):
                s_ = ridx[0] % 2
                cs = slice(c * 512, (c + 1) * 512)
                round_chunk(
                    lambda e, p=p, cs=cs, s_=s_: e.tensor_scalar(
                        out=r_w[s_][:], in0=xst[p][:, cs], scalar1=st_rr[:, p:p + 1],
                        scalar2=st_bb[:, p:p + 1], op0=A.mult, op1=A.add),
                    xi_tok[p][:, cs])
            wait("tensor", "SXN", t + 1)
            for g in range(4):
                gt = 4 * t + g
                bnk = gt % 2
                wait("tensor", "TEV", TEV0 + gt - 1)
                for j in range(8):
                    f = g * 8 + j
                    op("tensor", lambda e, p=p, f=f, bnk=bnk, j=j: e.matmul(
                        pb_t[bnk][:, j * 128:(j + 1) * 128],
                        xn_tok[p][:, f * 128:(f + 1) * 128], c_id[:],
                        start=(j == 0), stop=(j == 7), is_transpose=True),
                        sem=("TPS" if j == 7 else None))
            for g in range(2):
                gt = 4 * t + g
                wait("scalar", "TPS", TPS0 + gt + 1)
                wait("scalar", "PRJ", max(0, m - 1))
                op("scalar", lambda e, g=g, bnk=gt % 2, half=half, mp=m % 2: e.copy(
                    out=xn_fm2[mp][:, g * 8:(g + 1) * 8, half * 128:half * 128 + 128],
                    in_=pb_t[bnk][:]), sem="TEV")
            for c in range(4, 8):
                s_ = ridx[0] % 2
                cs = slice(c * 512, (c + 1) * 512)
                round_chunk(
                    lambda e, p=p, cs=cs, s_=s_: e.tensor_scalar(
                        out=r_w[s_][:], in0=xst[p][:, cs], scalar1=st_rr[:, p:p + 1],
                        scalar2=st_bb[:, p:p + 1], op0=A.mult, op1=A.add),
                    xi_tok[p][:, cs])
            wait("vector", "VW", cnt["VW"])
            op("vector", lambda e, sf=sf: e.nop().then_inc(S[sf], 1))
            for g in range(2, 4):
                gt = 4 * t + g
                wait("scalar", "TPS", TPS0 + gt + 1)
                wait("scalar", "PRJ", max(0, m - 1))
                op("scalar", lambda e, g=g, bnk=gt % 2, half=half, mp=m % 2: e.copy(
                    out=xn_fm2[mp][:, g * 8:(g + 1) * 8, half * 128:half * 128 + 128],
                    in_=pb_t[bnk][:]), sem="TEV")

        wait("sync", "VX", 8 * NT + P1VX0)
        dma("sync", d_xiscr[(NT - 1) * 128:NT * 128, :], xi_tok[0][:], sem="DMO")
        emit_proj(NMAC - 1)
        for mm in (NMAC - 2, NMAC - 1):
            wait("sync", "PEVV", PEV_AT[mm][0])
            wait("sync", "PEVK", PEV_AT[mm][1])
            for ch in range(4):
                dma("sync", d_kscr[ch * 128:(ch + 1) * 128, mm * 256:(mm + 1) * 256],
                    k_sb[:, ch, :], sem="KVQ")
                dma("sync", d_qscr[ch * 128:(ch + 1) * 128, mm * 256:(mm + 1) * 256],
                    q_sb[:, ch, :], sem="KVQ")
            for tc in range(2):
                dma("sync", d_vscr[mm * 256 + tc * 128:mm * 256 + (tc + 1) * 128, :],
                    v_sb[:, tc, :], sem="KVQ")

    PEV_AT = {}
    p1()
    flush("LDX", "LDXB", "DMO", "DMOB", "KVQ")
    flush("XFR", "XFRB")
    barrier()

    nst = [xi2_tok[0], xi2_tok[1]]
    nsq = [at("nsq0", [128, DM], BF16, 114 * KB), at("nsq1", [128, DM], BF16, 122 * KB)]
    for scr, dstmax in ((d_qscr, qnmax), (d_kscr, knmax)):
        for h in range(2):
            for half in range(2):
                wait("sync", "SQN", cnt["SQN"])
                for rr_ in range(2):
                    dma("sync", nst[rr_][:],
                        scr[(2 * h + rr_) * 128:(2 * h + rr_ + 1) * 128,
                            half * DM:(half + 1) * DM], sem="KLD")
                wait_cur("scalar", "KLD")
                wait("scalar", "TNM", cnt["TNM"])
                selfwait("scalar")
                for rr_ in range(2):
                    op("scalar", lambda e, rr_=rr_: e.activation(
                        out=nsq[rr_][:], in_=nst[rr_][:], func=ACT.Square), sem="SQN")
                for cb in range(8):
                    bnk = cb % 2
                    wait("tensor", "SQN", cnt["SQN"])
                    wait("tensor", "NRM", max(0, cnt["NRM"] - 1))
                    for rr_ in range(2):
                        op("tensor", lambda e, rr_=rr_, cb=cb, bnk=bnk: e.matmul(
                            pb_f[bnk][0:1, 0:512], c_onesb[:],
                            nsq[rr_][:, cb * 512:(cb + 1) * 512],
                            start=(rr_ == 0), stop=(rr_ == 1)),
                            sem=("TNM" if rr_ == 1 else None))
                    wait("vector", "TNM", cnt["TNM"])
                    wait("vector", "NRM", cnt["NRM"])
                    op("vector", lambda e, bnk=bnk: e.reduce_max(
                        out=sm_all[0:1, 4:5], in_=pb_f[bnk][0:1, 0:512], axis=AX),
                        sem="VSF")
                    selfwait("vector")
                    op("vector", lambda e, dstmax=dstmax: e.tensor_max(
                        out=dstmax, in0=dstmax, in1=sm_all[0:1, 4:5]), sem="NRM")
    barrier()
    op("scalar", lambda e: e.activation(out=cc_q, in_=qnmax, func=ACT.Sqrt))
    op("scalar", lambda e: e.activation(out=cc_k, in_=knmax, func=ACT.Sqrt))
    drain("scalar", sem="CB1")
    wait_cur("vector", "CB1")
    op("vector", lambda e: e.tensor_mul(out=cc_c, in0=cc_q, in1=cc_k))
    drain("vector")
    op("vector", lambda e: e.tensor_scalar_mul(out=cc_c2, in0=cc_c, scalar1=-0.5))
    drain("vector", sem="CB2")
    wait_cur("tensor", "CB2")
    op("tensor", lambda e: e.matmul(ps_m0[:, 0:1], c_onesf[:], cc_c2,
                                    start=True, stop=True), sem="CB3")
    wait_cur("scalar", "CB3")
    op("scalar", lambda e: e.copy(out=eb_exp[:], in_=ps_m0[:, 0:1]))
    drain("scalar", sem="CB1")
    dma("sync", wiT[:], d_wiscr.rearrange("p (a b) -> p a b", a=KCH), sem="KLD")
    flush("KLD")
    barrier()

    v_re = d_vscr.rearrange("(mc p) f -> p mc f", p=128)

    glt = [0]
    ridx2 = [0]
    qk_i = [0]
    ffb = [0]

    def p2_head(hh):
        for ch in range(2):
            dma("sync", K_res[:, ch, :],
                d_kscr[hh * 256 + ch * 128:hh * 256 + (ch + 1) * 128, :], sem="KLD")
        dma("sync", V_res[:, :, 0:256], v_re[:, :, hh * 256:(hh + 1) * 256], sem="KLD")
        wait_cur("vector", "KLD")
        op("vector", lambda e: e.memset(V_res[:, :, 256:257], 1.0))
        drain("vector", sem="NRD")
        wait_cur("tensor", "NRD")
        wait_cur("scalar", "CB1")
        for lt in range(NLT):
            g = glt[0]
            wait("sync", "QF", g - 1)
            sq_ = "QLD" if lt % 2 == 0 else "QLDB"
            for ch in range(2):
                dma("sync", q2_sb[lt % 2][:, ch, :],
                    d_qscr[hh * 256 + ch * 128:hh * 256 + (ch + 1) * 128,
                           lt * 512:(lt + 1) * 512], sem=sq_)
            if hh == 1:
                for lc in range(4):
                    row = lt * 512 + lc * 128
                    wait("sync", "VOUT", max(0, cnt["VOUT"] - 4 + lc + 1))
                    wait("sync", "TXF", max(0, ridx2[0] // 4 - 1))
                    sx2 = "XI2" if lc % 2 == 0 else "XI2B"
                    sa2 = "A2" if lc % 2 == 0 else "A2B"
                    ss2 = "XSL" if lc % 2 == 0 else "XSLB"
                    dma("sync", xi2_tok[lc % 2][:], d_xiscr[row:row + 128, :], sem=sx2)
                    dma("sync", a0_sb[lc][:], d_ascr[row:row + 128, :], sem=sa2)
                    dma("sync", xsl_sb[lc][:], d_xsl[row:row + 128, :], sem=ss2)
                    wait("tensor", sx2, cnt[sx2])
                    for gg in range(4):
                        bnk = ridx2[0] % 2
                        wait("tensor", "FFV", max(0, ffb[0] - 1))
                        wait("tensor", "TEV", cnt["TEV"] - 1)
                        for j in range(8):
                            f = gg * 8 + j
                            op("tensor", lambda e, lc=lc, f=f, bnk=bnk, j=j: e.matmul(
                                pb_t[bnk][:, j * 128:(j + 1) * 128],
                                xi2_tok[lc % 2][:, f * 128:(f + 1) * 128], c_id[:],
                                start=(j == 0), stop=(j == 7), is_transpose=True),
                                sem=("TPS" if j == 7 else None))
                        wait("vector", "TPS", cnt["TPS"])
                        op("vector", lambda e, gg=gg, bnk=bnk, lc=lc: e.tensor_copy(
                            out=xi2_fm[:, gg * 8:(gg + 1) * 8, lc * 128:(lc + 1) * 128],
                            in_=pb_t[bnk][:]), sem="TEV")
                        ridx2[0] += 1
                    drain("tensor", sem="TXF")
                    b = lc % 2
                    wait("tensor", "TEV", cnt["TEV"])
                    wait("tensor", "ER", cnt["ER"])
                    wait("tensor", "FFV", max(0, ffb[0] - 1))
                    for k in range(KCH):
                        op("tensor", lambda e, lc=lc, k=k, b=b: e.matmul(
                            ps_pj[b][:], xi2_fm[:, k, lc * 128:(lc + 1) * 128],
                            wiT[:, k, :], start=(k == 0), stop=(k == KCH - 1)),
                            sem=("FPB" if k == KCH - 1 else None))
                    wait("vector", "FPB", cnt["FPB"])
                    t_idx = lt * 4 + lc
                    op("vector", lambda e, lc=lc, b=b, t_idx=t_idx: e.tensor_scalar_mul(
                        out=ffn_sb[lc][:], in0=ps_pj[b][:],
                        scalar1=sxw[:, t_idx:t_idx + 1]), sem="FFV")
                    ffb[0] += 1
            wait("tensor", sq_, cnt[sq_])
            def emit_qk(i, mc):
                b = i % 2
                wait("tensor", "ER", i - 1)
                if hh == 1 and mc < 2:
                    wait("tensor", "FFV", ffb[0])
                for d in range(2):
                    op("tensor", lambda e, d=d, mc=mc, b=b, lt=lt: e.matmul(
                        ps_pj[b][:], K_res[:, d, mc * 128:(mc + 1) * 128],
                        q2_sb[lt % 2][:, d, :], start=(d == 0), stop=(d == 1)),
                        sem=("SR" if d == 1 else None))
                wait("scalar", "SR", i + 1)
                wait("scalar", "EF", i - 2)
                op("scalar", lambda e, b=b, e3=i % 3: e.activation(
                    out=expS[e3][:], in_=ps_pj[b][:], func=ACT.Exp,
                    bias=eb_exp[:]), sem="ER")

            def emit_pv(i, mc):
                e3 = i % 3
                wait("tensor", "ER", i + 1)
                for lc in range(4):
                    op("tensor", lambda e, lc=lc, e3=e3, mc=mc: e.matmul(
                        ps_o[lc][:, 0:257], expS[e3][:, lc * 128:(lc + 1) * 128],
                        V_res[:, mc, :], start=(mc == 0), stop=(mc == MCN - 1)),
                        sem=("EF" if lc == 3 else None))

            i0 = qk_i[0]
            qk_i[0] += MCN
            wait("tensor", "OF", 4 * g)
            emit_qk(i0, 0)
            for mc in range(1, MCN):
                emit_qk(i0 + mc, mc)
                emit_pv(i0 + mc - 1, mc - 1)
            emit_pv(i0 + MCN - 1, MCN - 1)
            drain("tensor", sem="QF")
            for lc in range(4):
                wait("vector", "EF", 64 * (g + 1))
                wait("vector", "OF", cnt["OF"])
                op("vector", lambda e, lc=lc: e.reciprocal(
                    out=rec_sb[:], in_=ps_o[lc][:, 256:257]), sem="VSF")
                selfwait("vector")
                if hh == 0:
                    sad = "ASD" if lc % 2 == 0 else "ASDB"
                    wait("vector", sad, cnt[sad])
                    op("vector", lambda e, lc=lc: e.tensor_scalar_mul(
                        out=a0_sb[lc][:], in0=ps_o[lc][:, 0:256],
                        scalar1=rec_sb[:]), sem="OF")
                    wait("sync", "OF", cnt["OF"])
                    row = lt * 512 + lc * 128
                    dma("sync", d_ascr[row:row + 128, :], a0_sb[lc][:], sem=sad)
                else:
                    op("vector", lambda e, lc=lc: e.tensor_scalar_mul(
                        out=attn_f[lc][:, 256:512], in0=ps_o[lc][:, 0:256],
                        scalar1=rec_sb[:]), sem="OF")
                    wait("vector", "A2" if lc % 2 == 0 else "A2B",
                         cnt["A2" if lc % 2 == 0 else "A2B"])
                    wait("vector", "XSL" if lc % 2 == 0 else "XSLB",
                         cnt["XSL" if lc % 2 == 0 else "XSLB"])
                    op("vector", lambda e, lc=lc: e.tensor_copy(
                        out=attn_f[lc][:, 0:256], in_=a0_sb[lc][:]), sem="VSF")
                    selfwait("vector")
                    op("vector", lambda e, lc=lc: e.tensor_add(
                        out=tadd_sb[lc][:], in0=attn_f[lc][:], in1=ffn_sb[lc][:]),
                        sem="VSF")
                    selfwait("vector")
                    wait("vector", "OUT", 16 * max(0, cnt["VOUT"] - lc))
                    op("vector", lambda e, lc=lc: e.tensor_add(
                        out=out_sb[lc][:], in0=tadd_sb[lc][:], in1=xsl_sb[lc][:]),
                        sem="VOUT")
                    wait("sync", "VOUT", cnt["VOUT"])
                    row = lt * 512 + lc * 128
                    dma("sync", d_out[row:row + 128, :], out_sb[lc][:], sem="OUT")
            glt[0] += 1

    p2_head(0)
    flush("ASD", "ASDB", "QLD", "QLDB", "KLD")
    barrier()
    p2_head(1)
    wait_cur("sync", "OUT")
    wait_cur("sync", "ASD")
    if dev:
        op("vector", lambda e: e.memset(st_bn[:, 0:1], 0.0))
        drain("vector", sem="V0")
        wait_cur("sync", "V0")
        dma("sync", d_dbg[:, 0:64], sxw[:], sem="OUT", slow=True)
        dma("sync", d_dbg[:, 64:65], s_wcol[:], sem="OUT", slow=True)
        dma("sync", d_dbg[:, 65:66], r_wcol[:], sem="OUT", slow=True)
        dma("sync", d_dbg[:, 66:67], eb_exp[:], sem="OUT", slow=True)
        dma("sync", d_dbg[0:1, 67:68], qnmax, sem="OUT", slow=True)
        dma("sync", d_dbg[0:1, 68:69], knmax, sem="OUT", slow=True)
        wait_cur("sync", "OUT")

    for role, semn, val in deferred_waits:
        assert val <= cnt[semn], ("unsatisfiable wait", role, semn, val, cnt[semn])
    with nc.Block() as block:
        for role in ROLES:
            getattr(block, role)(lambda eng, role=role: [f(eng) for f in plan[role]])
    return nc


def make_in_maps(inputs, ncore=8):
    cst = host_constants()
    x = np.ascontiguousarray(np.asarray(inputs["L0_attn_x"], np.float32).reshape(L, DM))
    wq = np.asarray(inputs["L0_attn_Wq"], np.float32)
    wk = np.asarray(inputs["L0_attn_Wk"], np.float32)
    wv = np.asarray(inputs["L0_attn_Wv"], np.float32)
    bq = np.asarray(inputs["L0_attn_bq"], np.float32)
    bk = np.asarray(inputs["L0_attn_bk"], np.float32)
    bv = np.asarray(inputs["L0_attn_bv"], np.float32)
    bw = np.asarray(inputs["L0_attn_bias_weight"], np.float32)
    wf = np.ascontiguousarray(np.asarray(inputs["L0_ffn_weight"], np.float32))
    maps = []
    for c in range(ncore):
        sl = slice(FS * c, FS * (c + 1))
        m = dict(
            x=x,
            xsl=np.ascontiguousarray(x[:, sl]),
            wq=np.ascontiguousarray(wq[sl]),
            wk=np.ascontiguousarray(wk[sl]),
            wv=np.ascontiguousarray(wv[sl]),
            wf=wf,
            wfs=np.ascontiguousarray(wf[sl]),
            bq=np.ascontiguousarray(bq[sl]).reshape(1, FS),
            bk=np.ascontiguousarray(bk[sl]).reshape(1, FS),
            bv=np.ascontiguousarray(bv[sl]).reshape(1, FS),
            bwT=np.ascontiguousarray(bw[2 * c:2 * c + 2].T),
            **cst,
        )
        maps.append(m)
    return maps


def assemble(results):
    return np.concatenate([np.asarray(r["out"]) for r in results],
                          axis=1).reshape(1, L, DM)


from concourse.bass_utils import run_bass_kernel_spmd

_NC_CACHE = {}


def kernel(**inputs):
    maps = make_in_maps(inputs, ncore=8)
    if "nc" not in _NC_CACHE:
        _NC_CACHE["nc"] = build(dev=False)
    nc = _NC_CACHE["nc"]
    res = run_bass_kernel_spmd(nc, maps, core_ids=list(range(8)))
    return assemble(res.results)


# version 10
# speedup vs baseline: 1.5105x; 1.4303x over previous
import math
import numpy as np
import ml_dtypes
import concourse.bass as bass
import concourse.mybir as mybir

F32 = mybir.dt.float32
BF16 = mybir.dt.bfloat16
I32 = mybir.dt.int32
A = mybir.AluOpType
AX = mybir.AxisListType.X
ACT = mybir.ActivationFunctionType
BF = ml_dtypes.bfloat16

L = 8192
DM = 4096
FS = 512
HD = 256
NT = 64
NMAC = 32
NLT = 16
MCN = 64
KCH = 32
EPS = 1e-5
CLIP1 = 1.4999
SHIFT = 256.5
ROLES = ("sync", "vector", "scalar", "tensor", "gpsimd")
KB = 1024


def _build_tables():
    g = [1.0, 1.0, 1.0]
    n = 8
    masks = sorted(range(n), key=lambda m: (bin(m).count("1"), m))
    inv = {m: i for i, m in enumerate(masks)}
    C = np.zeros((n, n, n), dtype=np.float32)
    for i, a in enumerate(masks):
        for j, b in enumerate(masks):
            sw, at_ = (0, a >> 1)
            while at_:
                sw += bin(at_ & b).count("1")
                at_ >>= 1
            s = -1.0 if sw & 1 else 1.0
            for t in range(3):
                if a >> t & 1 and b >> t & 1:
                    s *= float(g[t])
            C[i, j, inv[a ^ b]] = s
    grades = np.array([bin(m).count("1") for m in masks])
    rev = np.where(grades * (grades - 1) // 2 % 2 == 1, -1.0, 1.0).astype(np.float32)
    met = np.array([float(np.prod([g[t] for t in range(3) if m >> t & 1] + [1.0]))
                    for m in masks], dtype=np.float32)
    return C, rev, met


def host_constants():
    C, rev, met = _build_tables()
    biv = C[:, :, [4, 5, 6]]
    scale = 1.0 / math.sqrt(HD)
    bivf = np.ascontiguousarray(biv.transpose(2, 0, 1).reshape(3, 64)).astype(np.float32)
    rconst = np.tile(np.repeat(rev, 8)[None, :], (2, 1)).astype(np.float32)
    dconst = np.tile((scale * np.diag(rev * met)).reshape(1, 64), (2, 1)).astype(np.float32)
    return dict(
        bivf=bivf, rconst=rconst, dconst=dconst,
        iden=np.eye(128).astype(BF),
        onesf=np.ones((1, 128), np.float32),
        onesb=np.ones((128, 1)).astype(BF),
        onesrb=np.ones((1, 128)).astype(BF),
        onescf=np.ones((128, 1), np.float32),
    )


def build(dev=False):
    nc = bass.Bass("TRN2", target_bir_lowering=False, debug=False)

    d_x = nc.dram_tensor("x", [L, DM], F32, kind="ExternalInput").ap()
    d_xsl = nc.dram_tensor("xsl", [L, FS], F32, kind="ExternalInput").ap()
    d_wq = nc.dram_tensor("wq", [FS, DM], F32, kind="ExternalInput").ap()
    d_wk = nc.dram_tensor("wk", [FS, DM], F32, kind="ExternalInput").ap()
    d_wv = nc.dram_tensor("wv", [FS, DM], F32, kind="ExternalInput").ap()
    d_wf = nc.dram_tensor("wf", [DM, DM], F32, kind="ExternalInput").ap()
    d_wfs = nc.dram_tensor("wfs", [FS, DM], F32, kind="ExternalInput").ap()
    d_bq = nc.dram_tensor("bq", [1, FS], F32, kind="ExternalInput").ap()
    d_bk = nc.dram_tensor("bk", [1, FS], F32, kind="ExternalInput").ap()
    d_bv = nc.dram_tensor("bv", [1, FS], F32, kind="ExternalInput").ap()
    d_bwT = nc.dram_tensor("bwT", [3, 2], F32, kind="ExternalInput").ap()
    d_bivf = nc.dram_tensor("bivf", [3, 64], F32, kind="ExternalInput").ap()
    d_rc = nc.dram_tensor("rconst", [2, 64], F32, kind="ExternalInput").ap()
    d_dc = nc.dram_tensor("dconst", [2, 64], F32, kind="ExternalInput").ap()
    d_id = nc.dram_tensor("iden", [128, 128], BF16, kind="ExternalInput").ap()
    d_onesf = nc.dram_tensor("onesf", [1, 128], F32, kind="ExternalInput").ap()
    d_onesb = nc.dram_tensor("onesb", [128, 1], BF16, kind="ExternalInput").ap()
    d_onesrb = nc.dram_tensor("onesrb", [1, 128], BF16, kind="ExternalInput").ap()
    d_onescf = nc.dram_tensor("onescf", [128, 1], F32, kind="ExternalInput").ap()
    d_out = nc.dram_tensor("out", [L, FS], F32, kind="ExternalOutput").ap()

    skind = "ExternalOutput" if dev else "Internal"
    d_kscr = nc.dram_tensor("k_scr", [FS, L], BF16, kind=skind).ap()
    d_vscr = nc.dram_tensor("v_scr", [L, FS], BF16, kind=skind).ap()
    d_qscr = nc.dram_tensor("q_scr", [FS, L], BF16, kind=skind).ap()
    d_xiscr = nc.dram_tensor("xi_scr", [L, DM], BF16, kind=skind).ap()
    d_wiscr = nc.dram_tensor("wi_scr", [128, KCH * FS], BF16, kind=skind).ap()
    d_ascr = nc.dram_tensor("a_scr", [L, HD], BF16, kind=skind).ap()
    if dev:
        d_dbg = nc.dram_tensor("dbg", [128, 80], F32, kind="ExternalOutput").ap()

    ARENA_PP = 207 * KB + 512
    arena = nc.sbuf_tensor("arena", [128, ARENA_PP], mybir.dt.uint8).__enter__()
    base = nc.lookup_mloc(arena).addr

    def at(name, shape, dtype, off):
        end = off + int(np.prod(shape[1:])) * mybir.dt.size(dtype)
        assert end <= ARENA_PP, (name, end)
        return nc.alloc_sbuf_tensor_at(name, shape, dtype, offset=base + off)

    wqT = at("wqT", [128, KCH, FS], BF16, 0)
    wkT = at("wkT", [128, KCH, FS], BF16, 32 * KB)
    wvT = at("wvT", [128, KCH, FS], BF16, 64 * KB)
    xst = [at(f"xst{i}", [128, DM], F32, 96 * KB + 16 * KB * i) for i in range(2)]
    xn_tok = [at("xntok", [128, DM], BF16, 128 * KB)] * 2
    xn_fm2 = [at(f"xnfm{i}", [128, KCH, 256], BF16, 136 * KB + 16 * KB * i) for i in range(2)]
    xi_tok = [at("xitok", [128, DM], BF16, 168 * KB)] * 2
    RS = 176 * KB
    cB = [at(f"cB{i}", [128, 1024], mybir.dt.int16, RS + 2 * KB * i) for i in range(4)]
    cC = [at(f"cC{i}", [128, 1024], BF16, RS + 8 * KB + 2 * KB * i) for i in range(4)]
    cF = [at(f"cF{i}", [128, 1024], BF16, RS + 16 * KB + 2 * KB * i) for i in range(2)]
    k_sb = at("ksb", [128, 4, 256], BF16, 196 * KB)
    v_sb = at("vsb", [128, 2, FS], BF16, 198 * KB)
    q_sb = at("qsb", [128, 4, 256], BF16, 200 * KB)
    wstage = [xst[0], xst[1]]
    wcast = at("wcast", [128, DM], BF16, 128 * KB)
    wcast2 = at("wcast2", [128, DM], BF16, 136 * KB)
    wiT_b = at("wiTb", [128, KCH, FS], BF16, 0)
    wiT = at("wiT", [128, KCH, FS], BF16, 0)
    K_res = at("Kres", [128, 2, L], BF16, 32 * KB)
    V_res = at("Vres", [128, MCN, HD + 1], BF16, 64 * KB)
    xi2_tok = [at(f"xi2t{i}", [128, DM], BF16, 98 * KB + 8 * KB * i) for i in range(2)]
    xi2_fm = at("xi2fm", [128, KCH, FS], BF16, 114 * KB)
    q2_sb = [at(f"q2sb{i}", [128, 2, FS], BF16, 146 * KB + 2 * KB * i) for i in range(2)]
    expS = [at(f"expS{i}", [128, 512], BF16, 150 * KB + KB * i) for i in range(3)]
    attn_f = [at(f"attnf{i}", [128, FS], BF16, 166 * KB + KB * i) for i in range(4)]
    a0_sb = [at(f"a0sb{i}", [128, HD], BF16, 170 * KB + 512 * i) for i in range(4)]
    xsl_sb = [at(f"xslb{i}", [128, FS], F32, 172 * KB + 2 * KB * i) for i in range(4)]
    ffn_sb = [at(f"ffnsb{i}", [128, FS], BF16, 180 * KB + KB * i) for i in range(4)]
    tadd_sb = [at(f"tadd{i}", [128, FS], F32, 184 * KB + 2 * KB * i) for i in range(4)]
    out_sb = [at(f"outsb{i}", [128, FS], F32, 192 * KB + 2 * KB * i) for i in range(4)]

    import contextlib
    es = contextlib.ExitStack()
    _pk = [202 * KB]

    def sb(n, s, d):
        off = _pk[0]
        t = at(n, s, d, off)
        sz = int(np.prod(s[1:])) * mybir.dt.size(d)
        _pk[0] = off + ((sz + 63) // 64) * 64
        return t
    c_id = sb("c_id", [128, 128], BF16)
    c_onesf = sb("c_onesf", [1, 128], F32)
    c_onesb = sb("c_onesb", [128, 1], BF16)
    c_onesrb = sb("c_onesrb", [1, 128], BF16)
    b_v_bf = sb("b_v_bf", [1, FS], BF16)
    c_onescf = sb("c_onescf", [128, 1], F32)
    c_bivf = at("c_bivf", [3, 64], F32, RS + 6 * KB)
    c_rc = at("c_rc", [2, 64], F32, RS + 6 * KB + 256)
    c_dc = at("c_dc", [2, 64], F32, RS + 6 * KB + 512)
    c_bwT = at("c_bwT", [3, 2], F32, RS + 6 * KB + 768)
    c_T = sb("c_T", [128, 256], BF16)
    c_A = at("c_A", [2, 64], F32, RS + 7 * KB)
    c_Abf = at("c_Abf", [2, 64], BF16, RS + 7 * KB + 256)
    b_q = at("b_q", [1, FS], F32, RS)
    b_k = at("b_k", [1, FS], F32, RS + 2 * KB)
    b_v = at("b_v", [1, FS], F32, RS + 4 * KB)
    bkT = sb("bkT", [128, 4], F32)
    bqT = sb("bqT", [128, 4], F32)
    bqTb = sb("bqTb", [128, 4], BF16)
    bqe = sb("bqe", [128, 4], F32)
    st_all = sb("st_all", [128, 84], F32)
    _st = [0]

    def _stc(n):
        a, _st[0] = _st[0], _st[0] + n
        return st_all[:, a:a + n]

    st_bn = _stc(48)
    st_mv = _stc(4)
    st_varc = _stc(2)
    st_rstd = _stc(2)
    st_nrstd = _stc(2)
    st_nmrs = _stc(2)
    st_mrs = _stc(2)
    st_xmax = _stc(2)
    st_xmin = _stc(2)
    st_a = _stc(2)
    st_b = _stc(2)
    st_amax = _stc(2)
    st_sx = _stc(2)
    st_rx = _stc(2)
    st_rr = _stc(2)
    st_bb = _stc(2)
    st_bb1 = _stc(2)
    c_2565 = sb("c_2565", [128, 1], F32)
    c_m257 = sb("c_m257", [128, 1], F32)
    c_eps = sb("c_eps", [128, 1], F32)
    sxw = sb("sxw", [128, NT], F32)
    s_wcol = sb("s_wcol", [128, 1], F32)
    r_wcol = sb("r_wcol", [128, 1], F32)
    wfsum = sb("wfsum", [128, 1], F32)
    wfpart = sb("wfpart", [128, 1], F32)
    sm_all = sb("sm_all", [1, 16], F32)
    s_sw = sm_all[0:1, 0:1]
    s_rw = sm_all[0:1, 1:2]
    qnmax = sm_all[0:1, 2:3]
    knmax = sm_all[0:1, 3:4]
    nrm_red = sm_all[0:1, 4:6]
    cc_q = sm_all[0:1, 6:7]
    cc_k = sm_all[0:1, 7:8]
    cc_c = sm_all[0:1, 8:9]
    cc_c2 = sm_all[0:1, 9:10]
    eb_exp = sb("eb_exp", [128, 1], F32)
    rec_sb = sb("rec_sb", [128, 1], F32)

    pb_f = [es.enter_context(nc.psum_tensor(f"pbf{i}", [128, 512], F32)) for i in range(6)]
    pb_t = [es.enter_context(nc.psum_tensor(f"pbt{i}", [128, 1024], BF16)) for i in range(2)]
    ps_pj = pb_f[0:2]
    ps_o = pb_f[2:6]
    ps_nm = pb_f[2]
    ps_m0 = pb_f[3]

    sem_names = [
        "BAR", "C0", "T0", "V0", "S0", "G0", "W0", "W1",
        "LDX", "LDXB", "W0B", "W1B", "DMOB", "XI2B", "A2B", "XSLB", "QLDB", "VS", "GS", "SSM", "SXN",
        "VW", "SC1", "SC2", "VSS", "GX", "DMO",
        "TPS", "TEV", "PRJ", "PBK", "PEV", "PEVK", "SQN", "TNM", "NRD", "KVQ",
        "KLD", "QLD", "QF", "SR", "ER", "EF", "OF",
        "A2", "XSL", "XI2", "TXF", "FPB", "FFV", "VOUT", "OUT", "ASD", "ASDB",
        "CB1", "CB2", "CB3", "SSQ", "VRS", "GE1", "VSF", "SSF", "GSF", "VT", "NRM",
    ]
    S = {n: es.enter_context(nc.semaphore(n)) for n in sem_names}

    plan = {r: [] for r in ROLES}
    cnt = {n: 0 for n in sem_names}

    def op(role, fn, sem=None, n=1):
        if sem is not None:
            cnt[sem] += n
            plan[role].append(lambda eng, fn=fn, s=S[sem], m=n: fn(eng).then_inc(s, m))
        else:
            plan[role].append(lambda eng, fn=fn: fn(eng))

    def dma(role, out, in_, sem=None, slow=False):
        op(role, lambda e, o=out, i=in_, sl=slow: e.dma_start(
            out=o, in_=i, allow_slow_non_contiguous=sl), sem=sem, n=16)

    deferred_waits = []

    def wait(role, sem, val):
        deferred_waits.append((role, sem, val))
        if val > 0:
            plan[role].append(lambda eng, s=S[sem], v=val: eng.wait_ge(s, v))

    def wait_cur(role, sem):
        wait(role, sem, cnt[sem])

    def drain(role, sem=None, n=1):
        op(role, lambda e: e.drain(), sem=sem, n=n)

    SELF = {"vector": "VSF", "scalar": "SSF", "gpsimd": "GSF"}

    def selfwait(role):
        wait(role, SELF[role], cnt[SELF[role]])

    def chop(role, fn, sem=None):
        selfwait(role)
        op(role, fn, sem=(sem if sem is not None else SELF[role]))

    nbar = [0]

    def flush(*sems, role="sync"):
        for s_ in sems:
            wait_cur(role, s_)

    def barrier():
        nbar[0] += 1
        k = nbar[0] * len(ROLES)
        cnt["BAR"] += len(ROLES)
        for r in ROLES:
            def f(eng, v=k):
                eng.drain().then_inc(S["BAR"], 1)
                eng.wait_ge(S["BAR"], v)
            plan[r].append(f)

    for dst, src in [(c_id, d_id), (c_onesf, d_onesf), (c_onesb, d_onesb),
                     (c_onescf, d_onescf), (c_bivf, d_bivf), (c_rc, d_rc),
                     (c_dc, d_dc), (c_bwT, d_bwT), (b_q, d_bq), (b_k, d_bk),
                     (b_v, d_bv), (c_onesrb, d_onesrb)]:
        dma("sync", dst[:], src, sem="C0")
    op("vector", lambda e: e.memset(c_2565[:], SHIFT))
    op("vector", lambda e: e.memset(c_m257[:], -257.0))
    op("vector", lambda e: e.memset(c_eps[:], EPS))
    op("vector", lambda e: e.memset(qnmax, 0.0))
    op("vector", lambda e: e.memset(knmax, 0.0))
    op("vector", lambda e: e.memset(wfsum[:], 0.0))
    op("vector", lambda e: e.memset(c_T[:], 0.0))
    flush("C0")
    barrier()

    for f in range(4):
        dma("sync", bkT[:, f:f + 1], b_k[0:1, f * 128:(f + 1) * 128], sem="C0")
        dma("sync", bqT[:, f:f + 1], b_q[0:1, f * 128:(f + 1) * 128], sem="C0")
    op("tensor", lambda e: e.matmul(ps_m0[0:2, 0:64], c_bwT[:], c_bivf[:],
                                    start=True, stop=True), sem="T0")
    wait_cur("vector", "T0")
    op("vector", lambda e: e.tensor_mul(out=c_A[:], in0=ps_m0[0:2, 0:64], in1=c_rc[:]))
    drain("vector")
    op("vector", lambda e: e.tensor_add(out=c_A[:], in0=c_A[:], in1=c_dc[:]))
    drain("vector")
    op("vector", lambda e: e.tensor_copy(out=c_Abf[:], in_=c_A[:]))
    wait_cur("vector", "C0")
    op("vector", lambda e: e.tensor_copy(out=bqTb[:], in_=bqT[:]))
    drain("vector", sem="V0")
    wait_cur("sync", "V0")
    for hh in range(2):
        for dd in range(16):
            dma("sync", c_T[8 * dd:8 * dd + 8, hh * 128 + 8 * dd:hh * 128 + 8 * dd + 8],
                c_Abf[hh:hh + 1, 0:64], sem="C0")
    op("vector", lambda e: e.tensor_copy(out=b_v_bf[:], in_=b_v[:]))
    drain("vector", sem="V0")
    wait_cur("tensor", "C0")
    wait_cur("tensor", "V0")
    for j in range(4):
        if j >= 2:
            wait("tensor", "S0", cnt["S0"] - 1)
        op("tensor", lambda e, j=j: e.matmul(
            ps_pj[j % 2][:, 0:1], c_T[:, (j // 2) * 128:(j // 2) * 128 + 128],
            bqTb[:, j:j + 1], start=True, stop=True), sem="T0")
        wait_cur("scalar", "T0")
        op("scalar", lambda e, j=j: e.copy(out=bqe[:, j:j + 1], in_=ps_pj[j % 2][:, 0:1]))
        drain("scalar", sem="S0")
    flush("C0")
    barrier()

    def transp32(src_bf, dst, rcslice, tag_rc):
        for g in range(4):
            bnk = g % 2
            wait("tensor", "TEV", cnt["TEV"] - 1)
            for j in range(8):
                f = g * 8 + j
                op("tensor", lambda e, f=f, bnk=bnk, j=j, s=src_bf: e.matmul(
                    pb_t[bnk][:, j * 128:(j + 1) * 128], s[:, f * 128:(f + 1) * 128],
                    c_id[:], start=(j == 0), stop=(j == 7), is_transpose=True),
                    sem=("TPS" if j == 7 else None))
            wait_cur("vector", "TPS")
            op("vector", lambda e, g=g, bnk=bnk, d=dst, rcs=rcslice: e.tensor_copy(
                out=d[:, g * 8:(g + 1) * 8, rcs], in_=pb_t[bnk][:]), sem="TEV")

    def w_prep_plain(dsrc, wT):
        for rc in range(4):
            par = rc % 2
            sw = "W0" if par == 0 else "W0B"
            wait("sync", "V0", cnt["V0"] - 1)
            dma("sync", wstage[par][:], dsrc[rc * 128:(rc + 1) * 128, :], sem=sw)
            wait_cur("vector", sw)
            wait("vector", "TEV", cnt["TEV"])
            op("vector", lambda e, par=par: e.tensor_copy(out=wcast[:], in_=wstage[par][:]))
            drain("vector", sem="V0")
            wait_cur("tensor", "V0")
            transp32(wcast, wT, slice(rc * 128, (rc + 1) * 128), rc)

    w_prep_plain(d_wk, wkT)
    w_prep_plain(d_wv, wvT)
    flush("W0", "W0B")
    barrier()

    for ch in range(32):
        par = ch % 2
        sw = "W0" if par == 0 else "W0B"
        wait("sync", "V0", cnt["V0"] - 1)
        dma("sync", wstage[par][:], d_wf[ch * 128:(ch + 1) * 128, :], sem=sw)
        wait("vector", sw, cnt[sw])
        wait("vector", "V0", cnt["V0"])
        op("vector", lambda e, par=par: e.tensor_reduce(
            out=wfpart[:], in_=wstage[par][:], axis=AX, op=A.add,
            apply_absolute_value=True), sem="VSF")
        selfwait("vector")
        op("vector", lambda e: e.tensor_add(out=wfsum[:], in0=wfsum[:], in1=wfpart[:]),
            sem="V0")
    wait_cur("tensor", "V0")
    op("tensor", lambda e: e.matmul(ps_m0[0:1, 0:1], c_onescf[:], wfsum[:],
                                    start=True, stop=True), sem="T0")
    wait_cur("vector", "T0")
    op("vector", lambda e: e.tensor_scalar_mul(out=s_sw, in0=ps_m0[0:1, 0:1],
                                               scalar1=1.0 / (DM * DM)))
    drain("vector")
    op("vector", lambda e: e.tensor_scalar_add(out=s_rw, in0=s_sw, scalar1=1e-8))
    drain("vector")
    op("vector", lambda e: e.reciprocal(out=s_rw, in_=s_rw))
    drain("vector", sem="V0")
    wait_cur("tensor", "V0")
    op("tensor", lambda e: e.matmul(ps_m0[:, 0:1], c_onesf[:], s_sw,
                                    start=True, stop=True))
    op("tensor", lambda e: e.matmul(pb_f[4][:, 0:1], c_onesf[:], s_rw,
                                    start=True, stop=True), sem="T0")
    wait_cur("scalar", "T0")
    op("scalar", lambda e: e.copy(out=s_wcol[:], in_=ps_m0[:, 0:1]))
    op("scalar", lambda e: e.copy(out=r_wcol[:], in_=pb_f[4][:, 0:1]))
    drain("scalar", sem="S0")
    flush("W0", "W0B")
    barrier()

    jq = [0]

    def ch_S1(j, p, q, vw):
        wait("scalar", "VW", vw)
        wait("scalar", "GX", j - 3)
        op("scalar", lambda e, p=p, q=q, j=j: e.copy(
            out=cB[j % 4][:], in_=xst[p][:, q * 1024:(q + 1) * 1024]), sem="SC1")

    def ch_S2(j):
        wait("scalar", "SC1", j + 1)
        wait("scalar", "VSS", j - 3)
        op("scalar", lambda e, j=j: e.activation(
            out=cC[j % 4][:], in_=cB[j % 4][:], func=ACT.Identity,
            bias=c_m257[:]), sem="SC2")

    def ch_E(j, p, q):
        wait("gpsimd", "SC2", j + 1)
        op("gpsimd", lambda e, p=p, q=q, j=j: e.tensor_tensor(
            out=xst[p][:, q * 1024:(q + 1) * 1024],
            in0=xst[p][:, q * 1024:(q + 1) * 1024],
            in1=cC[j % 4][:], op=A.subtract), sem="GE1")

    def ch_C12(j, p, q):
        wait("vector", "GE1", j + 1)
        wait("vector", "VSS", j - 1)
        op("vector", lambda e, p=p, q=q, j=j: e.tensor_single_scalar(
            out=cF[j % 2][:], in_=xst[p][:, q * 1024:(q + 1) * 1024],
            scalar=257.0, op=A.is_ge), sem="VT")
        op("vector", lambda e, p=p, q=q, j=j: e.tensor_single_scalar(
            out=cB[j % 4][:].bitcast(BF16), in_=xst[p][:, q * 1024:(q + 1) * 1024],
            scalar=258.0, op=A.is_ge), sem="VT")

    def ch_X1(j, dst_ap):
        wait("vector", "VT", 2 * (j + 1))
        op("vector", lambda e, j=j, d=dst_ap: e.tensor_tensor(
            out=d, in0=cC[j % 4][:], in1=cF[j % 2][:], op=A.add), sem="VSS")

    def ch_X2(j, dst_ap):
        wait("gpsimd", "VSS", j + 1)
        op("gpsimd", lambda e, j=j, d=dst_ap: e.tensor_tensor(
            out=d, in0=d, in1=cB[j % 4][:].bitcast(BF16), op=A.add), sem="GX")

    tev_hist = [cnt["TEV"]]
    for rc in range(4):
        par = rc % 2
        sw = "W1" if par == 0 else "W1B"
        if rc >= 2:
            wait("sync", "VT", 2 * (4 * (rc - 2) + 4))
        dma("sync", wstage[par][:], d_wfs[rc * 128:(rc + 1) * 128, :], sem=sw)
        wait("vector", sw, cnt[sw])
        selfwait("vector")
        op("vector", lambda e, par=par: e.tensor_scalar(
            out=wstage[par][:], in0=wstage[par][:], scalar1=r_wcol[:],
            scalar2=CLIP1, op0=A.mult, op1=A.min), sem="VSF")
        selfwait("vector")
        op("vector", lambda e, par=par: e.tensor_scalar(
            out=wstage[par][:], in0=wstage[par][:], scalar1=-CLIP1,
            scalar2=SHIFT, op0=A.max, op1=A.add), sem="VW")
        vw_now = cnt["VW"]
        for q in range(4):
            j = jq[0]
            jq[0] += 1
            ch_S1(j, par, q, vw_now)
            ch_S2(j)
            ch_E(j, par, q)
            ch_C12(j, par, q)
            if q == 0:
                wait("vector", "TEV", tev_hist[-1])
            ch_X1(j, wcast[:, q * 1024:(q + 1) * 1024])
            ch_X2(j, wcast[:, q * 1024:(q + 1) * 1024])
        wait_cur("tensor", "GX")
        transp32(wcast, wiT_b, slice(rc * 128, (rc + 1) * 128), rc)
        tev_hist.append(cnt["TEV"])
    wait_cur("sync", "TEV")
    dma("sync", d_wiscr.rearrange("p (a b) -> p a b", a=KCH), wiT_b[:], sem="W1")
    flush("W1", "W1B")
    barrier()

    for rc in range(4):
        hh = rc // 2
        par = rc % 2
        sw = "W0" if par == 0 else "W0B"
        wait("sync", "V0", cnt["V0"] - 1)
        dma("sync", wstage[par][:], d_wq[rc * 128:(rc + 1) * 128, :], sem=sw)
        wait_cur("vector", sw)
        wait("vector", "PEV", cnt["PEV"])
        op("vector", lambda e, par=par: e.tensor_copy(out=wcast[:], in_=wstage[par][:]))
        drain("vector", sem="V0")
        wait_cur("tensor", "V0")
        wait_cur("scalar", "TEV")
        for g in range(8):
            bnk = g % 2
            wait("tensor", "PEV", cnt["PBK"] - 1)
            op("tensor", lambda e, g=g, bnk=bnk, hh=hh: e.matmul(
                ps_pj[bnk][:], c_T[:, hh * 128:(hh + 1) * 128],
                wcast[:, g * 512:(g + 1) * 512], start=True, stop=True), sem="PBK")
            wait_cur("scalar", "PBK")
            op("scalar", lambda e, g=g, bnk=bnk: e.copy(
                out=wcast2[:, g * 512:(g + 1) * 512], in_=ps_pj[bnk][:]))
            drain("scalar", sem="PEV")
        wait_cur("tensor", "PEV")
        transp32(wcast2, wqT, slice(rc * 128, (rc + 1) * 128), rc)
    flush("W0", "W0B")
    barrier()

    TEV0 = cnt["TEV"]
    TPS0 = cnt["TPS"]
    SSM0 = cnt["SSM"]
    VS0 = cnt["VS"]
    SSQ0 = cnt["SSQ"]
    VRS0 = cnt["VRS"]
    SXN0 = cnt["SXN"]
    VW0 = cnt["VW"]
    jb = jq[0]

    def p1():
        LDXc = {}
        pev_hist = []
        pbk = [cnt["PBK"]]

        def em_load(u):
            if not (0 <= u < NT):
                return
            sl = "LDX" if u % 2 == 0 else "LDXB"
            if u >= 2:
                wait("sync", "VT", 2 * (jb + 4 * (u - 2) + 4))
            dma("sync", xst[u % 2][:], d_x[u * 128:(u + 1) * 128, :], sem=sl)
            LDXc[u] = cnt[sl]

        def em_stats(u):
            if not (0 <= u < NT):
                return
            p = u % 2
            sl = "LDX" if p == 0 else "LDXB"
            wait("vector", sl, LDXc[u])
            wait("vector", "VS", VS0 + 3 * u)
            wait("vector", "SSM", SSM0 + max(0, u - 1))
            for i in range(8):
                op("vector", lambda e, p=p, i=i: e.bn_stats(
                    out=st_bn[:, i * 6:(i + 1) * 6],
                    in_=xst[p][:, i * 512:(i + 1) * 512]), sem="VSF")
            op("vector", lambda e, p=p: e.reduce_max(
                out=st_xmax[:, p:p + 1], in_=xst[p][:], axis=AX), sem="VS")
            op("vector", lambda e, p=p: e.tensor_reduce(
                out=st_xmin[:, p:p + 1], in_=xst[p][:], axis=AX, op=A.min), sem="VS")
            selfwait("vector")
            op("vector", lambda e, p=p: e.bn_aggr(
                out=st_mv[:, 2 * p:2 * p + 2], in_=st_bn[:]), sem="VS")

        def em_smalls(u):
            if not (0 <= u < NT):
                return
            p = u % 2
            wait("scalar", "VS", VS0 + 3 * (u + 1))
            selfwait("scalar")
            op("scalar", lambda e, p=p: e.activation(
                out=st_varc[:, p:p + 1], in_=st_mv[:, 2 * p + 1:2 * p + 2],
                func=ACT.Relu), sem="SSF")
            selfwait("scalar")
            op("scalar", lambda e, p=p: e.activation(
                out=st_varc[:, p:p + 1], in_=st_varc[:, p:p + 1], func=ACT.Ln,
                bias=c_eps[:]), sem="SSF")
            selfwait("scalar")
            op("scalar", lambda e, p=p: e.activation(
                out=st_rstd[:, p:p + 1], in_=st_varc[:, p:p + 1], func=ACT.Exp,
                scale=-0.5), sem="SSF")
            selfwait("scalar")
            op("scalar", lambda e, p=p: e.mul(
                out=st_nrstd[:, p:p + 1], in_=st_rstd[:, p:p + 1], mul=-1.0),
                sem="SSF")
            selfwait("scalar")
            op("scalar", lambda e, p=p: e.activation(
                out=st_nmrs[:, p:p + 1], in_=st_mv[:, 2 * p:2 * p + 1],
                func=ACT.Identity, scale=st_nrstd[:, p:p + 1]), sem="SSF")
            selfwait("scalar")
            op("scalar", lambda e, p=p: e.mul(
                out=st_mrs[:, p:p + 1], in_=st_nmrs[:, p:p + 1], mul=-1.0), sem="SSF")
            op("scalar", lambda e, p=p: e.activation(
                out=st_a[:, p:p + 1], in_=st_xmax[:, p:p + 1], func=ACT.Identity,
                scale=st_rstd[:, p:p + 1], bias=st_nmrs[:, p:p + 1]), sem="SSF")
            selfwait("scalar")
            op("scalar", lambda e, p=p: e.activation(
                out=st_b[:, p:p + 1], in_=st_xmin[:, p:p + 1], func=ACT.Identity,
                scale=st_nrstd[:, p:p + 1], bias=st_mrs[:, p:p + 1]), sem="SSM")

        def em_tinies(u):
            p = u % 2
            wait("vector", "SSM", SSM0 + u + 1)
            op("vector", lambda e, p=p: e.tensor_max(
                out=st_amax[:, p:p + 1], in0=st_a[:, p:p + 1], in1=st_b[:, p:p + 1]),
                sem="VSF")
            selfwait("vector")
            op("vector", lambda e, p=p: e.tensor_scalar(
                out=st_sx[:, p:p + 1], in0=st_amax[:, p:p + 1],
                scalar1=1.0 / 127.0, scalar2=1e-8, op0=A.mult, op1=A.max), sem="VSF")
            selfwait("vector")
            op("vector", lambda e, p=p: e.reciprocal(
                out=st_rx[:, p:p + 1], in_=st_sx[:, p:p + 1]), sem="VSF")
            op("vector", lambda e, p=p, u=u: e.tensor_mul(
                out=sxw[:, u:u + 1], in0=st_sx[:, p:p + 1], in1=s_wcol[:]), sem="VSF")
            selfwait("vector")
            op("vector", lambda e, p=p: e.tensor_mul(
                out=st_rr[:, p:p + 1], in0=st_rstd[:, p:p + 1], in1=st_rx[:, p:p + 1]),
                sem="VSF")
            op("vector", lambda e, p=p: e.scalar_tensor_tensor(
                out=st_bb[:, p:p + 1], in0=st_nmrs[:, p:p + 1],
                scalar=st_rx[:, p:p + 1], in1=c_2565[:], op0=A.mult, op1=A.add),
                sem="VSF")

        def em_xn(u):
            p = u % 2
            wait("scalar", "TPS", TPS0 + 4 * u)
            selfwait("scalar")
            op("scalar", lambda e, p=p: e.activation(
                out=xn_tok[p][:], in_=xst[p][:], func=ACT.Identity,
                scale=st_rstd[:, p:p + 1], bias=st_nmrs[:, p:p + 1]), sem="SXN")

        def em_makew(u):
            p = u % 2
            wait("vector", "SXN", SXN0 + u + 1)
            selfwait("vector")
            op("vector", lambda e, p=p: e.tensor_scalar(
                out=xst[p][:], in0=xst[p][:], scalar1=st_rr[:, p:p + 1],
                scalar2=st_bb[:, p:p + 1], op0=A.mult, op1=A.add), sem="VW")

        def em_cmps(u):
            if not (0 <= u < NT):
                return
            p = u % 2
            js = [jb + 4 * u + q for q in range(4)]
            ch_C12(js[0], p, 0)
            ch_C12(js[1], p, 1)
            wait("vector", "DMO", 16 * u)
            ch_X1(js[0], xi_tok[0][:, 0:1024])
            ch_C12(js[2], p, 2)
            ch_X1(js[1], xi_tok[0][:, 1024:2048])
            ch_C12(js[3], p, 3)
            ch_X1(js[2], xi_tok[0][:, 2048:3072])
            ch_X1(js[3], xi_tok[0][:, 3072:4096])

        def em_transp(u, g0, g1):
            p = u % 2
            m = u // 2
            wait("tensor", "SXN", SXN0 + u + 1)
            for g in range(g0, g1):
                gt = 4 * u + g
                bnk = gt % 2
                wait("tensor", "TEV", TEV0 + gt - 1)
                for jx in range(8):
                    f = g * 8 + jx
                    op("tensor", lambda e, p=p, f=f, bnk=bnk, jx=jx: e.matmul(
                        pb_t[bnk][:, jx * 128:(jx + 1) * 128],
                        xn_tok[p][:, f * 128:(f + 1) * 128], c_id[:],
                        start=(jx == 0), stop=(jx == 7), is_transpose=True),
                        sem=("TPS" if jx == 7 else None))
                wait("scalar", "TPS", TPS0 + gt + 1)
                wait("scalar", "PRJ", max(0, m - 1))
                op("scalar", lambda e, g=g, bnk=bnk, half=u % 2, mp=m % 2: e.copy(
                    out=xn_fm2[mp][:, g * 8:(g + 1) * 8, half * 128:half * 128 + 128],
                    in_=pb_t[bnk][:]), sem="TEV")

        def em_proj_q(jm):
            jp = jm % 2
            wait("tensor", "TEV", TEV0 + 8 * (jm + 1))
            wait("scalar", "KVQ", 160 * jm)
            for ch in range(4):
                b = pbk[0] % 6
                if len(pev_hist) >= 6:
                    wait("tensor", *pev_hist[-6])
                for k in range(KCH):
                    op("tensor", lambda e, ch=ch, k=k, b=b, jp=jp: e.matmul(
                        pb_f[b][:, 0:256], wqT[:, k, ch * 128:(ch + 1) * 128],
                        xn_fm2[jp][:, k, :], start=(k == 0), stop=(k == KCH - 1)),
                        sem=("PBK" if k == KCH - 1 else None))
                pbk[0] += 1
                wait("scalar", "PBK", pbk[0])
                op("scalar", lambda e, ch=ch, b=b: e.activation(
                    out=q_sb[:, ch, :], in_=pb_f[b][:, 0:256], func=ACT.Identity,
                    bias=bqe[:, ch:ch + 1]), sem="PEV")
                pev_hist.append(("PEV", cnt["PEV"]))

        def em_proj_k(jm):
            jp = jm % 2
            for ch in range(4):
                b = pbk[0] % 6
                if len(pev_hist) >= 6:
                    wait("tensor", *pev_hist[-6])
                for k in range(KCH):
                    op("tensor", lambda e, ch=ch, k=k, b=b, jp=jp: e.matmul(
                        pb_f[b][:, 0:256], wkT[:, k, ch * 128:(ch + 1) * 128],
                        xn_fm2[jp][:, k, :], start=(k == 0), stop=(k == KCH - 1)),
                        sem=("PBK" if k == KCH - 1 else None))
                pbk[0] += 1
                wait("scalar", "PBK", pbk[0])
                op("scalar", lambda e, ch=ch, b=b: e.activation(
                    out=k_sb[:, ch, :], in_=pb_f[b][:, 0:256], func=ACT.Identity,
                    bias=bkT[:, ch:ch + 1]), sem="PEV")
                pev_hist.append(("PEV", cnt["PEV"]))

        def em_proj_v(jm):
            jp = jm % 2
            for tc in range(2):
                b = pbk[0] % 6
                if len(pev_hist) >= 6:
                    wait("tensor", *pev_hist[-6])
                for k in range(KCH):
                    op("tensor", lambda e, tc=tc, k=k, b=b, jp=jp: e.matmul(
                        pb_f[b][:], xn_fm2[jp][:, k, tc * 128:(tc + 1) * 128],
                        wvT[:, k, :], start=(k == 0), stop=False), sem=None)
                op("tensor", lambda e, tc=tc, b=b: e.matmul(
                    pb_f[b][:], c_onesrb[:], b_v_bf[:],
                    start=False, stop=True), sem="PBK")
                pbk[0] += 1
                wait("scalar", "PBK", pbk[0])
                op("scalar", lambda e, tc=tc, b=b: e.copy(
                    out=v_sb[:, tc, :], in_=pb_f[b][:]), sem="PEVK")
                pev_hist.append(("PEVK", cnt["PEVK"]))
            drain("tensor", sem="PRJ")
            PEV_AT[jm] = (cnt["PEV"], cnt["PEVK"])

        def em_kvq_stores(mm):
            wait("sync", "PEV", PEV_AT[mm][0])
            wait("sync", "PEVK", PEV_AT[mm][1])
            for ch in range(4):
                dma("sync", d_kscr[ch * 128:(ch + 1) * 128, mm * 256:(mm + 1) * 256],
                    k_sb[:, ch, :], sem="KVQ")
                dma("sync", d_qscr[ch * 128:(ch + 1) * 128, mm * 256:(mm + 1) * 256],
                    q_sb[:, ch, :], sem="KVQ")
            for tc in range(2):
                dma("sync", d_vscr[mm * 256 + tc * 128:mm * 256 + (tc + 1) * 128, :],
                    v_sb[:, tc, :], sem="KVQ")

        em_load(0)
        em_stats(0)
        em_smalls(0)

        for t in range(NT):
            p = t % 2
            m = t // 2
            if t % 2 == 0 and m >= 2:
                em_kvq_stores(m - 2)
            em_load(t + 1)
            if t >= 1:
                wait("sync", "GX", jb + 4 * (t - 1) + 4)
                dma("sync", d_xiscr[(t - 1) * 128:t * 128, :], xi_tok[0][:], sem="DMO")
            em_xn(t)
            if t % 2 == 0 and m >= 1:
                em_proj_q(m - 1)
            em_transp(t, 0, 2)
            if t % 2 == 0 and m >= 1:
                em_proj_k(m - 1)
            em_transp(t, 2, 4)
            if t % 2 == 0 and m >= 1:
                em_proj_v(m - 1)
            em_cmps(t - 1)
            em_tinies(t)
            em_makew(t)
            vw_now = cnt["VW"]
            for q in range(4):
                ch_S1(jb + 4 * t + q, p, q, vw_now)
                if q >= 1:
                    ch_S2(jb + 4 * t + q - 1)
            ch_S2(jb + 4 * t + 3)
            if t >= 1:
                for q in range(4):
                    ch_X2(jb + 4 * (t - 1) + q,
                          xi_tok[0][:, q * 1024:(q + 1) * 1024])
            for q in range(4):
                ch_E(jb + 4 * t + q, p, q)
            em_stats(t + 1)
            em_smalls(t + 1)

        em_cmps(NT - 1)
        for q in range(4):
            ch_X2(jb + 4 * (NT - 1) + q, xi_tok[0][:, q * 1024:(q + 1) * 1024])
        wait("sync", "GX", jb + 4 * NT)
        dma("sync", d_xiscr[(NT - 1) * 128:NT * 128, :], xi_tok[0][:], sem="DMO")
        em_proj_q(NMAC - 1)
        em_proj_k(NMAC - 1)
        em_proj_v(NMAC - 1)
        for mm in (NMAC - 2, NMAC - 1):
            em_kvq_stores(mm)

    PEV_AT = {}
    p1()
    flush("LDX", "LDXB", "DMO", "KVQ")
    barrier()

    nst = [xi2_tok[0], xi2_tok[1]]
    nsq = [at("nsq0", [128, DM], BF16, 114 * KB), at("nsq1", [128, DM], BF16, 122 * KB)]
    for scr, dstmax in ((d_qscr, qnmax), (d_kscr, knmax)):
        for h in range(2):
            for half in range(2):
                wait("sync", "SQN", cnt["SQN"])
                for rr_ in range(2):
                    dma("sync", nst[rr_][:],
                        scr[(2 * h + rr_) * 128:(2 * h + rr_ + 1) * 128,
                            half * DM:(half + 1) * DM], sem="KLD")
                wait_cur("scalar", "KLD")
                wait("scalar", "TNM", cnt["TNM"])
                selfwait("scalar")
                for rr_ in range(2):
                    op("scalar", lambda e, rr_=rr_: e.activation(
                        out=nsq[rr_][:], in_=nst[rr_][:], func=ACT.Square), sem="SQN")
                for cb in range(8):
                    bnk = cb % 2
                    wait("tensor", "SQN", cnt["SQN"])
                    wait("tensor", "NRM", max(0, cnt["NRM"] - 1))
                    for rr_ in range(2):
                        op("tensor", lambda e, rr_=rr_, cb=cb, bnk=bnk: e.matmul(
                            pb_f[bnk][0:1, 0:512], c_onesb[:],
                            nsq[rr_][:, cb * 512:(cb + 1) * 512],
                            start=(rr_ == 0), stop=(rr_ == 1)),
                            sem=("TNM" if rr_ == 1 else None))
                    wait("vector", "TNM", cnt["TNM"])
                    wait("vector", "NRM", cnt["NRM"])
                    op("vector", lambda e, bnk=bnk: e.reduce_max(
                        out=sm_all[0:1, 4:5], in_=pb_f[bnk][0:1, 0:512], axis=AX),
                        sem="VSF")
                    selfwait("vector")
                    op("vector", lambda e, dstmax=dstmax: e.tensor_max(
                        out=dstmax, in0=dstmax, in1=sm_all[0:1, 4:5]), sem="NRM")
    barrier()
    op("scalar", lambda e: e.activation(out=cc_q, in_=qnmax, func=ACT.Sqrt))
    op("scalar", lambda e: e.activation(out=cc_k, in_=knmax, func=ACT.Sqrt))
    drain("scalar", sem="CB1")
    wait_cur("vector", "CB1")
    op("vector", lambda e: e.tensor_mul(out=cc_c, in0=cc_q, in1=cc_k))
    drain("vector")
    op("vector", lambda e: e.tensor_scalar_mul(out=cc_c2, in0=cc_c, scalar1=-0.5))
    drain("vector", sem="CB2")
    wait_cur("tensor", "CB2")
    op("tensor", lambda e: e.matmul(ps_m0[:, 0:1], c_onesf[:], cc_c2,
                                    start=True, stop=True), sem="CB3")
    wait_cur("scalar", "CB3")
    op("scalar", lambda e: e.copy(out=eb_exp[:], in_=ps_m0[:, 0:1]))
    drain("scalar", sem="CB1")
    dma("sync", wiT[:], d_wiscr.rearrange("p (a b) -> p a b", a=KCH), sem="KLD")
    flush("KLD")
    barrier()

    v_re = d_vscr.rearrange("(mc p) f -> p mc f", p=128)

    glt = [0]
    ridx2 = [0]
    qk_i = [0]
    ffb = [0]

    def p2_head(hh):
        for ch in range(2):
            dma("sync", K_res[:, ch, :],
                d_kscr[hh * 256 + ch * 128:hh * 256 + (ch + 1) * 128, :], sem="KLD")
        dma("sync", V_res[:, :, 0:256], v_re[:, :, hh * 256:(hh + 1) * 256], sem="KLD")
        wait_cur("vector", "KLD")
        op("vector", lambda e: e.memset(V_res[:, :, 256:257], 1.0))
        drain("vector", sem="NRD")
        wait_cur("tensor", "NRD")
        wait_cur("scalar", "CB1")
        for lt in range(NLT):
            g = glt[0]
            wait("sync", "QF", g - 1)
            sq_ = "QLD" if lt % 2 == 0 else "QLDB"
            for ch in range(2):
                dma("sync", q2_sb[lt % 2][:, ch, :],
                    d_qscr[hh * 256 + ch * 128:hh * 256 + (ch + 1) * 128,
                           lt * 512:(lt + 1) * 512], sem=sq_)
            if hh == 1:
                for lc in range(4):
                    row = lt * 512 + lc * 128
                    wait("sync", "VOUT", max(0, cnt["VOUT"] - 4 + lc + 1))
                    wait("sync", "TXF", max(0, ridx2[0] // 4 - 1))
                    sx2 = "XI2" if lc % 2 == 0 else "XI2B"
                    sa2 = "A2" if lc % 2 == 0 else "A2B"
                    ss2 = "XSL" if lc % 2 == 0 else "XSLB"
                    dma("sync", xi2_tok[lc % 2][:], d_xiscr[row:row + 128, :], sem=sx2)
                    dma("sync", a0_sb[lc][:], d_ascr[row:row + 128, :], sem=sa2)
                    dma("sync", xsl_sb[lc][:], d_xsl[row:row + 128, :], sem=ss2)
                    wait("tensor", sx2, cnt[sx2])
                    for gg in range(4):
                        bnk = ridx2[0] % 2
                        wait("tensor", "FFV", max(0, ffb[0] - 1))
                        wait("tensor", "TEV", cnt["TEV"] - 1)
                        for j in range(8):
                            f = gg * 8 + j
                            op("tensor", lambda e, lc=lc, f=f, bnk=bnk, j=j: e.matmul(
                                pb_t[bnk][:, j * 128:(j + 1) * 128],
                                xi2_tok[lc % 2][:, f * 128:(f + 1) * 128], c_id[:],
                                start=(j == 0), stop=(j == 7), is_transpose=True),
                                sem=("TPS" if j == 7 else None))
                        wait("vector", "TPS", cnt["TPS"])
                        op("vector", lambda e, gg=gg, bnk=bnk, lc=lc: e.tensor_copy(
                            out=xi2_fm[:, gg * 8:(gg + 1) * 8, lc * 128:(lc + 1) * 128],
                            in_=pb_t[bnk][:]), sem="TEV")
                        ridx2[0] += 1
                    drain("tensor", sem="TXF")
                    b = lc % 2
                    wait("tensor", "TEV", cnt["TEV"])
                    wait("tensor", "ER", cnt["ER"])
                    wait("tensor", "FFV", max(0, ffb[0] - 1))
                    for k in range(KCH):
                        op("tensor", lambda e, lc=lc, k=k, b=b: e.matmul(
                            ps_pj[b][:], xi2_fm[:, k, lc * 128:(lc + 1) * 128],
                            wiT[:, k, :], start=(k == 0), stop=(k == KCH - 1)),
                            sem=("FPB" if k == KCH - 1 else None))
                    wait("vector", "FPB", cnt["FPB"])
                    t_idx = lt * 4 + lc
                    op("vector", lambda e, lc=lc, b=b, t_idx=t_idx: e.tensor_scalar_mul(
                        out=ffn_sb[lc][:], in0=ps_pj[b][:],
                        scalar1=sxw[:, t_idx:t_idx + 1]), sem="FFV")
                    ffb[0] += 1
            wait("tensor", sq_, cnt[sq_])
            sbk = [ps_pj[0][:], ps_pj[1][:]]
            if hh == 0:
                sbk += [pb_t[0][:].bitcast(F32), pb_t[1][:].bitcast(F32)]
            nb = len(sbk)
            LA = nb - 1

            def emit_qk(i, mc):
                b = i % nb
                wait("tensor", "ER", i - (nb - 1))
                if hh == 1 and mc < 2:
                    wait("tensor", "FFV", ffb[0])
                for d in range(2):
                    op("tensor", lambda e, d=d, mc=mc, sb_=sbk[b], lt=lt: e.matmul(
                        sb_, K_res[:, d, mc * 128:(mc + 1) * 128],
                        q2_sb[lt % 2][:, d, :], start=(d == 0), stop=(d == 1)),
                        sem=("SR" if d == 1 else None))
                wait("scalar", "SR", i + 1)
                wait("scalar", "EF", i - 2)
                op("scalar", lambda e, sb_=sbk[b], e3=i % 3: e.activation(
                    out=expS[e3][:], in_=sb_, func=ACT.Exp,
                    bias=eb_exp[:]), sem="ER")

            def emit_pv(i, mc):
                e3 = i % 3
                wait("tensor", "ER", i + 1)
                for lc in range(4):
                    op("tensor", lambda e, lc=lc, e3=e3, mc=mc: e.matmul(
                        ps_o[lc][:, 0:257], expS[e3][:, lc * 128:(lc + 1) * 128],
                        V_res[:, mc, :], start=(mc == 0), stop=(mc == MCN - 1)),
                        sem=("EF" if lc == 3 else None))

            i0 = qk_i[0]
            qk_i[0] += MCN
            wait("tensor", "OF", 4 * g)
            for k in range(LA):
                emit_qk(i0 + k, k)
            for mc in range(LA, MCN):
                emit_qk(i0 + mc, mc)
                emit_pv(i0 + mc - LA, mc - LA)
            for k in range(MCN - LA, MCN):
                emit_pv(i0 + k, k)
            drain("tensor", sem="QF")
            for lc in range(4):
                wait("vector", "EF", 64 * (g + 1))
                wait("vector", "OF", cnt["OF"])
                op("vector", lambda e, lc=lc: e.reciprocal(
                    out=rec_sb[:], in_=ps_o[lc][:, 256:257]), sem="VSF")
                selfwait("vector")
                if hh == 0:
                    sad = "ASD" if lc % 2 == 0 else "ASDB"
                    wait("vector", sad, cnt[sad])
                    op("vector", lambda e, lc=lc: e.tensor_scalar_mul(
                        out=a0_sb[lc][:], in0=ps_o[lc][:, 0:256],
                        scalar1=rec_sb[:]), sem="OF")
                    wait("sync", "OF", cnt["OF"])
                    row = lt * 512 + lc * 128
                    dma("sync", d_ascr[row:row + 128, :], a0_sb[lc][:], sem=sad)
                else:
                    op("vector", lambda e, lc=lc: e.tensor_scalar_mul(
                        out=attn_f[lc][:, 256:512], in0=ps_o[lc][:, 0:256],
                        scalar1=rec_sb[:]), sem="OF")
                    wait("vector", "A2" if lc % 2 == 0 else "A2B",
                         cnt["A2" if lc % 2 == 0 else "A2B"])
                    wait("vector", "XSL" if lc % 2 == 0 else "XSLB",
                         cnt["XSL" if lc % 2 == 0 else "XSLB"])
                    op("vector", lambda e, lc=lc: e.tensor_copy(
                        out=attn_f[lc][:, 0:256], in_=a0_sb[lc][:]), sem="VSF")
                    selfwait("vector")
                    op("vector", lambda e, lc=lc: e.tensor_add(
                        out=tadd_sb[lc][:], in0=attn_f[lc][:], in1=ffn_sb[lc][:]),
                        sem="VSF")
                    selfwait("vector")
                    wait("vector", "OUT", 16 * max(0, cnt["VOUT"] - lc))
                    op("vector", lambda e, lc=lc: e.tensor_add(
                        out=out_sb[lc][:], in0=tadd_sb[lc][:], in1=xsl_sb[lc][:]),
                        sem="VOUT")
                    wait("sync", "VOUT", cnt["VOUT"])
                    row = lt * 512 + lc * 128
                    dma("sync", d_out[row:row + 128, :], out_sb[lc][:], sem="OUT")
            glt[0] += 1

    p2_head(0)
    flush("ASD", "ASDB", "QLD", "QLDB", "KLD")
    barrier()
    p2_head(1)
    wait_cur("sync", "OUT")
    wait_cur("sync", "ASD")
    if dev:
        op("vector", lambda e: e.memset(st_bn[:, 0:1], 0.0))
        drain("vector", sem="V0")
        wait_cur("sync", "V0")
        dma("sync", d_dbg[:, 0:64], sxw[:], sem="OUT", slow=True)
        dma("sync", d_dbg[:, 64:65], s_wcol[:], sem="OUT", slow=True)
        dma("sync", d_dbg[:, 65:66], r_wcol[:], sem="OUT", slow=True)
        dma("sync", d_dbg[:, 66:67], eb_exp[:], sem="OUT", slow=True)
        dma("sync", d_dbg[0:1, 67:68], qnmax, sem="OUT", slow=True)
        dma("sync", d_dbg[0:1, 68:69], knmax, sem="OUT", slow=True)
        wait_cur("sync", "OUT")

    for role, semn, val in deferred_waits:
        assert val <= cnt[semn], ("unsatisfiable wait", role, semn, val, cnt[semn])
    with nc.Block() as block:
        for role in ROLES:
            getattr(block, role)(lambda eng, role=role: [f(eng) for f in plan[role]])
    return nc


def make_in_maps(inputs, ncore=8):
    cst = host_constants()
    x = np.ascontiguousarray(np.asarray(inputs["L0_attn_x"], np.float32).reshape(L, DM))
    wq = np.asarray(inputs["L0_attn_Wq"], np.float32)
    wk = np.asarray(inputs["L0_attn_Wk"], np.float32)
    wv = np.asarray(inputs["L0_attn_Wv"], np.float32)
    bq = np.asarray(inputs["L0_attn_bq"], np.float32)
    bk = np.asarray(inputs["L0_attn_bk"], np.float32)
    bv = np.asarray(inputs["L0_attn_bv"], np.float32)
    bw = np.asarray(inputs["L0_attn_bias_weight"], np.float32)
    wf = np.ascontiguousarray(np.asarray(inputs["L0_ffn_weight"], np.float32))
    maps = []
    for c in range(ncore):
        sl = slice(FS * c, FS * (c + 1))
        m = dict(
            x=x,
            xsl=np.ascontiguousarray(x[:, sl]),
            wq=np.ascontiguousarray(wq[sl]),
            wk=np.ascontiguousarray(wk[sl]),
            wv=np.ascontiguousarray(wv[sl]),
            wf=wf,
            wfs=np.ascontiguousarray(wf[sl]),
            bq=np.ascontiguousarray(bq[sl]).reshape(1, FS),
            bk=np.ascontiguousarray(bk[sl]).reshape(1, FS),
            bv=np.ascontiguousarray(bv[sl]).reshape(1, FS),
            bwT=np.ascontiguousarray(bw[2 * c:2 * c + 2].T),
            **cst,
        )
        maps.append(m)
    return maps


def assemble(results):
    return np.concatenate([np.asarray(r["out"]) for r in results],
                          axis=1).reshape(1, L, DM)


from concourse.bass_utils import run_bass_kernel_spmd

_NC_CACHE = {}


def kernel(**inputs):
    maps = make_in_maps(inputs, ncore=8)
    if "nc" not in _NC_CACHE:
        _NC_CACHE["nc"] = build(dev=False)
    nc = _NC_CACHE["nc"]
    res = run_bass_kernel_spmd(nc, maps, core_ids=list(range(8)))
    return assemble(res.results)


# version 11
# speedup vs baseline: 1.6402x; 1.0859x over previous
import math
import numpy as np
import ml_dtypes
import concourse.bass as bass
import concourse.mybir as mybir

F32 = mybir.dt.float32
BF16 = mybir.dt.bfloat16
I32 = mybir.dt.int32
A = mybir.AluOpType
AX = mybir.AxisListType.X
ACT = mybir.ActivationFunctionType
BF = ml_dtypes.bfloat16

L = 8192
DM = 4096
FS = 512
HD = 256
NT = 64
NMAC = 32
NLT = 16
MCN = 64
KCH = 32
EPS = 1e-5
CLIP1 = 1.4999
SHIFT = 256.5
ROLES = ("sync", "vector", "scalar", "tensor", "gpsimd")
KB = 1024


def _build_tables():
    g = [1.0, 1.0, 1.0]
    n = 8
    masks = sorted(range(n), key=lambda m: (bin(m).count("1"), m))
    inv = {m: i for i, m in enumerate(masks)}
    C = np.zeros((n, n, n), dtype=np.float32)
    for i, a in enumerate(masks):
        for j, b in enumerate(masks):
            sw, at_ = (0, a >> 1)
            while at_:
                sw += bin(at_ & b).count("1")
                at_ >>= 1
            s = -1.0 if sw & 1 else 1.0
            for t in range(3):
                if a >> t & 1 and b >> t & 1:
                    s *= float(g[t])
            C[i, j, inv[a ^ b]] = s
    grades = np.array([bin(m).count("1") for m in masks])
    rev = np.where(grades * (grades - 1) // 2 % 2 == 1, -1.0, 1.0).astype(np.float32)
    met = np.array([float(np.prod([g[t] for t in range(3) if m >> t & 1] + [1.0]))
                    for m in masks], dtype=np.float32)
    return C, rev, met


def host_constants():
    C, rev, met = _build_tables()
    biv = C[:, :, [4, 5, 6]]
    scale = 1.0 / math.sqrt(HD)
    bivf = np.ascontiguousarray(biv.transpose(2, 0, 1).reshape(3, 64)).astype(np.float32)
    rconst = np.tile(np.repeat(rev, 8)[None, :], (2, 1)).astype(np.float32)
    dconst = np.tile((scale * np.diag(rev * met)).reshape(1, 64), (2, 1)).astype(np.float32)
    return dict(
        bivf=bivf, rconst=rconst, dconst=dconst,
        iden=np.eye(128).astype(BF),
        onesf=np.ones((1, 128), np.float32),
        onesb=np.ones((128, 1)).astype(BF),
        onesrb=np.ones((1, 128)).astype(BF),
        onescf=np.ones((128, 1), np.float32),
    )


def build(dev=False):
    nc = bass.Bass("TRN2", target_bir_lowering=False, debug=False)

    d_x = nc.dram_tensor("x", [L, DM], F32, kind="ExternalInput").ap()
    d_xsl = nc.dram_tensor("xsl", [L, FS], F32, kind="ExternalInput").ap()
    d_wq = nc.dram_tensor("wq", [FS, DM], F32, kind="ExternalInput").ap()
    d_wk = nc.dram_tensor("wk", [FS, DM], F32, kind="ExternalInput").ap()
    d_wv = nc.dram_tensor("wv", [FS, DM], F32, kind="ExternalInput").ap()
    d_wf = nc.dram_tensor("wf", [DM, DM], F32, kind="ExternalInput").ap()
    d_wfs = nc.dram_tensor("wfs", [FS, DM], F32, kind="ExternalInput").ap()
    d_bq = nc.dram_tensor("bq", [1, FS], F32, kind="ExternalInput").ap()
    d_bk = nc.dram_tensor("bk", [1, FS], F32, kind="ExternalInput").ap()
    d_bv = nc.dram_tensor("bv", [1, FS], F32, kind="ExternalInput").ap()
    d_bwT = nc.dram_tensor("bwT", [3, 2], F32, kind="ExternalInput").ap()
    d_bivf = nc.dram_tensor("bivf", [3, 64], F32, kind="ExternalInput").ap()
    d_rc = nc.dram_tensor("rconst", [2, 64], F32, kind="ExternalInput").ap()
    d_dc = nc.dram_tensor("dconst", [2, 64], F32, kind="ExternalInput").ap()
    d_id = nc.dram_tensor("iden", [128, 128], BF16, kind="ExternalInput").ap()
    d_onesf = nc.dram_tensor("onesf", [1, 128], F32, kind="ExternalInput").ap()
    d_onesb = nc.dram_tensor("onesb", [128, 1], BF16, kind="ExternalInput").ap()
    d_onesrb = nc.dram_tensor("onesrb", [1, 128], BF16, kind="ExternalInput").ap()
    d_onescf = nc.dram_tensor("onescf", [128, 1], F32, kind="ExternalInput").ap()
    d_out = nc.dram_tensor("out", [L, FS], F32, kind="ExternalOutput").ap()

    skind = "ExternalOutput" if dev else "Internal"
    d_kscr = nc.dram_tensor("k_scr", [FS, L], BF16, kind=skind).ap()
    d_vscr = nc.dram_tensor("v_scr", [L, FS], BF16, kind=skind).ap()
    d_qscr = nc.dram_tensor("q_scr", [FS, L], BF16, kind=skind).ap()
    d_xiscr = nc.dram_tensor("xi_scr", [L, DM], BF16, kind=skind).ap()
    d_wiscr = nc.dram_tensor("wi_scr", [128, KCH * FS], BF16, kind=skind).ap()
    d_ascr = nc.dram_tensor("a_scr", [L, HD], BF16, kind=skind).ap()
    if dev:
        d_dbg = nc.dram_tensor("dbg", [128, 80], F32, kind="ExternalOutput").ap()

    ARENA_PP = 207 * KB + 512
    arena = nc.sbuf_tensor("arena", [128, ARENA_PP], mybir.dt.uint8).__enter__()
    base = nc.lookup_mloc(arena).addr

    def at(name, shape, dtype, off):
        end = off + int(np.prod(shape[1:])) * mybir.dt.size(dtype)
        assert end <= ARENA_PP, (name, end)
        return nc.alloc_sbuf_tensor_at(name, shape, dtype, offset=base + off)

    wqT = at("wqT", [128, KCH, FS], BF16, 0)
    wkT = at("wkT", [128, KCH, FS], BF16, 32 * KB)
    wvT = at("wvT", [128, KCH, FS], BF16, 64 * KB)
    xst = [at(f"xst{i}", [128, DM], F32, 96 * KB + 16 * KB * i) for i in range(2)]
    xn_tok = [at("xntok", [128, DM], BF16, 128 * KB)] * 2
    xn_fm2 = [at(f"xnfm{i}", [128, KCH, 256], BF16, 136 * KB + 16 * KB * i) for i in range(2)]
    xi_tok = [at("xitok", [128, DM], BF16, 168 * KB)] * 2
    RS = 176 * KB
    cB = [at(f"cB{i}", [128, 1024], mybir.dt.int16, RS + 2 * KB * i) for i in range(4)]
    cC = [at(f"cC{i}", [128, 1024], BF16, RS + 8 * KB + 2 * KB * i) for i in range(4)]
    cF = [at(f"cF{i}", [128, 1024], BF16, RS + 16 * KB + 2 * KB * i) for i in range(2)]
    k_sb = at("ksb", [128, 4, 256], BF16, 196 * KB)
    v_sb = at("vsb", [128, 2, FS], BF16, 198 * KB)
    q_sb = at("qsb", [128, 4, 256], BF16, 200 * KB)
    wstage = [xst[0], xst[1]]
    wcast = at("wcast", [128, DM], BF16, 128 * KB)
    wcast2 = at("wcast2", [128, DM], BF16, 136 * KB)
    wiT_b = at("wiTb", [128, KCH, FS], BF16, 0)
    wiT = at("wiT", [128, KCH, FS], BF16, 0)
    K_res = at("Kres", [128, 2, L], BF16, 32 * KB)
    V_res = at("Vres", [128, MCN, HD + 1], BF16, 64 * KB)
    xi2_tok = [at(f"xi2t{i}", [128, DM], BF16, 98 * KB + 8 * KB * i) for i in range(2)]
    xi2_fm = at("xi2fm", [128, KCH, FS], BF16, 114 * KB)
    q2_sb = [at(f"q2sb{i}", [128, 2, FS], BF16, 146 * KB + 2 * KB * i) for i in range(2)]
    expS = [at(f"expS{i}", [128, 512], BF16, 150 * KB + KB * i) for i in range(4)]
    attn_f = [at(f"attnf{i}", [128, FS], BF16, 166 * KB + KB * i) for i in range(4)]
    a0_sb = [at(f"a0sb{i}", [128, HD], BF16, 170 * KB + 512 * i) for i in range(4)]
    xsl_sb = [at(f"xslb{i}", [128, FS], F32, 172 * KB + 2 * KB * i) for i in range(4)]
    ffn_sb = [at(f"ffnsb{i}", [128, FS], BF16, 180 * KB + KB * i) for i in range(4)]
    tadd_sb = [at(f"tadd{i}", [128, FS], F32, 184 * KB + 2 * KB * i) for i in range(4)]
    out_sb = [at(f"outsb{i}", [128, FS], F32, 192 * KB + 2 * KB * i) for i in range(4)]

    import contextlib
    es = contextlib.ExitStack()
    _pk = [202 * KB]

    def sb(n, s, d):
        off = _pk[0]
        t = at(n, s, d, off)
        sz = int(np.prod(s[1:])) * mybir.dt.size(d)
        _pk[0] = off + ((sz + 63) // 64) * 64
        return t
    c_id = sb("c_id", [128, 128], BF16)
    c_onesf = sb("c_onesf", [1, 128], F32)
    c_onesb = sb("c_onesb", [128, 1], BF16)
    c_onesrb = sb("c_onesrb", [1, 128], BF16)
    b_v_bf = sb("b_v_bf", [1, FS], BF16)
    c_onescf = sb("c_onescf", [128, 1], F32)
    c_bivf = at("c_bivf", [3, 64], F32, RS + 6 * KB)
    c_rc = at("c_rc", [2, 64], F32, RS + 6 * KB + 256)
    c_dc = at("c_dc", [2, 64], F32, RS + 6 * KB + 512)
    c_bwT = at("c_bwT", [3, 2], F32, RS + 6 * KB + 768)
    c_T = sb("c_T", [128, 256], BF16)
    c_A = at("c_A", [2, 64], F32, RS + 7 * KB)
    c_Abf = at("c_Abf", [2, 64], BF16, RS + 7 * KB + 256)
    b_q = at("b_q", [1, FS], F32, RS)
    b_k = at("b_k", [1, FS], F32, RS + 2 * KB)
    b_v = at("b_v", [1, FS], F32, RS + 4 * KB)
    bkT = sb("bkT", [128, 4], F32)
    bqT = sb("bqT", [128, 4], F32)
    bqTb = sb("bqTb", [128, 4], BF16)
    bqe = sb("bqe", [128, 4], F32)
    st_all = sb("st_all", [128, 84], F32)
    _st = [0]

    def _stc(n):
        a, _st[0] = _st[0], _st[0] + n
        return st_all[:, a:a + n]

    st_bn = _stc(48)
    st_mv = _stc(4)
    st_varc = _stc(2)
    st_rstd = _stc(2)
    st_nrstd = _stc(2)
    st_nmrs = _stc(2)
    st_mrs = _stc(2)
    st_xmax = _stc(2)
    st_xmin = _stc(2)
    st_a = _stc(2)
    st_b = _stc(2)
    st_amax = _stc(2)
    st_sx = _stc(2)
    st_rx = _stc(2)
    st_rr = _stc(2)
    st_bb = _stc(2)
    st_bb1 = _stc(2)
    c_2565 = sb("c_2565", [128, 1], F32)
    c_m257 = sb("c_m257", [128, 1], F32)
    c_eps = sb("c_eps", [128, 1], F32)
    sxw = sb("sxw", [128, NT], F32)
    s_wcol = sb("s_wcol", [128, 1], F32)
    r_wcol = sb("r_wcol", [128, 1], F32)
    wfsum = sb("wfsum", [128, 1], F32)
    wfpart = sb("wfpart", [128, 1], F32)
    sm_all = sb("sm_all", [1, 16], F32)
    s_sw = sm_all[0:1, 0:1]
    s_rw = sm_all[0:1, 1:2]
    qnmax = sm_all[0:1, 2:3]
    knmax = sm_all[0:1, 3:4]
    nrm_red = sm_all[0:1, 4:6]
    cc_q = sm_all[0:1, 6:7]
    cc_k = sm_all[0:1, 7:8]
    cc_c = sm_all[0:1, 8:9]
    cc_c2 = sm_all[0:1, 9:10]
    eb_exp = sb("eb_exp", [128, 1], F32)
    rec_sb = sb("rec_sb", [128, 1], F32)

    pb_f = [es.enter_context(nc.psum_tensor(f"pbf{i}", [128, 512], F32)) for i in range(6)]
    pb_t = [es.enter_context(nc.psum_tensor(f"pbt{i}", [128, 1024], BF16)) for i in range(2)]
    ps_pj = pb_f[0:2]
    ps_nm = pb_f[2]
    ps_m0 = pb_f[3]

    def pvs(lc, a, b2):
        return pb_f[2 + lc][:, a:b2]

    sem_names = [
        "BAR", "C0", "T0", "V0", "S0", "G0", "W0", "W1",
        "LDX", "LDXB", "W0B", "W1B", "DMOB", "XI2B", "A2B", "XSLB", "QLDB", "VS", "GS", "SSM", "SXN",
        "VW", "SC1", "SC2", "VSS", "GX", "DMO",
        "TPS", "TEV", "PRJ", "PBK", "PEV", "PEVK", "SQN", "TNM", "NRD", "KVQ",
        "KLD", "QLD", "QF", "SR", "ER", "EF", "OF",
        "A2", "XSL", "XI2", "TXF", "FPB", "FFV", "VOUT", "OUT", "ASD", "ASDB",
        "CB1", "CB2", "CB3", "SSQ", "VRS", "GE1", "VSF", "SSF", "GSF", "VT", "NRM",
    ]
    S = {n: es.enter_context(nc.semaphore(n)) for n in sem_names}

    plan = {r: [] for r in ROLES}
    cnt = {n: 0 for n in sem_names}

    def op(role, fn, sem=None, n=1):
        if sem is not None:
            cnt[sem] += n
            plan[role].append(lambda eng, fn=fn, s=S[sem], m=n: fn(eng).then_inc(s, m))
        else:
            plan[role].append(lambda eng, fn=fn: fn(eng))

    def dma(role, out, in_, sem=None, slow=False):
        op(role, lambda e, o=out, i=in_, sl=slow: e.dma_start(
            out=o, in_=i, allow_slow_non_contiguous=sl), sem=sem, n=16)

    deferred_waits = []

    def wait(role, sem, val):
        deferred_waits.append((role, sem, val))
        if val > 0:
            plan[role].append(lambda eng, s=S[sem], v=val: eng.wait_ge(s, v))

    def wait_cur(role, sem):
        wait(role, sem, cnt[sem])

    def drain(role, sem=None, n=1):
        op(role, lambda e: e.drain(), sem=sem, n=n)

    SELF = {"vector": "VSF", "scalar": "SSF", "gpsimd": "GSF"}

    def selfwait(role):
        wait(role, SELF[role], cnt[SELF[role]])

    def chop(role, fn, sem=None):
        selfwait(role)
        op(role, fn, sem=(sem if sem is not None else SELF[role]))

    nbar = [0]

    def flush(*sems, role="sync"):
        for s_ in sems:
            wait_cur(role, s_)

    def barrier():
        nbar[0] += 1
        k = nbar[0] * len(ROLES)
        cnt["BAR"] += len(ROLES)
        for r in ROLES:
            def f(eng, v=k):
                eng.drain().then_inc(S["BAR"], 1)
                eng.wait_ge(S["BAR"], v)
            plan[r].append(f)

    for dst, src in [(c_id, d_id), (c_onesf, d_onesf), (c_onesb, d_onesb),
                     (c_onescf, d_onescf), (c_bivf, d_bivf), (c_rc, d_rc),
                     (c_dc, d_dc), (c_bwT, d_bwT), (b_q, d_bq), (b_k, d_bk),
                     (b_v, d_bv), (c_onesrb, d_onesrb)]:
        dma("sync", dst[:], src, sem="C0")
    op("vector", lambda e: e.memset(c_2565[:], SHIFT))
    op("vector", lambda e: e.memset(c_m257[:], -257.0))
    op("vector", lambda e: e.memset(c_eps[:], EPS))
    op("vector", lambda e: e.memset(qnmax, 0.0))
    op("vector", lambda e: e.memset(knmax, 0.0))
    op("vector", lambda e: e.memset(wfsum[:], 0.0))
    op("vector", lambda e: e.memset(c_T[:], 0.0))
    flush("C0")
    barrier()

    for f in range(4):
        dma("sync", bkT[:, f:f + 1], b_k[0:1, f * 128:(f + 1) * 128], sem="C0")
        dma("sync", bqT[:, f:f + 1], b_q[0:1, f * 128:(f + 1) * 128], sem="C0")
    op("tensor", lambda e: e.matmul(ps_m0[0:2, 0:64], c_bwT[:], c_bivf[:],
                                    start=True, stop=True), sem="T0")
    wait_cur("vector", "T0")
    op("vector", lambda e: e.tensor_mul(out=c_A[:], in0=ps_m0[0:2, 0:64], in1=c_rc[:]))
    drain("vector")
    op("vector", lambda e: e.tensor_add(out=c_A[:], in0=c_A[:], in1=c_dc[:]))
    drain("vector")
    op("vector", lambda e: e.tensor_copy(out=c_Abf[:], in_=c_A[:]))
    wait_cur("vector", "C0")
    op("vector", lambda e: e.tensor_copy(out=bqTb[:], in_=bqT[:]))
    drain("vector", sem="V0")
    wait_cur("sync", "V0")
    for hh in range(2):
        for dd in range(16):
            dma("sync", c_T[8 * dd:8 * dd + 8, hh * 128 + 8 * dd:hh * 128 + 8 * dd + 8],
                c_Abf[hh:hh + 1, 0:64], sem="C0")
    op("vector", lambda e: e.tensor_copy(out=b_v_bf[:], in_=b_v[:]))
    drain("vector", sem="V0")
    wait_cur("tensor", "C0")
    wait_cur("tensor", "V0")
    for j in range(4):
        if j >= 2:
            wait("tensor", "S0", cnt["S0"] - 1)
        op("tensor", lambda e, j=j: e.matmul(
            ps_pj[j % 2][:, 0:1], c_T[:, (j // 2) * 128:(j // 2) * 128 + 128],
            bqTb[:, j:j + 1], start=True, stop=True), sem="T0")
        wait_cur("scalar", "T0")
        op("scalar", lambda e, j=j: e.copy(out=bqe[:, j:j + 1], in_=ps_pj[j % 2][:, 0:1]))
        drain("scalar", sem="S0")
    flush("C0")
    barrier()

    def transp32(src_bf, dst, rcslice, tag_rc):
        for g in range(4):
            bnk = g % 2
            wait("tensor", "TEV", cnt["TEV"] - 1)
            for j in range(8):
                f = g * 8 + j
                op("tensor", lambda e, f=f, bnk=bnk, j=j, s=src_bf: e.matmul(
                    pb_t[bnk][:, j * 128:(j + 1) * 128], s[:, f * 128:(f + 1) * 128],
                    c_id[:], start=(j == 0), stop=(j == 7), is_transpose=True),
                    sem=("TPS" if j == 7 else None))
            wait_cur("vector", "TPS")
            op("vector", lambda e, g=g, bnk=bnk, d=dst, rcs=rcslice: e.tensor_copy(
                out=d[:, g * 8:(g + 1) * 8, rcs], in_=pb_t[bnk][:]), sem="TEV")

    def w_prep_plain(dsrc, wT):
        for rc in range(4):
            par = rc % 2
            sw = "W0" if par == 0 else "W0B"
            wait("sync", "V0", cnt["V0"] - 1)
            dma("sync", wstage[par][:], dsrc[rc * 128:(rc + 1) * 128, :], sem=sw)
            wait_cur("vector", sw)
            wait("vector", "TEV", cnt["TEV"])
            op("vector", lambda e, par=par: e.tensor_copy(out=wcast[:], in_=wstage[par][:]))
            drain("vector", sem="V0")
            wait_cur("tensor", "V0")
            transp32(wcast, wT, slice(rc * 128, (rc + 1) * 128), rc)

    w_prep_plain(d_wk, wkT)
    w_prep_plain(d_wv, wvT)
    flush("W0", "W0B")
    barrier()

    for ch in range(32):
        par = ch % 2
        sw = "W0" if par == 0 else "W0B"
        wait("sync", "V0", cnt["V0"] - 1)
        dma("sync", wstage[par][:], d_wf[ch * 128:(ch + 1) * 128, :], sem=sw)
        wait("vector", sw, cnt[sw])
        wait("vector", "V0", cnt["V0"])
        op("vector", lambda e, par=par: e.tensor_reduce(
            out=wfpart[:], in_=wstage[par][:], axis=AX, op=A.add,
            apply_absolute_value=True), sem="VSF")
        selfwait("vector")
        op("vector", lambda e: e.tensor_add(out=wfsum[:], in0=wfsum[:], in1=wfpart[:]),
            sem="V0")
    wait_cur("tensor", "V0")
    op("tensor", lambda e: e.matmul(ps_m0[0:1, 0:1], c_onescf[:], wfsum[:],
                                    start=True, stop=True), sem="T0")
    wait_cur("vector", "T0")
    op("vector", lambda e: e.tensor_scalar_mul(out=s_sw, in0=ps_m0[0:1, 0:1],
                                               scalar1=1.0 / (DM * DM)))
    drain("vector")
    op("vector", lambda e: e.tensor_scalar_add(out=s_rw, in0=s_sw, scalar1=1e-8))
    drain("vector")
    op("vector", lambda e: e.reciprocal(out=s_rw, in_=s_rw))
    drain("vector", sem="V0")
    wait_cur("tensor", "V0")
    op("tensor", lambda e: e.matmul(ps_m0[:, 0:1], c_onesf[:], s_sw,
                                    start=True, stop=True))
    op("tensor", lambda e: e.matmul(pb_f[4][:, 0:1], c_onesf[:], s_rw,
                                    start=True, stop=True), sem="T0")
    wait_cur("scalar", "T0")
    op("scalar", lambda e: e.copy(out=s_wcol[:], in_=ps_m0[:, 0:1]))
    op("scalar", lambda e: e.copy(out=r_wcol[:], in_=pb_f[4][:, 0:1]))
    drain("scalar", sem="S0")
    flush("W0", "W0B")
    barrier()

    jq = [0]

    def ch_S1(j, p, q, vw):
        wait("scalar", "VW", vw)
        wait("scalar", "GX", j - 3)
        op("scalar", lambda e, p=p, q=q, j=j: e.copy(
            out=cB[j % 4][:], in_=xst[p][:, q * 1024:(q + 1) * 1024]), sem="SC1")

    def ch_S2(j):
        wait("scalar", "SC1", j + 1)
        wait("scalar", "VSS", j - 3)
        op("scalar", lambda e, j=j: e.activation(
            out=cC[j % 4][:], in_=cB[j % 4][:], func=ACT.Identity,
            bias=c_m257[:]), sem="SC2")

    def ch_E(j, p, q):
        wait("gpsimd", "SC2", j + 1)
        op("gpsimd", lambda e, p=p, q=q, j=j: e.tensor_tensor(
            out=xst[p][:, q * 1024:(q + 1) * 1024],
            in0=xst[p][:, q * 1024:(q + 1) * 1024],
            in1=cC[j % 4][:], op=A.subtract), sem="GE1")

    def ch_C12(j, p, q):
        wait("vector", "GE1", j + 1)
        wait("vector", "VSS", j - 1)
        op("vector", lambda e, p=p, q=q, j=j: e.tensor_single_scalar(
            out=cF[j % 2][:], in_=xst[p][:, q * 1024:(q + 1) * 1024],
            scalar=257.0, op=A.is_ge), sem="VT")
        op("vector", lambda e, p=p, q=q, j=j: e.tensor_single_scalar(
            out=cB[j % 4][:].bitcast(BF16), in_=xst[p][:, q * 1024:(q + 1) * 1024],
            scalar=258.0, op=A.is_ge), sem="VT")

    def ch_X1(j, dst_ap):
        wait("vector", "VT", 2 * (j + 1))
        op("vector", lambda e, j=j, d=dst_ap: e.tensor_tensor(
            out=d, in0=cC[j % 4][:], in1=cF[j % 2][:], op=A.add), sem="VSS")

    def ch_X2(j, dst_ap):
        wait("gpsimd", "VSS", j + 1)
        op("gpsimd", lambda e, j=j, d=dst_ap: e.tensor_tensor(
            out=d, in0=d, in1=cB[j % 4][:].bitcast(BF16), op=A.add), sem="GX")

    tev_hist = [cnt["TEV"]]
    for rc in range(4):
        par = rc % 2
        sw = "W1" if par == 0 else "W1B"
        if rc >= 2:
            wait("sync", "VT", 2 * (4 * (rc - 2) + 4))
        dma("sync", wstage[par][:], d_wfs[rc * 128:(rc + 1) * 128, :], sem=sw)
        wait("vector", sw, cnt[sw])
        selfwait("vector")
        op("vector", lambda e, par=par: e.tensor_scalar(
            out=wstage[par][:], in0=wstage[par][:], scalar1=r_wcol[:],
            scalar2=CLIP1, op0=A.mult, op1=A.min), sem="VSF")
        selfwait("vector")
        op("vector", lambda e, par=par: e.tensor_scalar(
            out=wstage[par][:], in0=wstage[par][:], scalar1=-CLIP1,
            scalar2=SHIFT, op0=A.max, op1=A.add), sem="VW")
        vw_now = cnt["VW"]
        for q in range(4):
            j = jq[0]
            jq[0] += 1
            ch_S1(j, par, q, vw_now)
            ch_S2(j)
            ch_E(j, par, q)
            ch_C12(j, par, q)
            if q == 0:
                wait("vector", "TEV", tev_hist[-1])
            ch_X1(j, wcast[:, q * 1024:(q + 1) * 1024])
            ch_X2(j, wcast[:, q * 1024:(q + 1) * 1024])
        wait_cur("tensor", "GX")
        transp32(wcast, wiT_b, slice(rc * 128, (rc + 1) * 128), rc)
        tev_hist.append(cnt["TEV"])
    wait_cur("sync", "TEV")
    dma("sync", d_wiscr.rearrange("p (a b) -> p a b", a=KCH), wiT_b[:], sem="W1")
    flush("W1", "W1B")
    barrier()

    for rc in range(4):
        hh = rc // 2
        par = rc % 2
        sw = "W0" if par == 0 else "W0B"
        wait("sync", "V0", cnt["V0"] - 1)
        dma("sync", wstage[par][:], d_wq[rc * 128:(rc + 1) * 128, :], sem=sw)
        wait_cur("vector", sw)
        wait("vector", "PEV", cnt["PEV"])
        op("vector", lambda e, par=par: e.tensor_copy(out=wcast[:], in_=wstage[par][:]))
        drain("vector", sem="V0")
        wait_cur("tensor", "V0")
        wait_cur("scalar", "TEV")
        for g in range(8):
            bnk = g % 2
            wait("tensor", "PEV", cnt["PBK"] - 1)
            op("tensor", lambda e, g=g, bnk=bnk, hh=hh: e.matmul(
                ps_pj[bnk][:], c_T[:, hh * 128:(hh + 1) * 128],
                wcast[:, g * 512:(g + 1) * 512], start=True, stop=True), sem="PBK")
            wait_cur("scalar", "PBK")
            op("scalar", lambda e, g=g, bnk=bnk: e.copy(
                out=wcast2[:, g * 512:(g + 1) * 512], in_=ps_pj[bnk][:]))
            drain("scalar", sem="PEV")
        wait_cur("tensor", "PEV")
        transp32(wcast2, wqT, slice(rc * 128, (rc + 1) * 128), rc)
    flush("W0", "W0B")
    barrier()

    TEV0 = cnt["TEV"]
    TPS0 = cnt["TPS"]
    SSM0 = cnt["SSM"]
    VS0 = cnt["VS"]
    SSQ0 = cnt["SSQ"]
    VRS0 = cnt["VRS"]
    SXN0 = cnt["SXN"]
    VW0 = cnt["VW"]
    jb = jq[0]

    def p1():
        LDXc = {}
        pev_hist = []
        pbk = [cnt["PBK"]]

        def em_load(u):
            if not (0 <= u < NT):
                return
            sl = "LDX" if u % 2 == 0 else "LDXB"
            if u >= 2:
                wait("sync", "VT", 2 * (jb + 4 * (u - 2) + 4))
            dma("sync", xst[u % 2][:], d_x[u * 128:(u + 1) * 128, :], sem=sl)
            LDXc[u] = cnt[sl]

        def em_stats(u):
            if not (0 <= u < NT):
                return
            p = u % 2
            sl = "LDX" if p == 0 else "LDXB"
            wait("vector", sl, LDXc[u])
            wait("vector", "VS", VS0 + 3 * u)
            wait("vector", "SSM", SSM0 + max(0, u - 1))
            for i in range(8):
                op("vector", lambda e, p=p, i=i: e.bn_stats(
                    out=st_bn[:, i * 6:(i + 1) * 6],
                    in_=xst[p][:, i * 512:(i + 1) * 512]), sem="VSF")
            op("vector", lambda e, p=p: e.reduce_max(
                out=st_xmax[:, p:p + 1], in_=xst[p][:], axis=AX), sem="VS")
            op("vector", lambda e, p=p: e.tensor_reduce(
                out=st_xmin[:, p:p + 1], in_=xst[p][:], axis=AX, op=A.min), sem="VS")
            selfwait("vector")
            op("vector", lambda e, p=p: e.bn_aggr(
                out=st_mv[:, 2 * p:2 * p + 2], in_=st_bn[:]), sem="VS")

        def em_smalls(u):
            if not (0 <= u < NT):
                return
            p = u % 2
            wait("scalar", "VS", VS0 + 3 * (u + 1))
            selfwait("scalar")
            op("scalar", lambda e, p=p: e.activation(
                out=st_varc[:, p:p + 1], in_=st_mv[:, 2 * p + 1:2 * p + 2],
                func=ACT.Relu), sem="SSF")
            selfwait("scalar")
            op("scalar", lambda e, p=p: e.activation(
                out=st_varc[:, p:p + 1], in_=st_varc[:, p:p + 1], func=ACT.Ln,
                bias=c_eps[:]), sem="SSF")
            selfwait("scalar")
            op("scalar", lambda e, p=p: e.activation(
                out=st_rstd[:, p:p + 1], in_=st_varc[:, p:p + 1], func=ACT.Exp,
                scale=-0.5), sem="SSF")
            selfwait("scalar")
            op("scalar", lambda e, p=p: e.mul(
                out=st_nrstd[:, p:p + 1], in_=st_rstd[:, p:p + 1], mul=-1.0),
                sem="SSF")
            selfwait("scalar")
            op("scalar", lambda e, p=p: e.activation(
                out=st_nmrs[:, p:p + 1], in_=st_mv[:, 2 * p:2 * p + 1],
                func=ACT.Identity, scale=st_nrstd[:, p:p + 1]), sem="SSF")
            selfwait("scalar")
            op("scalar", lambda e, p=p: e.mul(
                out=st_mrs[:, p:p + 1], in_=st_nmrs[:, p:p + 1], mul=-1.0), sem="SSF")
            op("scalar", lambda e, p=p: e.activation(
                out=st_a[:, p:p + 1], in_=st_xmax[:, p:p + 1], func=ACT.Identity,
                scale=st_rstd[:, p:p + 1], bias=st_nmrs[:, p:p + 1]), sem="SSF")
            selfwait("scalar")
            op("scalar", lambda e, p=p: e.activation(
                out=st_b[:, p:p + 1], in_=st_xmin[:, p:p + 1], func=ACT.Identity,
                scale=st_nrstd[:, p:p + 1], bias=st_mrs[:, p:p + 1]), sem="SSM")

        def em_tinies(u):
            p = u % 2
            wait("vector", "SSM", SSM0 + u + 1)
            op("vector", lambda e, p=p: e.tensor_max(
                out=st_amax[:, p:p + 1], in0=st_a[:, p:p + 1], in1=st_b[:, p:p + 1]),
                sem="VSF")
            selfwait("vector")
            op("vector", lambda e, p=p: e.tensor_scalar(
                out=st_sx[:, p:p + 1], in0=st_amax[:, p:p + 1],
                scalar1=1.0 / 127.0, scalar2=1e-8, op0=A.mult, op1=A.max), sem="VSF")
            selfwait("vector")
            op("vector", lambda e, p=p: e.reciprocal(
                out=st_rx[:, p:p + 1], in_=st_sx[:, p:p + 1]), sem="VSF")
            op("vector", lambda e, p=p, u=u: e.tensor_mul(
                out=sxw[:, u:u + 1], in0=st_sx[:, p:p + 1], in1=s_wcol[:]), sem="VSF")
            selfwait("vector")
            op("vector", lambda e, p=p: e.tensor_mul(
                out=st_rr[:, p:p + 1], in0=st_rstd[:, p:p + 1], in1=st_rx[:, p:p + 1]),
                sem="VSF")
            op("vector", lambda e, p=p: e.scalar_tensor_tensor(
                out=st_bb[:, p:p + 1], in0=st_nmrs[:, p:p + 1],
                scalar=st_rx[:, p:p + 1], in1=c_2565[:], op0=A.mult, op1=A.add),
                sem="VSF")

        def em_xn(u):
            p = u % 2
            wait("scalar", "TPS", TPS0 + 4 * u)
            selfwait("scalar")
            op("scalar", lambda e, p=p: e.activation(
                out=xn_tok[p][:], in_=xst[p][:], func=ACT.Identity,
                scale=st_rstd[:, p:p + 1], bias=st_nmrs[:, p:p + 1]), sem="SXN")

        def em_makew(u):
            p = u % 2
            wait("vector", "SXN", SXN0 + u + 1)
            selfwait("vector")
            op("vector", lambda e, p=p: e.tensor_scalar(
                out=xst[p][:], in0=xst[p][:], scalar1=st_rr[:, p:p + 1],
                scalar2=st_bb[:, p:p + 1], op0=A.mult, op1=A.add), sem="VW")

        def em_cmps(u):
            if not (0 <= u < NT):
                return
            p = u % 2
            js = [jb + 4 * u + q for q in range(4)]
            ch_C12(js[0], p, 0)
            ch_C12(js[1], p, 1)
            wait("vector", "DMO", 16 * u)
            ch_X1(js[0], xi_tok[0][:, 0:1024])
            ch_C12(js[2], p, 2)
            ch_X1(js[1], xi_tok[0][:, 1024:2048])
            ch_C12(js[3], p, 3)
            ch_X1(js[2], xi_tok[0][:, 2048:3072])
            ch_X1(js[3], xi_tok[0][:, 3072:4096])

        def em_transp(u, g0, g1):
            p = u % 2
            m = u // 2
            wait("tensor", "SXN", SXN0 + u + 1)
            for g in range(g0, g1):
                gt = 4 * u + g
                bnk = gt % 2
                wait("tensor", "TEV", TEV0 + gt - 1)
                for jx in range(8):
                    f = g * 8 + jx
                    op("tensor", lambda e, p=p, f=f, bnk=bnk, jx=jx: e.matmul(
                        pb_t[bnk][:, jx * 128:(jx + 1) * 128],
                        xn_tok[p][:, f * 128:(f + 1) * 128], c_id[:],
                        start=(jx == 0), stop=(jx == 7), is_transpose=True),
                        sem=("TPS" if jx == 7 else None))
                wait("scalar", "TPS", TPS0 + gt + 1)
                wait("scalar", "PRJ", max(0, m - 1))
                op("scalar", lambda e, g=g, bnk=bnk, half=u % 2, mp=m % 2: e.copy(
                    out=xn_fm2[mp][:, g * 8:(g + 1) * 8, half * 128:half * 128 + 128],
                    in_=pb_t[bnk][:]), sem="TEV")

        def em_proj_q(jm):
            jp = jm % 2
            wait("tensor", "TEV", TEV0 + 8 * (jm + 1))
            wait("scalar", "KVQ", 160 * jm)
            for ch in range(4):
                b = pbk[0] % 6
                if len(pev_hist) >= 6:
                    wait("tensor", *pev_hist[-6])
                for k in range(KCH):
                    op("tensor", lambda e, ch=ch, k=k, b=b, jp=jp: e.matmul(
                        pb_f[b][:, 0:256], wqT[:, k, ch * 128:(ch + 1) * 128],
                        xn_fm2[jp][:, k, :], start=(k == 0), stop=(k == KCH - 1)),
                        sem=("PBK" if k == KCH - 1 else None))
                pbk[0] += 1
                wait("scalar", "PBK", pbk[0])
                op("scalar", lambda e, ch=ch, b=b: e.activation(
                    out=q_sb[:, ch, :], in_=pb_f[b][:, 0:256], func=ACT.Identity,
                    bias=bqe[:, ch:ch + 1]), sem="PEV")
                pev_hist.append(("PEV", cnt["PEV"]))

        def em_proj_k(jm):
            jp = jm % 2
            for ch in range(4):
                b = pbk[0] % 6
                if len(pev_hist) >= 6:
                    wait("tensor", *pev_hist[-6])
                for k in range(KCH):
                    op("tensor", lambda e, ch=ch, k=k, b=b, jp=jp: e.matmul(
                        pb_f[b][:, 0:256], wkT[:, k, ch * 128:(ch + 1) * 128],
                        xn_fm2[jp][:, k, :], start=(k == 0), stop=(k == KCH - 1)),
                        sem=("PBK" if k == KCH - 1 else None))
                pbk[0] += 1
                wait("scalar", "PBK", pbk[0])
                op("scalar", lambda e, ch=ch, b=b: e.activation(
                    out=k_sb[:, ch, :], in_=pb_f[b][:, 0:256], func=ACT.Identity,
                    bias=bkT[:, ch:ch + 1]), sem="PEV")
                pev_hist.append(("PEV", cnt["PEV"]))

        def em_proj_v(jm):
            jp = jm % 2
            for tc in range(2):
                b = pbk[0] % 6
                if len(pev_hist) >= 6:
                    wait("tensor", *pev_hist[-6])
                for k in range(KCH):
                    op("tensor", lambda e, tc=tc, k=k, b=b, jp=jp: e.matmul(
                        pb_f[b][:], xn_fm2[jp][:, k, tc * 128:(tc + 1) * 128],
                        wvT[:, k, :], start=(k == 0), stop=False), sem=None)
                op("tensor", lambda e, tc=tc, b=b: e.matmul(
                    pb_f[b][:], c_onesrb[:], b_v_bf[:],
                    start=False, stop=True), sem="PBK")
                pbk[0] += 1
                wait("scalar", "PBK", pbk[0])
                op("scalar", lambda e, tc=tc, b=b: e.copy(
                    out=v_sb[:, tc, :], in_=pb_f[b][:]), sem="PEVK")
                pev_hist.append(("PEVK", cnt["PEVK"]))
            drain("tensor", sem="PRJ")
            PEV_AT[jm] = (cnt["PEV"], cnt["PEVK"])

        def em_kvq_stores(mm):
            wait("sync", "PEV", PEV_AT[mm][0])
            wait("sync", "PEVK", PEV_AT[mm][1])
            for ch in range(4):
                dma("sync", d_kscr[ch * 128:(ch + 1) * 128, mm * 256:(mm + 1) * 256],
                    k_sb[:, ch, :], sem="KVQ")
                dma("sync", d_qscr[ch * 128:(ch + 1) * 128, mm * 256:(mm + 1) * 256],
                    q_sb[:, ch, :], sem="KVQ")
            for tc in range(2):
                dma("sync", d_vscr[mm * 256 + tc * 128:mm * 256 + (tc + 1) * 128, :],
                    v_sb[:, tc, :], sem="KVQ")

        em_load(0)
        em_stats(0)
        em_smalls(0)

        for t in range(NT):
            p = t % 2
            m = t // 2
            if t % 2 == 0 and m >= 2:
                em_kvq_stores(m - 2)
            em_load(t + 1)
            if t >= 1:
                wait("sync", "GX", jb + 4 * (t - 1) + 4)
                dma("sync", d_xiscr[(t - 1) * 128:t * 128, :], xi_tok[0][:], sem="DMO")
            em_xn(t)
            if t % 2 == 0 and m >= 1:
                em_proj_q(m - 1)
            em_transp(t, 0, 2)
            if t % 2 == 0 and m >= 1:
                em_proj_k(m - 1)
            em_transp(t, 2, 4)
            if t % 2 == 0 and m >= 1:
                em_proj_v(m - 1)
            em_cmps(t - 1)
            em_tinies(t)
            em_makew(t)
            vw_now = cnt["VW"]
            for q in range(4):
                ch_S1(jb + 4 * t + q, p, q, vw_now)
                if q >= 1:
                    ch_S2(jb + 4 * t + q - 1)
            ch_S2(jb + 4 * t + 3)
            if t >= 1:
                for q in range(4):
                    ch_X2(jb + 4 * (t - 1) + q,
                          xi_tok[0][:, q * 1024:(q + 1) * 1024])
            for q in range(4):
                ch_E(jb + 4 * t + q, p, q)
            em_stats(t + 1)
            em_smalls(t + 1)

        em_cmps(NT - 1)
        for q in range(4):
            ch_X2(jb + 4 * (NT - 1) + q, xi_tok[0][:, q * 1024:(q + 1) * 1024])
        wait("sync", "GX", jb + 4 * NT)
        dma("sync", d_xiscr[(NT - 1) * 128:NT * 128, :], xi_tok[0][:], sem="DMO")
        em_proj_q(NMAC - 1)
        em_proj_k(NMAC - 1)
        em_proj_v(NMAC - 1)
        for mm in (NMAC - 2, NMAC - 1):
            em_kvq_stores(mm)

    PEV_AT = {}
    p1()
    flush("LDX", "LDXB", "DMO", "KVQ")
    barrier()

    nst = [xi2_tok[0], xi2_tok[1]]
    nsq = [at("nsq0", [128, DM], BF16, 114 * KB), at("nsq1", [128, DM], BF16, 122 * KB)]
    for scr, dstmax in ((d_qscr, qnmax), (d_kscr, knmax)):
        for h in range(2):
            for half in range(2):
                wait("sync", "SQN", cnt["SQN"])
                for rr_ in range(2):
                    dma("sync", nst[rr_][:],
                        scr[(2 * h + rr_) * 128:(2 * h + rr_ + 1) * 128,
                            half * DM:(half + 1) * DM], sem="KLD")
                wait_cur("scalar", "KLD")
                wait("scalar", "TNM", cnt["TNM"])
                selfwait("scalar")
                for rr_ in range(2):
                    op("scalar", lambda e, rr_=rr_: e.activation(
                        out=nsq[rr_][:], in_=nst[rr_][:], func=ACT.Square), sem="SQN")
                for cb in range(8):
                    bnk = cb % 2
                    wait("tensor", "SQN", cnt["SQN"])
                    wait("tensor", "NRM", max(0, cnt["NRM"] - 1))
                    for rr_ in range(2):
                        op("tensor", lambda e, rr_=rr_, cb=cb, bnk=bnk: e.matmul(
                            pb_f[bnk][0:1, 0:512], c_onesb[:],
                            nsq[rr_][:, cb * 512:(cb + 1) * 512],
                            start=(rr_ == 0), stop=(rr_ == 1)),
                            sem=("TNM" if rr_ == 1 else None))
                    wait("vector", "TNM", cnt["TNM"])
                    wait("vector", "NRM", cnt["NRM"])
                    op("vector", lambda e, bnk=bnk: e.reduce_max(
                        out=sm_all[0:1, 4:5], in_=pb_f[bnk][0:1, 0:512], axis=AX),
                        sem="VSF")
                    selfwait("vector")
                    op("vector", lambda e, dstmax=dstmax: e.tensor_max(
                        out=dstmax, in0=dstmax, in1=sm_all[0:1, 4:5]), sem="NRM")
    barrier()
    op("scalar", lambda e: e.activation(out=cc_q, in_=qnmax, func=ACT.Sqrt))
    op("scalar", lambda e: e.activation(out=cc_k, in_=knmax, func=ACT.Sqrt))
    drain("scalar", sem="CB1")
    wait_cur("vector", "CB1")
    op("vector", lambda e: e.tensor_mul(out=cc_c, in0=cc_q, in1=cc_k))
    drain("vector")
    op("vector", lambda e: e.tensor_scalar_mul(out=cc_c2, in0=cc_c, scalar1=-0.5))
    drain("vector", sem="CB2")
    wait_cur("tensor", "CB2")
    op("tensor", lambda e: e.matmul(ps_m0[:, 0:1], c_onesf[:], cc_c2,
                                    start=True, stop=True), sem="CB3")
    wait_cur("scalar", "CB3")
    op("scalar", lambda e: e.copy(out=eb_exp[:], in_=ps_m0[:, 0:1]))
    drain("scalar", sem="CB1")
    dma("sync", wiT[:], d_wiscr.rearrange("p (a b) -> p a b", a=KCH), sem="KLD")
    flush("KLD")
    barrier()

    v_re = d_vscr.rearrange("(mc p) f -> p mc f", p=128)

    glt = [0]
    ridx2 = [0]
    qk_i = [0]
    ffb = [0]

    def p2_head(hh):
        for ch in range(2):
            dma("sync", K_res[:, ch, :],
                d_kscr[hh * 256 + ch * 128:hh * 256 + (ch + 1) * 128, :], sem="KLD")
        dma("sync", V_res[:, :, 0:256], v_re[:, :, hh * 256:(hh + 1) * 256], sem="KLD")
        wait_cur("vector", "KLD")
        op("vector", lambda e: e.memset(V_res[:, :, 256:257], 1.0))
        drain("vector", sem="NRD")
        wait_cur("tensor", "NRD")
        wait_cur("scalar", "CB1")
        for lt in range(NLT):
            g = glt[0]
            wait("sync", "QF", g - 1)
            sq_ = "QLD" if lt % 2 == 0 else "QLDB"
            for ch in range(2):
                dma("sync", q2_sb[lt % 2][:, ch, :],
                    d_qscr[hh * 256 + ch * 128:hh * 256 + (ch + 1) * 128,
                           lt * 512:(lt + 1) * 512], sem=sq_)
            xi2c = {}
            if hh == 1:
                for lc in range(4):
                    row = lt * 512 + lc * 128
                    wait("sync", "VOUT", max(0, cnt["VOUT"] - 4 + lc + 1))
                    wait("sync", "TXF", max(0, 4 * (lt - 1) + lc + 3))
                    sx2 = "XI2" if lc % 2 == 0 else "XI2B"
                    sa2 = "A2" if lc % 2 == 0 else "A2B"
                    ss2 = "XSL" if lc % 2 == 0 else "XSLB"
                    dma("sync", xi2_tok[lc % 2][:], d_xiscr[row:row + 128, :], sem=sx2)
                    xi2c[lc] = (sx2, cnt[sx2])
                    dma("sync", a0_sb[lc][:], d_ascr[row:row + 128, :], sem=sa2)
                    dma("sync", xsl_sb[lc][:], d_xsl[row:row + 128, :], sem=ss2)

            def em_ffn(lc):
                wait("tensor", *xi2c[lc])
                for gg in range(4):
                    bnk = ridx2[0] % 2
                    if gg == 0:
                        wait("tensor", "FFV", ffb[0])
                    wait("tensor", "TEV", cnt["TEV"] - 1)
                    for j in range(8):
                        f = gg * 8 + j
                        op("tensor", lambda e, lc=lc, f=f, bnk=bnk, j=j: e.matmul(
                            pb_t[bnk][:, j * 128:(j + 1) * 128],
                            xi2_tok[lc % 2][:, f * 128:(f + 1) * 128], c_id[:],
                            start=(j == 0), stop=(j == 7), is_transpose=True),
                            sem=("TPS" if j == 7 else None))
                    wait("vector", "TPS", cnt["TPS"])
                    op("vector", lambda e, gg=gg, bnk=bnk, lc=lc: e.tensor_copy(
                        out=xi2_fm[:, gg * 8:(gg + 1) * 8, lc * 128:(lc + 1) * 128],
                        in_=pb_t[bnk][:]), sem="TEV")
                    ridx2[0] += 1
                drain("tensor", sem="TXF")
                wait("tensor", "TEV", cnt["TEV"])
                for k in range(KCH):
                    op("tensor", lambda e, lc=lc, k=k: e.matmul(
                        pb_t[0][:].bitcast(F32), xi2_fm[:, k, lc * 128:(lc + 1) * 128],
                        wiT[:, k, :], start=(k == 0), stop=(k == KCH - 1)),
                        sem=("FPB" if k == KCH - 1 else None))
                wait("vector", "FPB", cnt["FPB"])
                t_idx = lt * 4 + lc
                op("vector", lambda e, lc=lc, t_idx=t_idx: e.tensor_scalar_mul(
                    out=ffn_sb[lc][:], in0=pb_t[0][:].bitcast(F32),
                    scalar1=sxw[:, t_idx:t_idx + 1]), sem="FFV")
                ffb[0] += 1

            wait("tensor", sq_, cnt[sq_])
            sbk = [pb_f[0][:], pb_f[1][:]]
            if hh == 0:
                sbk += [pb_t[0][:].bitcast(F32), pb_t[1][:].bitcast(F32)]
            nb = len(sbk)
            LA = nb - 1

            def emit_qk(i, mc):
                b = i % nb
                wait("tensor", "ER", i - (nb - 1))
                for d in range(2):
                    op("tensor", lambda e, d=d, mc=mc, sb_=sbk[b], lt=lt: e.matmul(
                        sb_, K_res[:, d, mc * 128:(mc + 1) * 128],
                        q2_sb[lt % 2][:, d, :], start=(d == 0), stop=(d == 1)),
                        sem=("SR" if d == 1 else None))
                wait("scalar", "SR", i + 1)
                wait("scalar", "EF", i - 3)
                op("scalar", lambda e, sb_=sbk[b], e3=i % 4: e.activation(
                    out=expS[e3][:], in_=sb_, func=ACT.Exp,
                    bias=eb_exp[:]), sem="ER")

            def emit_pv(i, mc):
                e3 = i % 4
                wait("tensor", "ER", i + 1)
                for lc in range(4):
                    op("tensor", lambda e, lc=lc, e3=e3, mc=mc: e.matmul(
                        pvs(lc, 0, 257), expS[e3][:, lc * 128:(lc + 1) * 128],
                        V_res[:, mc, :], start=(mc == 0), stop=(mc == MCN - 1)),
                        sem=("EF" if lc == 3 else None))

            i0 = qk_i[0]
            qk_i[0] += MCN
            wait("tensor", "OF", 4 * g)
            for k in range(LA):
                emit_qk(i0 + k, k)
            for mc in range(LA, MCN):
                if hh == 1 and mc in (8, 24, 40, 56):
                    em_ffn(mc // 16)
                emit_qk(i0 + mc, mc)
                emit_pv(i0 + mc - LA, mc - LA)
            for k in range(MCN - LA, MCN):
                emit_pv(i0 + k, k)
            drain("tensor", sem="QF")
            for lc in range(4):
                wait("vector", "EF", 64 * (g + 1))
                wait("vector", "OF", cnt["OF"])
                op("vector", lambda e, lc=lc: e.reciprocal(
                    out=rec_sb[:], in_=pvs(lc, 256, 257)), sem="VSF")
                selfwait("vector")
                if hh == 0:
                    sad = "ASD" if lc % 2 == 0 else "ASDB"
                    wait("vector", sad, cnt[sad])
                    op("vector", lambda e, lc=lc: e.tensor_scalar_mul(
                        out=a0_sb[lc][:], in0=pvs(lc, 0, 256),
                        scalar1=rec_sb[:]), sem="OF")
                    wait("sync", "OF", cnt["OF"])
                    row = lt * 512 + lc * 128
                    dma("sync", d_ascr[row:row + 128, :], a0_sb[lc][:], sem=sad)
                else:
                    op("vector", lambda e, lc=lc: e.tensor_scalar_mul(
                        out=attn_f[lc][:, 256:512], in0=pvs(lc, 0, 256),
                        scalar1=rec_sb[:]), sem="OF")
                    wait("vector", "A2" if lc % 2 == 0 else "A2B",
                         cnt["A2" if lc % 2 == 0 else "A2B"])
                    wait("vector", "XSL" if lc % 2 == 0 else "XSLB",
                         cnt["XSL" if lc % 2 == 0 else "XSLB"])
                    op("vector", lambda e, lc=lc: e.tensor_copy(
                        out=attn_f[lc][:, 0:256], in_=a0_sb[lc][:]), sem="VSF")
                    selfwait("vector")
                    op("vector", lambda e, lc=lc: e.tensor_add(
                        out=tadd_sb[lc][:], in0=attn_f[lc][:], in1=ffn_sb[lc][:]),
                        sem="VSF")
                    selfwait("vector")
                    wait("vector", "OUT", 16 * max(0, cnt["VOUT"] - lc))
                    op("vector", lambda e, lc=lc: e.tensor_add(
                        out=out_sb[lc][:], in0=tadd_sb[lc][:], in1=xsl_sb[lc][:]),
                        sem="VOUT")
                    wait("sync", "VOUT", cnt["VOUT"])
                    row = lt * 512 + lc * 128
                    dma("sync", d_out[row:row + 128, :], out_sb[lc][:], sem="OUT")
            glt[0] += 1

    p2_head(0)
    flush("ASD", "ASDB", "QLD", "QLDB", "KLD")
    barrier()
    p2_head(1)
    wait_cur("sync", "OUT")
    wait_cur("sync", "ASD")
    if dev:
        op("vector", lambda e: e.memset(st_bn[:, 0:1], 0.0))
        drain("vector", sem="V0")
        wait_cur("sync", "V0")
        dma("sync", d_dbg[:, 0:64], sxw[:], sem="OUT", slow=True)
        dma("sync", d_dbg[:, 64:65], s_wcol[:], sem="OUT", slow=True)
        dma("sync", d_dbg[:, 65:66], r_wcol[:], sem="OUT", slow=True)
        dma("sync", d_dbg[:, 66:67], eb_exp[:], sem="OUT", slow=True)
        dma("sync", d_dbg[0:1, 67:68], qnmax, sem="OUT", slow=True)
        dma("sync", d_dbg[0:1, 68:69], knmax, sem="OUT", slow=True)
        wait_cur("sync", "OUT")

    for role, semn, val in deferred_waits:
        assert val <= cnt[semn], ("unsatisfiable wait", role, semn, val, cnt[semn])
    with nc.Block() as block:
        for role in ROLES:
            getattr(block, role)(lambda eng, role=role: [f(eng) for f in plan[role]])
    return nc


def make_in_maps(inputs, ncore=8):
    cst = host_constants()
    x = np.ascontiguousarray(np.asarray(inputs["L0_attn_x"], np.float32).reshape(L, DM))
    wq = np.asarray(inputs["L0_attn_Wq"], np.float32)
    wk = np.asarray(inputs["L0_attn_Wk"], np.float32)
    wv = np.asarray(inputs["L0_attn_Wv"], np.float32)
    bq = np.asarray(inputs["L0_attn_bq"], np.float32)
    bk = np.asarray(inputs["L0_attn_bk"], np.float32)
    bv = np.asarray(inputs["L0_attn_bv"], np.float32)
    bw = np.asarray(inputs["L0_attn_bias_weight"], np.float32)
    wf = np.ascontiguousarray(np.asarray(inputs["L0_ffn_weight"], np.float32))
    maps = []
    for c in range(ncore):
        sl = slice(FS * c, FS * (c + 1))
        m = dict(
            x=x,
            xsl=np.ascontiguousarray(x[:, sl]),
            wq=np.ascontiguousarray(wq[sl]),
            wk=np.ascontiguousarray(wk[sl]),
            wv=np.ascontiguousarray(wv[sl]),
            wf=wf,
            wfs=np.ascontiguousarray(wf[sl]),
            bq=np.ascontiguousarray(bq[sl]).reshape(1, FS),
            bk=np.ascontiguousarray(bk[sl]).reshape(1, FS),
            bv=np.ascontiguousarray(bv[sl]).reshape(1, FS),
            bwT=np.ascontiguousarray(bw[2 * c:2 * c + 2].T),
            **cst,
        )
        maps.append(m)
    return maps


def assemble(results):
    return np.concatenate([np.asarray(r["out"]) for r in results],
                          axis=1).reshape(1, L, DM)


from concourse.bass_utils import run_bass_kernel_spmd

_NC_CACHE = {}


def kernel(**inputs):
    maps = make_in_maps(inputs, ncore=8)
    if "nc" not in _NC_CACHE:
        _NC_CACHE["nc"] = build(dev=False)
    nc = _NC_CACHE["nc"]
    res = run_bass_kernel_spmd(nc, maps, core_ids=list(range(8)))
    return assemble(res.results)


# version 12
# speedup vs baseline: 1.6867x; 1.0284x over previous
import math
import numpy as np
import ml_dtypes
import concourse.bass as bass
import concourse.mybir as mybir

F32 = mybir.dt.float32
BF16 = mybir.dt.bfloat16
I32 = mybir.dt.int32
A = mybir.AluOpType
AX = mybir.AxisListType.X
ACT = mybir.ActivationFunctionType
BF = ml_dtypes.bfloat16

L = 8192
DM = 4096
FS = 512
HD = 256
NT = 64
NMAC = 32
NLT = 16
MCN = 64
KCH = 32
EPS = 1e-5
CLIP1 = 1.4999
SHIFT = 256.5
ROLES = ("sync", "vector", "scalar", "tensor", "gpsimd")
KB = 1024


def _build_tables():
    g = [1.0, 1.0, 1.0]
    n = 8
    masks = sorted(range(n), key=lambda m: (bin(m).count("1"), m))
    inv = {m: i for i, m in enumerate(masks)}
    C = np.zeros((n, n, n), dtype=np.float32)
    for i, a in enumerate(masks):
        for j, b in enumerate(masks):
            sw, at_ = (0, a >> 1)
            while at_:
                sw += bin(at_ & b).count("1")
                at_ >>= 1
            s = -1.0 if sw & 1 else 1.0
            for t in range(3):
                if a >> t & 1 and b >> t & 1:
                    s *= float(g[t])
            C[i, j, inv[a ^ b]] = s
    grades = np.array([bin(m).count("1") for m in masks])
    rev = np.where(grades * (grades - 1) // 2 % 2 == 1, -1.0, 1.0).astype(np.float32)
    met = np.array([float(np.prod([g[t] for t in range(3) if m >> t & 1] + [1.0]))
                    for m in masks], dtype=np.float32)
    return C, rev, met


def host_constants():
    C, rev, met = _build_tables()
    biv = C[:, :, [4, 5, 6]]
    scale = 1.0 / math.sqrt(HD)
    bivf = np.ascontiguousarray(biv.transpose(2, 0, 1).reshape(3, 64)).astype(np.float32)
    rconst = np.tile(np.repeat(rev, 8)[None, :], (2, 1)).astype(np.float32)
    dconst = np.tile((scale * np.diag(rev * met)).reshape(1, 64), (2, 1)).astype(np.float32)
    return dict(
        bivf=bivf, rconst=rconst, dconst=dconst,
        iden=np.eye(128).astype(BF),
        onesf=np.ones((1, 128), np.float32),
        onesb=np.ones((128, 1)).astype(BF),
        onesrb=np.ones((1, 128)).astype(BF),
        onescf=np.ones((128, 1), np.float32),
    )


def build(dev=False):
    nc = bass.Bass("TRN2", target_bir_lowering=False, debug=False)

    d_x = nc.dram_tensor("x", [L, DM], F32, kind="ExternalInput").ap()
    d_xsl = nc.dram_tensor("xsl", [L, FS], F32, kind="ExternalInput").ap()
    d_wq = nc.dram_tensor("wq", [FS, DM], F32, kind="ExternalInput").ap()
    d_wk = nc.dram_tensor("wk", [FS, DM], F32, kind="ExternalInput").ap()
    d_wv = nc.dram_tensor("wv", [FS, DM], F32, kind="ExternalInput").ap()
    d_wf = nc.dram_tensor("wf", [DM, DM], F32, kind="ExternalInput").ap()
    d_wfs = nc.dram_tensor("wfs", [FS, DM], F32, kind="ExternalInput").ap()
    d_bq = nc.dram_tensor("bq", [1, FS], F32, kind="ExternalInput").ap()
    d_bk = nc.dram_tensor("bk", [1, FS], F32, kind="ExternalInput").ap()
    d_bv = nc.dram_tensor("bv", [1, FS], F32, kind="ExternalInput").ap()
    d_bwT = nc.dram_tensor("bwT", [3, 2], F32, kind="ExternalInput").ap()
    d_bivf = nc.dram_tensor("bivf", [3, 64], F32, kind="ExternalInput").ap()
    d_rc = nc.dram_tensor("rconst", [2, 64], F32, kind="ExternalInput").ap()
    d_dc = nc.dram_tensor("dconst", [2, 64], F32, kind="ExternalInput").ap()
    d_id = nc.dram_tensor("iden", [128, 128], BF16, kind="ExternalInput").ap()
    d_onesf = nc.dram_tensor("onesf", [1, 128], F32, kind="ExternalInput").ap()
    d_onesb = nc.dram_tensor("onesb", [128, 1], BF16, kind="ExternalInput").ap()
    d_onesrb = nc.dram_tensor("onesrb", [1, 128], BF16, kind="ExternalInput").ap()
    d_onescf = nc.dram_tensor("onescf", [128, 1], F32, kind="ExternalInput").ap()
    d_out = nc.dram_tensor("out", [L, FS], F32, kind="ExternalOutput").ap()

    skind = "ExternalOutput" if dev else "Internal"
    d_kscr = nc.dram_tensor("k_scr", [FS, L], BF16, kind=skind).ap()
    d_vscr = nc.dram_tensor("v_scr", [L, FS], BF16, kind=skind).ap()
    d_qscr = nc.dram_tensor("q_scr", [FS, L], BF16, kind=skind).ap()
    d_xiscr = nc.dram_tensor("xi_scr", [L, DM], BF16, kind=skind).ap()
    d_wiscr = nc.dram_tensor("wi_scr", [128, KCH * FS], BF16, kind=skind).ap()
    d_ascr = nc.dram_tensor("a_scr", [L, HD], BF16, kind=skind).ap()
    if dev:
        d_dbg = nc.dram_tensor("dbg", [128, 80], F32, kind="ExternalOutput").ap()

    ARENA_PP = 207 * KB + 512
    arena = nc.sbuf_tensor("arena", [128, ARENA_PP], mybir.dt.uint8).__enter__()
    base = nc.lookup_mloc(arena).addr

    def at(name, shape, dtype, off):
        end = off + int(np.prod(shape[1:])) * mybir.dt.size(dtype)
        assert end <= ARENA_PP, (name, end)
        return nc.alloc_sbuf_tensor_at(name, shape, dtype, offset=base + off)

    wqT = at("wqT", [128, KCH, FS], BF16, 0)
    wkT = at("wkT", [128, KCH, FS], BF16, 32 * KB)
    wvT = at("wvT", [128, KCH, FS], BF16, 64 * KB)
    xst = [at(f"xst{i}", [128, DM], F32, 96 * KB + 16 * KB * i) for i in range(2)]
    xn_tok = [at("xntok", [128, DM], BF16, 128 * KB)] * 2
    xn_fm2 = [at(f"xnfm{i}", [128, KCH, 256], BF16, 136 * KB + 16 * KB * i) for i in range(2)]
    xi_tok = [at("xitok", [128, DM], BF16, 168 * KB)] * 2
    RS = 176 * KB
    cB = [at(f"cB{i}", [128, 1024], mybir.dt.int16, RS + 2 * KB * i) for i in range(4)]
    cC = [at(f"cC{i}", [128, 1024], BF16, RS + 8 * KB + 2 * KB * i) for i in range(4)]
    cF = [at(f"cF{i}", [128, 1024], BF16, RS + 16 * KB + 2 * KB * i) for i in range(2)]
    k_sb = at("ksb", [128, 4, 256], BF16, 196 * KB)
    v_sb = at("vsb", [128, 2, FS], BF16, 198 * KB)
    q_sb = at("qsb", [128, 4, 256], BF16, 200 * KB)
    wstage = [xst[0], xst[1]]
    wcast = at("wcast", [128, DM], BF16, 128 * KB)
    wcast2 = at("wcast2", [128, DM], BF16, 136 * KB)
    wiT_b = at("wiTb", [128, KCH, FS], BF16, 0)
    wiT = at("wiT", [128, KCH, FS], BF16, 0)
    K_res = at("Kres", [128, 2, L], BF16, 32 * KB)
    V_res = at("Vres", [128, MCN, HD + 1], BF16, 64 * KB)
    xi2_tok = [at(f"xi2t{i}", [128, DM], BF16, 98 * KB + 8 * KB * i) for i in range(2)]
    xi2_fm = at("xi2fm", [128, KCH, FS], BF16, 114 * KB)
    q2_sb = [at(f"q2sb{i}", [128, 2, FS], BF16, 146 * KB + 2 * KB * i) for i in range(2)]
    expS = [at(f"expS{i}", [128, 512], BF16, 150 * KB + KB * i) for i in range(4)]
    attn_f = [at(f"attnf{i}", [128, FS], BF16, 166 * KB + KB * i) for i in range(4)]
    a0_sb = [at(f"a0sb{i}", [128, HD], BF16, 170 * KB + 512 * i) for i in range(4)]
    xsl_sb = [at(f"xslb{i}", [128, FS], F32, 172 * KB + 2 * KB * i) for i in range(4)]
    ffn_sb = [at(f"ffnsb{i}", [128, FS], BF16, 180 * KB + KB * i) for i in range(4)]
    tadd_sb = [at(f"tadd{i}", [128, FS], F32, 184 * KB + 2 * KB * i) for i in range(4)]
    out_sb = [at(f"outsb{i}", [128, FS], F32, 192 * KB + 2 * KB * i) for i in range(4)]

    import contextlib
    es = contextlib.ExitStack()
    _pk = [202 * KB]

    def sb(n, s, d):
        off = _pk[0]
        t = at(n, s, d, off)
        sz = int(np.prod(s[1:])) * mybir.dt.size(d)
        _pk[0] = off + ((sz + 63) // 64) * 64
        return t
    c_id = sb("c_id", [128, 128], BF16)
    c_onesf = sb("c_onesf", [1, 128], F32)
    c_onesb = sb("c_onesb", [128, 1], BF16)
    c_onesrb = sb("c_onesrb", [1, 128], BF16)
    b_v_bf = sb("b_v_bf", [1, FS], BF16)
    c_onescf = sb("c_onescf", [128, 1], F32)
    c_bivf = at("c_bivf", [3, 64], F32, RS + 6 * KB)
    c_rc = at("c_rc", [2, 64], F32, RS + 6 * KB + 256)
    c_dc = at("c_dc", [2, 64], F32, RS + 6 * KB + 512)
    c_bwT = at("c_bwT", [3, 2], F32, RS + 6 * KB + 768)
    c_T = sb("c_T", [128, 256], BF16)
    c_A = at("c_A", [2, 64], F32, RS + 7 * KB)
    c_Abf = at("c_Abf", [2, 64], BF16, RS + 7 * KB + 256)
    b_q = at("b_q", [1, FS], F32, RS)
    b_k = at("b_k", [1, FS], F32, RS + 2 * KB)
    b_v = at("b_v", [1, FS], F32, RS + 4 * KB)
    bkT = sb("bkT", [128, 4], F32)
    bqT = sb("bqT", [128, 4], F32)
    bqTb = sb("bqTb", [128, 4], BF16)
    bqe = sb("bqe", [128, 4], F32)
    st_all = sb("st_all", [128, 84], F32)
    _st = [0]

    def _stc(n):
        a, _st[0] = _st[0], _st[0] + n
        return st_all[:, a:a + n]

    st_bn = _stc(48)
    st_mv = _stc(4)
    st_varc = _stc(2)
    st_rstd = _stc(2)
    st_nrstd = _stc(2)
    st_nmrs = _stc(2)
    st_mrs = _stc(2)
    st_xmax = _stc(2)
    st_xmin = _stc(2)
    st_a = _stc(2)
    st_b = _stc(2)
    st_amax = _stc(2)
    st_sx = _stc(2)
    st_rx = _stc(2)
    st_rr = _stc(2)
    st_bb = _stc(2)
    st_bb1 = _stc(2)
    c_2565 = sb("c_2565", [128, 1], F32)
    c_m257 = sb("c_m257", [128, 1], F32)
    c_eps = sb("c_eps", [128, 1], F32)
    sxw = sb("sxw", [128, NT], F32)
    s_wcol = sb("s_wcol", [128, 1], F32)
    r_wcol = sb("r_wcol", [128, 1], F32)
    wfsum = sb("wfsum", [128, 1], F32)
    wfpart = sb("wfpart", [128, 1], F32)
    sm_all = sb("sm_all", [1, 16], F32)
    s_sw = sm_all[0:1, 0:1]
    s_rw = sm_all[0:1, 1:2]
    qnmax = sm_all[0:1, 2:3]
    knmax = sm_all[0:1, 3:4]
    nrm_red = sm_all[0:1, 4:6]
    cc_q = sm_all[0:1, 6:7]
    cc_k = sm_all[0:1, 7:8]
    cc_c = sm_all[0:1, 8:9]
    cc_c2 = sm_all[0:1, 9:10]
    eb_exp = sb("eb_exp", [128, 1], F32)
    rec_sb = sb("rec_sb", [128, 1], F32)

    pb_f = [es.enter_context(nc.psum_tensor(f"pbf{i}", [128, 512], F32)) for i in range(6)]
    pb_t = [es.enter_context(nc.psum_tensor(f"pbt{i}", [128, 1024], BF16)) for i in range(2)]
    ps_pj = pb_f[0:2]
    ps_nm = pb_f[2]
    ps_m0 = pb_f[3]

    def pvs(lc, a, b2):
        return pb_f[2 + lc][:, a:b2]

    sem_names = [
        "BAR", "C0", "T0", "V0", "S0", "G0", "W0", "W1",
        "LDX", "LDXB", "LDH", "LDHB", "W0B", "W1B", "DMOB", "XI2B", "A2B", "XSLB", "QLDB", "VS", "GS", "SSM", "SXN",
        "VW", "SC1", "SC2", "VSS", "GX", "DMO",
        "TPS", "TEV", "PRJ", "PBK", "PEV", "PEVK", "SQN", "TNM", "NRD", "KVQ",
        "KLD", "QLD", "QF", "SR", "ER", "EF", "OF",
        "A2", "XSL", "XI2", "TXF", "FPB", "FFV", "VOUT", "OUT", "ASD", "ASDB",
        "CB1", "CB2", "CB3", "SSQ", "VRS", "GE1", "VSF", "SSF", "GSF", "VT", "NRM",
    ]
    S = {n: es.enter_context(nc.semaphore(n)) for n in sem_names}

    plan = {r: [] for r in ROLES}
    cnt = {n: 0 for n in sem_names}

    def op(role, fn, sem=None, n=1):
        if sem is not None:
            cnt[sem] += n
            plan[role].append(lambda eng, fn=fn, s=S[sem], m=n: fn(eng).then_inc(s, m))
        else:
            plan[role].append(lambda eng, fn=fn: fn(eng))

    def dma(role, out, in_, sem=None, slow=False):
        op(role, lambda e, o=out, i=in_, sl=slow: e.dma_start(
            out=o, in_=i, allow_slow_non_contiguous=sl), sem=sem, n=16)

    deferred_waits = []

    def wait(role, sem, val):
        deferred_waits.append((role, sem, val))
        if val > 0:
            plan[role].append(lambda eng, s=S[sem], v=val: eng.wait_ge(s, v))

    def wait_cur(role, sem):
        wait(role, sem, cnt[sem])

    def drain(role, sem=None, n=1):
        op(role, lambda e: e.drain(), sem=sem, n=n)

    SELF = {"vector": "VSF", "scalar": "SSF", "gpsimd": "GSF"}

    def selfwait(role):
        wait(role, SELF[role], cnt[SELF[role]])

    def chop(role, fn, sem=None):
        selfwait(role)
        op(role, fn, sem=(sem if sem is not None else SELF[role]))

    nbar = [0]

    def flush(*sems, role="sync"):
        for s_ in sems:
            wait_cur(role, s_)

    def barrier():
        nbar[0] += 1
        k = nbar[0] * len(ROLES)
        cnt["BAR"] += len(ROLES)
        for r in ROLES:
            def f(eng, v=k):
                eng.drain().then_inc(S["BAR"], 1)
                eng.wait_ge(S["BAR"], v)
            plan[r].append(f)

    for dst, src in [(c_id, d_id), (c_onesf, d_onesf), (c_onesb, d_onesb),
                     (c_onescf, d_onescf), (c_bivf, d_bivf), (c_rc, d_rc),
                     (c_dc, d_dc), (c_bwT, d_bwT), (b_q, d_bq), (b_k, d_bk),
                     (b_v, d_bv), (c_onesrb, d_onesrb)]:
        dma("sync", dst[:], src, sem="C0")
    op("vector", lambda e: e.memset(c_2565[:], SHIFT))
    op("vector", lambda e: e.memset(c_m257[:], -257.0))
    op("vector", lambda e: e.memset(c_eps[:], EPS))
    op("vector", lambda e: e.memset(qnmax, 0.0))
    op("vector", lambda e: e.memset(knmax, 0.0))
    op("vector", lambda e: e.memset(wfsum[:], 0.0))
    op("vector", lambda e: e.memset(c_T[:], 0.0))
    flush("C0")
    barrier()

    for f in range(4):
        dma("sync", bkT[:, f:f + 1], b_k[0:1, f * 128:(f + 1) * 128], sem="C0")
        dma("sync", bqT[:, f:f + 1], b_q[0:1, f * 128:(f + 1) * 128], sem="C0")
    op("tensor", lambda e: e.matmul(ps_m0[0:2, 0:64], c_bwT[:], c_bivf[:],
                                    start=True, stop=True), sem="T0")
    wait_cur("vector", "T0")
    op("vector", lambda e: e.tensor_mul(out=c_A[:], in0=ps_m0[0:2, 0:64], in1=c_rc[:]))
    drain("vector")
    op("vector", lambda e: e.tensor_add(out=c_A[:], in0=c_A[:], in1=c_dc[:]))
    drain("vector")
    op("vector", lambda e: e.tensor_copy(out=c_Abf[:], in_=c_A[:]))
    wait_cur("vector", "C0")
    op("vector", lambda e: e.tensor_copy(out=bqTb[:], in_=bqT[:]))
    drain("vector", sem="V0")
    wait_cur("sync", "V0")
    for hh in range(2):
        for dd in range(16):
            dma("sync", c_T[8 * dd:8 * dd + 8, hh * 128 + 8 * dd:hh * 128 + 8 * dd + 8],
                c_Abf[hh:hh + 1, 0:64], sem="C0")
    op("vector", lambda e: e.tensor_copy(out=b_v_bf[:], in_=b_v[:]))
    drain("vector", sem="V0")
    wait_cur("tensor", "C0")
    wait_cur("tensor", "V0")
    for j in range(4):
        if j >= 2:
            wait("tensor", "S0", cnt["S0"] - 1)
        op("tensor", lambda e, j=j: e.matmul(
            ps_pj[j % 2][:, 0:1], c_T[:, (j // 2) * 128:(j // 2) * 128 + 128],
            bqTb[:, j:j + 1], start=True, stop=True), sem="T0")
        wait_cur("scalar", "T0")
        op("scalar", lambda e, j=j: e.copy(out=bqe[:, j:j + 1], in_=ps_pj[j % 2][:, 0:1]))
        drain("scalar", sem="S0")
    flush("C0")
    barrier()

    def transp32(src_bf, dst, rcslice, tag_rc):
        for g in range(4):
            bnk = g % 2
            wait("tensor", "TEV", cnt["TEV"] - 1)
            for j in range(8):
                f = g * 8 + j
                op("tensor", lambda e, f=f, bnk=bnk, j=j, s=src_bf: e.matmul(
                    pb_t[bnk][:, j * 128:(j + 1) * 128], s[:, f * 128:(f + 1) * 128],
                    c_id[:], start=(j == 0), stop=(j == 7), is_transpose=True),
                    sem=("TPS" if j == 7 else None))
            wait_cur("vector", "TPS")
            op("vector", lambda e, g=g, bnk=bnk, d=dst, rcs=rcslice: e.tensor_copy(
                out=d[:, g * 8:(g + 1) * 8, rcs], in_=pb_t[bnk][:]), sem="TEV")

    def w_prep_plain(dsrc, wT):
        for rc in range(4):
            par = rc % 2
            sw = "W0" if par == 0 else "W0B"
            wait("sync", "V0", cnt["V0"] - 1)
            dma("sync", wstage[par][:], dsrc[rc * 128:(rc + 1) * 128, :], sem=sw)
            wait_cur("vector", sw)
            wait("vector", "TEV", cnt["TEV"])
            op("vector", lambda e, par=par: e.tensor_copy(out=wcast[:], in_=wstage[par][:]))
            drain("vector", sem="V0")
            wait_cur("tensor", "V0")
            transp32(wcast, wT, slice(rc * 128, (rc + 1) * 128), rc)

    w_prep_plain(d_wk, wkT)
    w_prep_plain(d_wv, wvT)
    flush("W0", "W0B")
    barrier()

    for ch in range(32):
        par = ch % 2
        sw = "W0" if par == 0 else "W0B"
        wait("sync", "V0", cnt["V0"] - 1)
        dma("sync", wstage[par][:], d_wf[ch * 128:(ch + 1) * 128, :], sem=sw)
        wait("vector", sw, cnt[sw])
        wait("vector", "V0", cnt["V0"])
        op("vector", lambda e, par=par: e.tensor_reduce(
            out=wfpart[:], in_=wstage[par][:], axis=AX, op=A.add,
            apply_absolute_value=True), sem="VSF")
        selfwait("vector")
        op("vector", lambda e: e.tensor_add(out=wfsum[:], in0=wfsum[:], in1=wfpart[:]),
            sem="V0")
    wait_cur("tensor", "V0")
    op("tensor", lambda e: e.matmul(ps_m0[0:1, 0:1], c_onescf[:], wfsum[:],
                                    start=True, stop=True), sem="T0")
    wait_cur("vector", "T0")
    op("vector", lambda e: e.tensor_scalar_mul(out=s_sw, in0=ps_m0[0:1, 0:1],
                                               scalar1=1.0 / (DM * DM)))
    drain("vector")
    op("vector", lambda e: e.tensor_scalar_add(out=s_rw, in0=s_sw, scalar1=1e-8))
    drain("vector")
    op("vector", lambda e: e.reciprocal(out=s_rw, in_=s_rw))
    drain("vector", sem="V0")
    wait_cur("tensor", "V0")
    op("tensor", lambda e: e.matmul(ps_m0[:, 0:1], c_onesf[:], s_sw,
                                    start=True, stop=True))
    op("tensor", lambda e: e.matmul(pb_f[4][:, 0:1], c_onesf[:], s_rw,
                                    start=True, stop=True), sem="T0")
    wait_cur("scalar", "T0")
    op("scalar", lambda e: e.copy(out=s_wcol[:], in_=ps_m0[:, 0:1]))
    op("scalar", lambda e: e.copy(out=r_wcol[:], in_=pb_f[4][:, 0:1]))
    drain("scalar", sem="S0")
    flush("W0", "W0B")
    barrier()

    jq = [0]

    def ch_S1(j, p, q, vw):
        wait("scalar", "VW", vw)
        wait("scalar", "GX", j - 3)
        op("scalar", lambda e, p=p, q=q, j=j: e.copy(
            out=cB[j % 4][:], in_=xst[p][:, q * 1024:(q + 1) * 1024]), sem="SC1")

    def ch_S2(j):
        wait("scalar", "SC1", j + 1)
        wait("scalar", "VSS", j - 3)
        op("scalar", lambda e, j=j: e.activation(
            out=cC[j % 4][:], in_=cB[j % 4][:], func=ACT.Identity,
            bias=c_m257[:]), sem="SC2")

    def ch_E(j, p, q):
        wait("gpsimd", "SC2", j + 1)
        op("gpsimd", lambda e, p=p, q=q, j=j: e.tensor_tensor(
            out=xst[p][:, q * 1024:(q + 1) * 1024],
            in0=xst[p][:, q * 1024:(q + 1) * 1024],
            in1=cC[j % 4][:], op=A.subtract), sem="GE1")

    def ch_C12(j, p, q):
        wait("vector", "GE1", j + 1)
        wait("vector", "VSS", j - 1)
        op("vector", lambda e, p=p, q=q, j=j: e.tensor_single_scalar(
            out=cF[j % 2][:], in_=xst[p][:, q * 1024:(q + 1) * 1024],
            scalar=257.0, op=A.is_ge), sem="VT")
        op("vector", lambda e, p=p, q=q, j=j: e.tensor_single_scalar(
            out=cB[j % 4][:].bitcast(BF16), in_=xst[p][:, q * 1024:(q + 1) * 1024],
            scalar=258.0, op=A.is_ge), sem="VT")

    def ch_X1(j, dst_ap):
        wait("vector", "VT", 2 * (j + 1))
        op("vector", lambda e, j=j, d=dst_ap: e.tensor_tensor(
            out=d, in0=cC[j % 4][:], in1=cF[j % 2][:], op=A.add), sem="VSS")

    def ch_X2(j, dst_ap):
        wait("gpsimd", "VSS", j + 1)
        op("gpsimd", lambda e, j=j, d=dst_ap: e.tensor_tensor(
            out=d, in0=d, in1=cB[j % 4][:].bitcast(BF16), op=A.add), sem="GX")

    tev_hist = [cnt["TEV"]]
    for rc in range(4):
        par = rc % 2
        sw = "W1" if par == 0 else "W1B"
        if rc >= 2:
            wait("sync", "VT", 2 * (4 * (rc - 2) + 4))
        dma("sync", wstage[par][:], d_wfs[rc * 128:(rc + 1) * 128, :], sem=sw)
        wait("vector", sw, cnt[sw])
        selfwait("vector")
        op("vector", lambda e, par=par: e.tensor_scalar(
            out=wstage[par][:], in0=wstage[par][:], scalar1=r_wcol[:],
            scalar2=CLIP1, op0=A.mult, op1=A.min), sem="VSF")
        selfwait("vector")
        op("vector", lambda e, par=par: e.tensor_scalar(
            out=wstage[par][:], in0=wstage[par][:], scalar1=-CLIP1,
            scalar2=SHIFT, op0=A.max, op1=A.add), sem="VW")
        vw_now = cnt["VW"]
        for q in range(4):
            j = jq[0]
            jq[0] += 1
            ch_S1(j, par, q, vw_now)
            ch_S2(j)
            ch_E(j, par, q)
            ch_C12(j, par, q)
            if q == 0:
                wait("vector", "TEV", tev_hist[-1])
            ch_X1(j, wcast[:, q * 1024:(q + 1) * 1024])
            ch_X2(j, wcast[:, q * 1024:(q + 1) * 1024])
        wait_cur("tensor", "GX")
        transp32(wcast, wiT_b, slice(rc * 128, (rc + 1) * 128), rc)
        tev_hist.append(cnt["TEV"])
    wait_cur("sync", "TEV")
    dma("sync", d_wiscr.rearrange("p (a b) -> p a b", a=KCH), wiT_b[:], sem="W1")
    flush("W1", "W1B")
    barrier()

    for rc in range(4):
        hh = rc // 2
        par = rc % 2
        sw = "W0" if par == 0 else "W0B"
        wait("sync", "V0", cnt["V0"] - 1)
        dma("sync", wstage[par][:], d_wq[rc * 128:(rc + 1) * 128, :], sem=sw)
        wait_cur("vector", sw)
        wait("vector", "PEV", cnt["PEV"])
        op("vector", lambda e, par=par: e.tensor_copy(out=wcast[:], in_=wstage[par][:]))
        drain("vector", sem="V0")
        wait_cur("tensor", "V0")
        wait_cur("scalar", "TEV")
        for g in range(8):
            bnk = g % 2
            wait("tensor", "PEV", cnt["PBK"] - 1)
            op("tensor", lambda e, g=g, bnk=bnk, hh=hh: e.matmul(
                ps_pj[bnk][:], c_T[:, hh * 128:(hh + 1) * 128],
                wcast[:, g * 512:(g + 1) * 512], start=True, stop=True), sem="PBK")
            wait_cur("scalar", "PBK")
            op("scalar", lambda e, g=g, bnk=bnk: e.copy(
                out=wcast2[:, g * 512:(g + 1) * 512], in_=ps_pj[bnk][:]))
            drain("scalar", sem="PEV")
        wait_cur("tensor", "PEV")
        transp32(wcast2, wqT, slice(rc * 128, (rc + 1) * 128), rc)
    flush("W0", "W0B")
    barrier()

    TEV0 = cnt["TEV"]
    TPS0 = cnt["TPS"]
    SSM0 = cnt["SSM"]
    VS0 = cnt["VS"]
    SSQ0 = cnt["SSQ"]
    VRS0 = cnt["VRS"]
    SXN0 = cnt["SXN"]
    VW0 = cnt["VW"]
    jb = jq[0]

    def p1():
        LDXc = {}
        pev_hist = []
        pbk = [cnt["PBK"]]

        def em_load(u):
            if not (0 <= u < NT):
                return
            sl = "LDX" if u % 2 == 0 else "LDXB"
            sh = "LDH" if u % 2 == 0 else "LDHB"
            if u >= 2:
                wait("sync", "VT", 2 * (jb + 4 * (u - 2) + 2))
            dma("sync", xst[u % 2][:, 0:2048],
                d_x[u * 128:(u + 1) * 128, 0:2048], sem=sl)
            if u >= 2:
                wait("sync", "VT", 2 * (jb + 4 * (u - 2) + 4))
            dma("sync", xst[u % 2][:, 2048:4096],
                d_x[u * 128:(u + 1) * 128, 2048:4096], sem=sh)
            LDXc[u] = (sl, cnt[sl], sh, cnt[sh])

        def em_stats(u):
            if not (0 <= u < NT):
                return
            p = u % 2
            wait("vector", LDXc[u][0], LDXc[u][1])
            wait("vector", "VS", VS0 + 3 * u)
            wait("vector", "SSM", SSM0 + max(0, u - 1))
            for i in range(4):
                op("vector", lambda e, p=p, i=i: e.bn_stats(
                    out=st_bn[:, i * 6:(i + 1) * 6],
                    in_=xst[p][:, i * 512:(i + 1) * 512]), sem="VSF")
            wait("vector", LDXc[u][2], LDXc[u][3])
            for i in range(4, 8):
                op("vector", lambda e, p=p, i=i: e.bn_stats(
                    out=st_bn[:, i * 6:(i + 1) * 6],
                    in_=xst[p][:, i * 512:(i + 1) * 512]), sem="VSF")
            op("vector", lambda e, p=p: e.reduce_max(
                out=st_xmax[:, p:p + 1], in_=xst[p][:], axis=AX), sem="VS")
            op("vector", lambda e, p=p: e.tensor_reduce(
                out=st_xmin[:, p:p + 1], in_=xst[p][:], axis=AX, op=A.min), sem="VS")
            selfwait("vector")
            op("vector", lambda e, p=p: e.bn_aggr(
                out=st_mv[:, 2 * p:2 * p + 2], in_=st_bn[:]), sem="VS")

        def em_smalls(u):
            if not (0 <= u < NT):
                return
            p = u % 2
            wait("scalar", "VS", VS0 + 3 * (u + 1))
            selfwait("scalar")
            op("scalar", lambda e, p=p: e.activation(
                out=st_varc[:, p:p + 1], in_=st_mv[:, 2 * p + 1:2 * p + 2],
                func=ACT.Relu), sem="SSF")
            selfwait("scalar")
            op("scalar", lambda e, p=p: e.activation(
                out=st_varc[:, p:p + 1], in_=st_varc[:, p:p + 1], func=ACT.Ln,
                bias=c_eps[:]), sem="SSF")
            selfwait("scalar")
            op("scalar", lambda e, p=p: e.activation(
                out=st_rstd[:, p:p + 1], in_=st_varc[:, p:p + 1], func=ACT.Exp,
                scale=-0.5), sem="SSF")
            selfwait("scalar")
            op("scalar", lambda e, p=p: e.mul(
                out=st_nrstd[:, p:p + 1], in_=st_rstd[:, p:p + 1], mul=-1.0),
                sem="SSF")
            selfwait("scalar")
            op("scalar", lambda e, p=p: e.activation(
                out=st_nmrs[:, p:p + 1], in_=st_mv[:, 2 * p:2 * p + 1],
                func=ACT.Identity, scale=st_nrstd[:, p:p + 1]), sem="SSF")
            selfwait("scalar")
            op("scalar", lambda e, p=p: e.mul(
                out=st_mrs[:, p:p + 1], in_=st_nmrs[:, p:p + 1], mul=-1.0), sem="SSF")
            op("scalar", lambda e, p=p: e.activation(
                out=st_a[:, p:p + 1], in_=st_xmax[:, p:p + 1], func=ACT.Identity,
                scale=st_rstd[:, p:p + 1], bias=st_nmrs[:, p:p + 1]), sem="SSF")
            selfwait("scalar")
            op("scalar", lambda e, p=p: e.activation(
                out=st_b[:, p:p + 1], in_=st_xmin[:, p:p + 1], func=ACT.Identity,
                scale=st_nrstd[:, p:p + 1], bias=st_mrs[:, p:p + 1]), sem="SSM")

        def em_tinies(u):
            p = u % 2
            wait("vector", "SSM", SSM0 + u + 1)
            op("vector", lambda e, p=p: e.tensor_max(
                out=st_amax[:, p:p + 1], in0=st_a[:, p:p + 1], in1=st_b[:, p:p + 1]),
                sem="VSF")
            selfwait("vector")
            op("vector", lambda e, p=p: e.tensor_scalar(
                out=st_sx[:, p:p + 1], in0=st_amax[:, p:p + 1],
                scalar1=1.0 / 127.0, scalar2=1e-8, op0=A.mult, op1=A.max), sem="VSF")
            selfwait("vector")
            op("vector", lambda e, p=p: e.reciprocal(
                out=st_rx[:, p:p + 1], in_=st_sx[:, p:p + 1]), sem="VSF")
            op("vector", lambda e, p=p, u=u: e.tensor_mul(
                out=sxw[:, u:u + 1], in0=st_sx[:, p:p + 1], in1=s_wcol[:]), sem="VSF")
            selfwait("vector")
            op("vector", lambda e, p=p: e.tensor_mul(
                out=st_rr[:, p:p + 1], in0=st_rstd[:, p:p + 1], in1=st_rx[:, p:p + 1]),
                sem="VSF")
            op("vector", lambda e, p=p: e.scalar_tensor_tensor(
                out=st_bb[:, p:p + 1], in0=st_nmrs[:, p:p + 1],
                scalar=st_rx[:, p:p + 1], in1=c_2565[:], op0=A.mult, op1=A.add),
                sem="VSF")

        def em_xn(u):
            p = u % 2
            wait("scalar", "TPS", TPS0 + 4 * u)
            selfwait("scalar")
            op("scalar", lambda e, p=p: e.activation(
                out=xn_tok[p][:], in_=xst[p][:], func=ACT.Identity,
                scale=st_rstd[:, p:p + 1], bias=st_nmrs[:, p:p + 1]), sem="SXN")

        def em_makew(u):
            p = u % 2
            wait("vector", "SXN", SXN0 + u + 1)
            selfwait("vector")
            op("vector", lambda e, p=p: e.tensor_scalar(
                out=xst[p][:], in0=xst[p][:], scalar1=st_rr[:, p:p + 1],
                scalar2=st_bb[:, p:p + 1], op0=A.mult, op1=A.add), sem="VW")

        def em_cmps_a(u):
            if not (0 <= u < NT):
                return
            p = u % 2
            js = [jb + 4 * u + q for q in range(4)]
            ch_C12(js[0], p, 0)
            ch_C12(js[1], p, 1)
            wait("vector", "DMO", 16 * u)
            ch_X1(js[0], xi_tok[0][:, 0:1024])
            ch_X1(js[1], xi_tok[0][:, 1024:2048])

        def em_cmps_b(u):
            if not (0 <= u < NT):
                return
            p = u % 2
            js = [jb + 4 * u + q for q in range(4)]
            ch_C12(js[2], p, 2)
            ch_C12(js[3], p, 3)
            ch_X1(js[2], xi_tok[0][:, 2048:3072])
            ch_X1(js[3], xi_tok[0][:, 3072:4096])

        evq = []

        def em_transp(u, g0, g1):
            p = u % 2
            m = u // 2
            wait("tensor", "SXN", SXN0 + u + 1)
            for g in range(g0, g1):
                gt = 4 * u + g
                bnk = gt % 2
                wait("tensor", "TEV", TEV0 + gt - 1)
                for jx in range(8):
                    f = g * 8 + jx
                    op("tensor", lambda e, p=p, f=f, bnk=bnk, jx=jx: e.matmul(
                        pb_t[bnk][:, jx * 128:(jx + 1) * 128],
                        xn_tok[p][:, f * 128:(f + 1) * 128], c_id[:],
                        start=(jx == 0), stop=(jx == 7), is_transpose=True),
                        sem=("TPS" if jx == 7 else None))

                def ev(g=g, gt=gt, bnk=bnk, half=u % 2, mp=m % 2, m=m):
                    wait("scalar", "TPS", TPS0 + gt + 1)
                    wait("scalar", "PRJ", max(0, m - 1))
                    op("scalar", lambda e: e.copy(
                        out=xn_fm2[mp][:, g * 8:(g + 1) * 8,
                                       half * 128:half * 128 + 128],
                        in_=pb_t[bnk][:]), sem="TEV")
                evq.append(ev)

        pev_cnt = {}

        def mk_evac(semn, fn, pre=None):
            if semn not in pev_cnt:
                pev_cnt[semn] = cnt[semn]
            pev_cnt[semn] += 1
            val = pev_cnt[semn]
            pbkv = pbk[0]
            pev_hist.append((semn, val))

            def th():
                if pre:
                    pre()
                wait("scalar", "PBK", pbkv)
                op("scalar", fn, sem=semn)
                assert cnt[semn] == val, (semn, cnt[semn], val)
            return th

        def em_proj_q(jm):
            jp = jm % 2
            wait("tensor", "TEV", TEV0 + 8 * (jm + 1))
            out = []
            for ch in range(4):
                b = pbk[0] % 6
                if len(pev_hist) >= 6:
                    wait("tensor", *pev_hist[-6])
                for k in range(KCH):
                    op("tensor", lambda e, ch=ch, k=k, b=b, jp=jp: e.matmul(
                        pb_f[b][:, 0:256], wqT[:, k, ch * 128:(ch + 1) * 128],
                        xn_fm2[jp][:, k, :], start=(k == 0), stop=(k == KCH - 1)),
                        sem=("PBK" if k == KCH - 1 else None))
                pbk[0] += 1
                pre = (lambda jm=jm: wait("scalar", "KVQ", 160 * jm)) if ch == 0 else None
                out.append(mk_evac("PEV", lambda e, ch=ch, b=b: e.activation(
                    out=q_sb[:, ch, :], in_=pb_f[b][:, 0:256], func=ACT.Identity,
                    bias=bqe[:, ch:ch + 1]), pre=pre))
            return out

        def em_proj_k(jm):
            jp = jm % 2
            out = []
            for ch in range(4):
                b = pbk[0] % 6
                if len(pev_hist) >= 6:
                    wait("tensor", *pev_hist[-6])
                for k in range(KCH):
                    op("tensor", lambda e, ch=ch, k=k, b=b, jp=jp: e.matmul(
                        pb_f[b][:, 0:256], wkT[:, k, ch * 128:(ch + 1) * 128],
                        xn_fm2[jp][:, k, :], start=(k == 0), stop=(k == KCH - 1)),
                        sem=("PBK" if k == KCH - 1 else None))
                pbk[0] += 1
                out.append(mk_evac("PEV", lambda e, ch=ch, b=b: e.activation(
                    out=k_sb[:, ch, :], in_=pb_f[b][:, 0:256], func=ACT.Identity,
                    bias=bkT[:, ch:ch + 1])))
            return out

        def em_proj_v(jm):
            jp = jm % 2
            out = []
            for tc in range(2):
                b = pbk[0] % 6
                if len(pev_hist) >= 6:
                    wait("tensor", *pev_hist[-6])
                for k in range(KCH):
                    op("tensor", lambda e, tc=tc, k=k, b=b, jp=jp: e.matmul(
                        pb_f[b][:], xn_fm2[jp][:, k, tc * 128:(tc + 1) * 128],
                        wvT[:, k, :], start=(k == 0), stop=False), sem=None)
                op("tensor", lambda e, tc=tc, b=b: e.matmul(
                    pb_f[b][:], c_onesrb[:], b_v_bf[:],
                    start=False, stop=True), sem="PBK")
                pbk[0] += 1
                out.append(mk_evac("PEVK", lambda e, tc=tc, b=b: e.copy(
                    out=v_sb[:, tc, :], in_=pb_f[b][:])))
            drain("tensor", sem="PRJ")
            PEV_AT[jm] = (pev_cnt.get("PEV", cnt["PEV"]),
                          pev_cnt.get("PEVK", cnt["PEVK"]))
            return out

        def em_kvq_stores(mm):
            wait("sync", "PEV", PEV_AT[mm][0])
            wait("sync", "PEVK", PEV_AT[mm][1])
            for ch in range(4):
                dma("sync", d_kscr[ch * 128:(ch + 1) * 128, mm * 256:(mm + 1) * 256],
                    k_sb[:, ch, :], sem="KVQ")
                dma("sync", d_qscr[ch * 128:(ch + 1) * 128, mm * 256:(mm + 1) * 256],
                    q_sb[:, ch, :], sem="KVQ")
            for tc in range(2):
                dma("sync", d_vscr[mm * 256 + tc * 128:mm * 256 + (tc + 1) * 128, :],
                    v_sb[:, tc, :], sem="KVQ")

        em_load(0)
        em_stats(0)
        em_smalls(0)

        for t in range(NT):
            p = t % 2
            m = t // 2
            if t % 2 == 0 and m >= 2:
                em_kvq_stores(m - 2)
            em_load(t + 1)
            if t >= 1:
                wait("sync", "GX", jb + 4 * (t - 1) + 4)
                dma("sync", d_xiscr[(t - 1) * 128:t * 128, :], xi_tok[0][:], sem="DMO")
            em_xn(t)
            qev = kev = vev = ()
            if t % 2 == 0 and m >= 1:
                qev = em_proj_q(m - 1)
            for f in qev:
                f()
            em_transp(t, 0, 2)
            if t % 2 == 0 and m >= 1:
                kev = em_proj_k(m - 1)
            em_transp(t, 2, 4)
            if t % 2 == 0 and m >= 1:
                vev = em_proj_v(m - 1)
            em_cmps_a(t - 1)
            em_tinies(t)
            em_makew(t)
            vw_now = cnt["VW"]
            js = [jb + 4 * t + q for q in range(4)]
            jp_ = [jb + 4 * (t - 1) + q for q in range(4)]
            ch_S1(js[0], p, 0, vw_now)
            evq.pop(0)()
            ch_S1(js[1], p, 1, vw_now)
            ch_S2(js[0])
            evq.pop(0)()
            ch_S2(js[1])
            for f in kev:
                f()
            evq.pop(0)()
            evq.pop(0)()
            if t >= 1:
                ch_X2(jp_[0], xi_tok[0][:, 0:1024])
                ch_X2(jp_[1], xi_tok[0][:, 1024:2048])
            ch_E(js[0], p, 0)
            ch_E(js[1], p, 1)
            em_cmps_b(t - 1)
            em_stats(t + 1)
            em_smalls(t + 1)
            ch_S1(js[2], p, 2, vw_now)
            ch_S1(js[3], p, 3, vw_now)
            ch_S2(js[2])
            ch_S2(js[3])
            for f in vev:
                f()
            if t >= 1:
                ch_X2(jp_[2], xi_tok[0][:, 2048:3072])
                ch_X2(jp_[3], xi_tok[0][:, 3072:4096])
            ch_E(js[2], p, 2)
            ch_E(js[3], p, 3)

        em_cmps_a(NT - 1)
        em_cmps_b(NT - 1)
        for q in range(4):
            ch_X2(jb + 4 * (NT - 1) + q, xi_tok[0][:, q * 1024:(q + 1) * 1024])
        wait("sync", "GX", jb + 4 * NT)
        dma("sync", d_xiscr[(NT - 1) * 128:NT * 128, :], xi_tok[0][:], sem="DMO")
        for f in em_proj_q(NMAC - 1):
            f()
        for f in em_proj_k(NMAC - 1):
            f()
        for f in em_proj_v(NMAC - 1):
            f()
        for mm in (NMAC - 2, NMAC - 1):
            em_kvq_stores(mm)

    PEV_AT = {}
    p1()
    flush("LDX", "LDXB", "DMO", "KVQ")
    barrier()

    nst = [xi2_tok[0], xi2_tok[1]]
    nsq = [at("nsq0", [128, DM], BF16, 114 * KB), at("nsq1", [128, DM], BF16, 122 * KB)]
    for scr, dstmax in ((d_qscr, qnmax), (d_kscr, knmax)):
        for h in range(2):
            for half in range(2):
                wait("sync", "SQN", cnt["SQN"])
                for rr_ in range(2):
                    dma("sync", nst[rr_][:],
                        scr[(2 * h + rr_) * 128:(2 * h + rr_ + 1) * 128,
                            half * DM:(half + 1) * DM], sem="KLD")
                wait_cur("scalar", "KLD")
                wait("scalar", "TNM", cnt["TNM"])
                selfwait("scalar")
                for rr_ in range(2):
                    op("scalar", lambda e, rr_=rr_: e.activation(
                        out=nsq[rr_][:], in_=nst[rr_][:], func=ACT.Square), sem="SQN")
                for cb in range(8):
                    bnk = cb % 2
                    wait("tensor", "SQN", cnt["SQN"])
                    wait("tensor", "NRM", max(0, cnt["NRM"] - 1))
                    for rr_ in range(2):
                        op("tensor", lambda e, rr_=rr_, cb=cb, bnk=bnk: e.matmul(
                            pb_f[bnk][0:1, 0:512], c_onesb[:],
                            nsq[rr_][:, cb * 512:(cb + 1) * 512],
                            start=(rr_ == 0), stop=(rr_ == 1)),
                            sem=("TNM" if rr_ == 1 else None))
                    wait("vector", "TNM", cnt["TNM"])
                    wait("vector", "NRM", cnt["NRM"])
                    op("vector", lambda e, bnk=bnk: e.reduce_max(
                        out=sm_all[0:1, 4:5], in_=pb_f[bnk][0:1, 0:512], axis=AX),
                        sem="VSF")
                    selfwait("vector")
                    op("vector", lambda e, dstmax=dstmax: e.tensor_max(
                        out=dstmax, in0=dstmax, in1=sm_all[0:1, 4:5]), sem="NRM")
    barrier()
    op("scalar", lambda e: e.activation(out=cc_q, in_=qnmax, func=ACT.Sqrt))
    op("scalar", lambda e: e.activation(out=cc_k, in_=knmax, func=ACT.Sqrt))
    drain("scalar", sem="CB1")
    wait_cur("vector", "CB1")
    op("vector", lambda e: e.tensor_mul(out=cc_c, in0=cc_q, in1=cc_k))
    drain("vector")
    op("vector", lambda e: e.tensor_scalar_mul(out=cc_c2, in0=cc_c, scalar1=-0.5))
    drain("vector", sem="CB2")
    wait_cur("tensor", "CB2")
    op("tensor", lambda e: e.matmul(ps_m0[:, 0:1], c_onesf[:], cc_c2,
                                    start=True, stop=True), sem="CB3")
    wait_cur("scalar", "CB3")
    op("scalar", lambda e: e.copy(out=eb_exp[:], in_=ps_m0[:, 0:1]))
    drain("scalar", sem="CB1")
    dma("sync", wiT[:], d_wiscr.rearrange("p (a b) -> p a b", a=KCH), sem="KLD")
    flush("KLD")
    barrier()

    v_re = d_vscr.rearrange("(mc p) f -> p mc f", p=128)

    glt = [0]
    ridx2 = [0]
    qk_i = [0]
    ffb = [0]

    def p2_head(hh):
        for ch in range(2):
            dma("sync", K_res[:, ch, :],
                d_kscr[hh * 256 + ch * 128:hh * 256 + (ch + 1) * 128, :], sem="KLD")
        dma("sync", V_res[:, :, 0:256], v_re[:, :, hh * 256:(hh + 1) * 256], sem="KLD")
        wait_cur("vector", "KLD")
        op("vector", lambda e: e.memset(V_res[:, :, 256:257], 1.0))
        drain("vector", sem="NRD")
        wait_cur("tensor", "NRD")
        wait_cur("scalar", "CB1")
        for lt in range(NLT):
            g = glt[0]
            wait("sync", "QF", g - 1)
            sq_ = "QLD" if lt % 2 == 0 else "QLDB"
            for ch in range(2):
                dma("sync", q2_sb[lt % 2][:, ch, :],
                    d_qscr[hh * 256 + ch * 128:hh * 256 + (ch + 1) * 128,
                           lt * 512:(lt + 1) * 512], sem=sq_)
            xi2c = {}
            if hh == 1:
                for lc in range(4):
                    row = lt * 512 + lc * 128
                    wait("sync", "VOUT", max(0, cnt["VOUT"] - 4 + lc + 1))
                    wait("sync", "TXF", max(0, 4 * (lt - 1) + lc + 3))
                    sx2 = "XI2" if lc % 2 == 0 else "XI2B"
                    sa2 = "A2" if lc % 2 == 0 else "A2B"
                    ss2 = "XSL" if lc % 2 == 0 else "XSLB"
                    dma("sync", xi2_tok[lc % 2][:], d_xiscr[row:row + 128, :], sem=sx2)
                    xi2c[lc] = (sx2, cnt[sx2])
                    dma("sync", a0_sb[lc][:], d_ascr[row:row + 128, :], sem=sa2)
                    dma("sync", xsl_sb[lc][:], d_xsl[row:row + 128, :], sem=ss2)

            def em_ffn(lc):
                wait("tensor", *xi2c[lc])
                for gg in range(4):
                    bnk = ridx2[0] % 2
                    if gg == 0:
                        wait("tensor", "FFV", ffb[0])
                    wait("tensor", "TEV", cnt["TEV"] - 1)
                    for j in range(8):
                        f = gg * 8 + j
                        op("tensor", lambda e, lc=lc, f=f, bnk=bnk, j=j: e.matmul(
                            pb_t[bnk][:, j * 128:(j + 1) * 128],
                            xi2_tok[lc % 2][:, f * 128:(f + 1) * 128], c_id[:],
                            start=(j == 0), stop=(j == 7), is_transpose=True),
                            sem=("TPS" if j == 7 else None))
                    wait("vector", "TPS", cnt["TPS"])
                    op("vector", lambda e, gg=gg, bnk=bnk, lc=lc: e.tensor_copy(
                        out=xi2_fm[:, gg * 8:(gg + 1) * 8, lc * 128:(lc + 1) * 128],
                        in_=pb_t[bnk][:]), sem="TEV")
                    ridx2[0] += 1
                drain("tensor", sem="TXF")
                wait("tensor", "TEV", cnt["TEV"])
                for k in range(KCH):
                    op("tensor", lambda e, lc=lc, k=k: e.matmul(
                        pb_t[0][:].bitcast(F32), xi2_fm[:, k, lc * 128:(lc + 1) * 128],
                        wiT[:, k, :], start=(k == 0), stop=(k == KCH - 1)),
                        sem=("FPB" if k == KCH - 1 else None))
                wait("vector", "FPB", cnt["FPB"])
                t_idx = lt * 4 + lc
                op("vector", lambda e, lc=lc, t_idx=t_idx: e.tensor_scalar_mul(
                    out=ffn_sb[lc][:], in0=pb_t[0][:].bitcast(F32),
                    scalar1=sxw[:, t_idx:t_idx + 1]), sem="FFV")
                ffb[0] += 1

            wait("tensor", sq_, cnt[sq_])
            sbk = [pb_f[0][:], pb_f[1][:]]
            if hh == 0:
                sbk += [pb_t[0][:].bitcast(F32), pb_t[1][:].bitcast(F32)]
            nb = len(sbk)
            LA = nb - 1

            def emit_qk(i, mc):
                b = i % nb
                wait("tensor", "ER", i - (nb - 1))
                for d in range(2):
                    op("tensor", lambda e, d=d, mc=mc, sb_=sbk[b], lt=lt: e.matmul(
                        sb_, K_res[:, d, mc * 128:(mc + 1) * 128],
                        q2_sb[lt % 2][:, d, :], start=(d == 0), stop=(d == 1)),
                        sem=("SR" if d == 1 else None))
                wait("scalar", "SR", i + 1)
                wait("scalar", "EF", i - 3)
                op("scalar", lambda e, sb_=sbk[b], e3=i % 4: e.activation(
                    out=expS[e3][:], in_=sb_, func=ACT.Exp,
                    bias=eb_exp[:]), sem="ER")

            def emit_pv(i, mc):
                e3 = i % 4
                wait("tensor", "ER", i + 1)
                for lc in range(4):
                    op("tensor", lambda e, lc=lc, e3=e3, mc=mc: e.matmul(
                        pvs(lc, 0, 257), expS[e3][:, lc * 128:(lc + 1) * 128],
                        V_res[:, mc, :], start=(mc == 0), stop=(mc == MCN - 1)),
                        sem=("EF" if lc == 3 else None))

            i0 = qk_i[0]
            qk_i[0] += MCN
            wait("tensor", "OF", 4 * g)
            for k in range(LA):
                emit_qk(i0 + k, k)
            for mc in range(LA, MCN):
                if hh == 1 and mc in (8, 24, 40, 56):
                    em_ffn(mc // 16)
                emit_qk(i0 + mc, mc)
                emit_pv(i0 + mc - LA, mc - LA)
            for k in range(MCN - LA, MCN):
                emit_pv(i0 + k, k)
            drain("tensor", sem="QF")
            for lc in range(4):
                wait("vector", "EF", 64 * (g + 1))
                wait("vector", "OF", cnt["OF"])
                op("vector", lambda e, lc=lc: e.reciprocal(
                    out=rec_sb[:], in_=pvs(lc, 256, 257)), sem="VSF")
                selfwait("vector")
                if hh == 0:
                    sad = "ASD" if lc % 2 == 0 else "ASDB"
                    wait("vector", sad, cnt[sad])
                    op("vector", lambda e, lc=lc: e.tensor_scalar_mul(
                        out=a0_sb[lc][:], in0=pvs(lc, 0, 256),
                        scalar1=rec_sb[:]), sem="OF")
                    wait("sync", "OF", cnt["OF"])
                    row = lt * 512 + lc * 128
                    dma("sync", d_ascr[row:row + 128, :], a0_sb[lc][:], sem=sad)
                else:
                    op("vector", lambda e, lc=lc: e.tensor_scalar_mul(
                        out=attn_f[lc][:, 256:512], in0=pvs(lc, 0, 256),
                        scalar1=rec_sb[:]), sem="OF")
                    wait("vector", "A2" if lc % 2 == 0 else "A2B",
                         cnt["A2" if lc % 2 == 0 else "A2B"])
                    wait("vector", "XSL" if lc % 2 == 0 else "XSLB",
                         cnt["XSL" if lc % 2 == 0 else "XSLB"])
                    op("vector", lambda e, lc=lc: e.tensor_copy(
                        out=attn_f[lc][:, 0:256], in_=a0_sb[lc][:]), sem="VSF")
                    selfwait("vector")
                    op("vector", lambda e, lc=lc: e.tensor_add(
                        out=tadd_sb[lc][:], in0=attn_f[lc][:], in1=ffn_sb[lc][:]),
                        sem="VSF")
                    selfwait("vector")
                    wait("vector", "OUT", 16 * max(0, cnt["VOUT"] - lc))
                    op("vector", lambda e, lc=lc: e.tensor_add(
                        out=out_sb[lc][:], in0=tadd_sb[lc][:], in1=xsl_sb[lc][:]),
                        sem="VOUT")
                    wait("sync", "VOUT", cnt["VOUT"])
                    row = lt * 512 + lc * 128
                    dma("sync", d_out[row:row + 128, :], out_sb[lc][:], sem="OUT")
            glt[0] += 1

    p2_head(0)
    flush("ASD", "ASDB", "QLD", "QLDB", "KLD")
    barrier()
    p2_head(1)
    wait_cur("sync", "OUT")
    wait_cur("sync", "ASD")
    if dev:
        op("vector", lambda e: e.memset(st_bn[:, 0:1], 0.0))
        drain("vector", sem="V0")
        wait_cur("sync", "V0")
        dma("sync", d_dbg[:, 0:64], sxw[:], sem="OUT", slow=True)
        dma("sync", d_dbg[:, 64:65], s_wcol[:], sem="OUT", slow=True)
        dma("sync", d_dbg[:, 65:66], r_wcol[:], sem="OUT", slow=True)
        dma("sync", d_dbg[:, 66:67], eb_exp[:], sem="OUT", slow=True)
        dma("sync", d_dbg[0:1, 67:68], qnmax, sem="OUT", slow=True)
        dma("sync", d_dbg[0:1, 68:69], knmax, sem="OUT", slow=True)
        wait_cur("sync", "OUT")

    for role, semn, val in deferred_waits:
        assert val <= cnt[semn], ("unsatisfiable wait", role, semn, val, cnt[semn])
    with nc.Block() as block:
        for role in ROLES:
            getattr(block, role)(lambda eng, role=role: [f(eng) for f in plan[role]])
    return nc


def make_in_maps(inputs, ncore=8):
    cst = host_constants()
    x = np.ascontiguousarray(np.asarray(inputs["L0_attn_x"], np.float32).reshape(L, DM))
    wq = np.asarray(inputs["L0_attn_Wq"], np.float32)
    wk = np.asarray(inputs["L0_attn_Wk"], np.float32)
    wv = np.asarray(inputs["L0_attn_Wv"], np.float32)
    bq = np.asarray(inputs["L0_attn_bq"], np.float32)
    bk = np.asarray(inputs["L0_attn_bk"], np.float32)
    bv = np.asarray(inputs["L0_attn_bv"], np.float32)
    bw = np.asarray(inputs["L0_attn_bias_weight"], np.float32)
    wf = np.ascontiguousarray(np.asarray(inputs["L0_ffn_weight"], np.float32))
    maps = []
    for c in range(ncore):
        sl = slice(FS * c, FS * (c + 1))
        m = dict(
            x=x,
            xsl=np.ascontiguousarray(x[:, sl]),
            wq=np.ascontiguousarray(wq[sl]),
            wk=np.ascontiguousarray(wk[sl]),
            wv=np.ascontiguousarray(wv[sl]),
            wf=wf,
            wfs=np.ascontiguousarray(wf[sl]),
            bq=np.ascontiguousarray(bq[sl]).reshape(1, FS),
            bk=np.ascontiguousarray(bk[sl]).reshape(1, FS),
            bv=np.ascontiguousarray(bv[sl]).reshape(1, FS),
            bwT=np.ascontiguousarray(bw[2 * c:2 * c + 2].T),
            **cst,
        )
        maps.append(m)
    return maps


def assemble(results):
    return np.concatenate([np.asarray(r["out"]) for r in results],
                          axis=1).reshape(1, L, DM)


from concourse.bass_utils import run_bass_kernel_spmd

_NC_CACHE = {}


def kernel(**inputs):
    maps = make_in_maps(inputs, ncore=8)
    if "nc" not in _NC_CACHE:
        _NC_CACHE["nc"] = build(dev=False)
    nc = _NC_CACHE["nc"]
    res = run_bass_kernel_spmd(nc, maps, core_ids=list(range(8)))
    return assemble(res.results)


# version 13
# speedup vs baseline: 1.7196x; 1.0195x over previous
import math
import numpy as np
import ml_dtypes
import concourse.bass as bass
import concourse.mybir as mybir

F32 = mybir.dt.float32
BF16 = mybir.dt.bfloat16
I32 = mybir.dt.int32
A = mybir.AluOpType
AX = mybir.AxisListType.X
ACT = mybir.ActivationFunctionType
BF = ml_dtypes.bfloat16

L = 8192
DM = 4096
FS = 512
HD = 256
NT = 64
NMAC = 32
NLT = 16
MCN = 64
KCH = 32
EPS = 1e-5
CLIP1 = 1.4999
SHIFT = 256.5
ROLES = ("sync", "vector", "scalar", "tensor", "gpsimd")
KB = 1024


def _build_tables():
    g = [1.0, 1.0, 1.0]
    n = 8
    masks = sorted(range(n), key=lambda m: (bin(m).count("1"), m))
    inv = {m: i for i, m in enumerate(masks)}
    C = np.zeros((n, n, n), dtype=np.float32)
    for i, a in enumerate(masks):
        for j, b in enumerate(masks):
            sw, at_ = (0, a >> 1)
            while at_:
                sw += bin(at_ & b).count("1")
                at_ >>= 1
            s = -1.0 if sw & 1 else 1.0
            for t in range(3):
                if a >> t & 1 and b >> t & 1:
                    s *= float(g[t])
            C[i, j, inv[a ^ b]] = s
    grades = np.array([bin(m).count("1") for m in masks])
    rev = np.where(grades * (grades - 1) // 2 % 2 == 1, -1.0, 1.0).astype(np.float32)
    met = np.array([float(np.prod([g[t] for t in range(3) if m >> t & 1] + [1.0]))
                    for m in masks], dtype=np.float32)
    return C, rev, met


def host_constants():
    C, rev, met = _build_tables()
    biv = C[:, :, [4, 5, 6]]
    scale = 1.0 / math.sqrt(HD)
    bivf = np.ascontiguousarray(biv.transpose(2, 0, 1).reshape(3, 64)).astype(np.float32)
    rconst = np.tile(np.repeat(rev, 8)[None, :], (2, 1)).astype(np.float32)
    dconst = np.tile((scale * np.diag(rev * met)).reshape(1, 64), (2, 1)).astype(np.float32)
    return dict(
        bivf=bivf, rconst=rconst, dconst=dconst,
        iden=np.eye(128).astype(BF),
        onesf=np.ones((1, 128), np.float32),
        onesb=np.ones((128, 1)).astype(BF),
        onesrb=np.ones((1, 128)).astype(BF),
        onescf=np.ones((128, 1), np.float32),
    )


def build(dev=False):
    nc = bass.Bass("TRN2", target_bir_lowering=False, debug=False)

    d_x = nc.dram_tensor("x", [L, DM], F32, kind="ExternalInput").ap()
    d_xsl = nc.dram_tensor("xsl", [L, FS], F32, kind="ExternalInput").ap()
    d_wq = nc.dram_tensor("wq", [FS, DM], F32, kind="ExternalInput").ap()
    d_wk = nc.dram_tensor("wk", [FS, DM], F32, kind="ExternalInput").ap()
    d_wv = nc.dram_tensor("wv", [FS, DM], F32, kind="ExternalInput").ap()
    d_wf = nc.dram_tensor("wf", [DM, DM], F32, kind="ExternalInput").ap()
    d_wfs = nc.dram_tensor("wfs", [FS, DM], F32, kind="ExternalInput").ap()
    d_bq = nc.dram_tensor("bq", [1, FS], F32, kind="ExternalInput").ap()
    d_bk = nc.dram_tensor("bk", [1, FS], F32, kind="ExternalInput").ap()
    d_bv = nc.dram_tensor("bv", [1, FS], F32, kind="ExternalInput").ap()
    d_bwT = nc.dram_tensor("bwT", [3, 2], F32, kind="ExternalInput").ap()
    d_bivf = nc.dram_tensor("bivf", [3, 64], F32, kind="ExternalInput").ap()
    d_rc = nc.dram_tensor("rconst", [2, 64], F32, kind="ExternalInput").ap()
    d_dc = nc.dram_tensor("dconst", [2, 64], F32, kind="ExternalInput").ap()
    d_id = nc.dram_tensor("iden", [128, 128], BF16, kind="ExternalInput").ap()
    d_onesf = nc.dram_tensor("onesf", [1, 128], F32, kind="ExternalInput").ap()
    d_onesb = nc.dram_tensor("onesb", [128, 1], BF16, kind="ExternalInput").ap()
    d_onesrb = nc.dram_tensor("onesrb", [1, 128], BF16, kind="ExternalInput").ap()
    d_onescf = nc.dram_tensor("onescf", [128, 1], F32, kind="ExternalInput").ap()
    d_out = nc.dram_tensor("out", [L, FS], F32, kind="ExternalOutput").ap()

    skind = "ExternalOutput" if dev else "Internal"
    d_kscr = nc.dram_tensor("k_scr", [FS, L], BF16, kind=skind).ap()
    d_vscr = nc.dram_tensor("v_scr", [L, FS], BF16, kind=skind).ap()
    d_qscr = nc.dram_tensor("q_scr", [FS, L], BF16, kind=skind).ap()
    d_xiscr = nc.dram_tensor("xi_scr", [L, DM], BF16, kind=skind).ap()
    d_wiscr = nc.dram_tensor("wi_scr", [128, KCH * FS], BF16, kind=skind).ap()
    d_ascr = nc.dram_tensor("a_scr", [L, HD], BF16, kind=skind).ap()
    if dev:
        d_dbg = nc.dram_tensor("dbg", [128, 80], F32, kind="ExternalOutput").ap()

    ARENA_PP = 207 * KB + 512
    arena = nc.sbuf_tensor("arena", [128, ARENA_PP], mybir.dt.uint8).__enter__()
    base = nc.lookup_mloc(arena).addr

    def at(name, shape, dtype, off):
        end = off + int(np.prod(shape[1:])) * mybir.dt.size(dtype)
        assert end <= ARENA_PP, (name, end)
        return nc.alloc_sbuf_tensor_at(name, shape, dtype, offset=base + off)

    wqT = at("wqT", [128, KCH, FS], BF16, 0)
    wkT = at("wkT", [128, KCH, FS], BF16, 32 * KB)
    wvT = at("wvT", [128, KCH, FS], BF16, 64 * KB)
    xst = [at(f"xst{i}", [128, DM], F32, 96 * KB + 16 * KB * i) for i in range(2)]
    xn_tok = [at("xntok", [128, DM], BF16, 128 * KB)] * 2
    xn_fm2 = [at(f"xnfm{i}", [128, KCH, 256], BF16, 136 * KB + 16 * KB * i) for i in range(2)]
    xi_tok = [at("xitok", [128, DM], BF16, 168 * KB)] * 2
    RS = 176 * KB
    cB = [at(f"cB{i}", [128, 1024], mybir.dt.int16, RS + 2 * KB * i) for i in range(4)]
    cC = [at(f"cC{i}", [128, 1024], BF16, RS + 8 * KB + 2 * KB * i) for i in range(4)]
    cF = [at(f"cF{i}", [128, 1024], BF16, RS + 16 * KB + 2 * KB * i) for i in range(2)]
    k_sb = at("ksb", [128, 4, 256], BF16, 196 * KB)
    v_sb = at("vsb", [128, 2, FS], BF16, 198 * KB)
    q_sb = at("qsb", [128, 4, 256], BF16, 200 * KB)
    wstage = [xst[0], xst[1]]
    wcast = at("wcast", [128, DM], BF16, 128 * KB)
    wcast2 = at("wcast2", [128, DM], BF16, 136 * KB)
    wiT_b = at("wiTb", [128, KCH, FS], BF16, 0)
    wstA = at("wstA", [128, DM], F32, 136 * KB)
    wstB = at("wstB", [128, DM], F32, 152 * KB)
    wiT = at("wiT", [128, KCH, FS], BF16, 0)
    K_res = at("Kres", [128, 2, L], BF16, 32 * KB)
    V_res = at("Vres", [128, MCN, HD + 1], BF16, 64 * KB)
    xi2_tok = [at(f"xi2t{i}", [128, DM], BF16, 98 * KB + 8 * KB * i) for i in range(2)]
    xi2_fm = at("xi2fm", [128, KCH, FS], BF16, 114 * KB)
    q2_sb = [at(f"q2sb{i}", [128, 2, FS], BF16, 146 * KB + 2 * KB * i) for i in range(2)]
    expS = [at(f"expS{i}", [128, 512], BF16, 150 * KB + KB * i) for i in range(4)]
    attn_f = [at(f"attnf{i}", [128, FS], BF16, 166 * KB + KB * i) for i in range(4)]
    a0_sb = [at(f"a0sb{i}", [128, HD], BF16, 170 * KB + 512 * i) for i in range(4)]
    xsl_sb = [at(f"xslb{i}", [128, FS], F32, 172 * KB + 2 * KB * i) for i in range(4)]
    ffn_sb = [at(f"ffnsb{i}", [128, FS], BF16, 180 * KB + KB * i) for i in range(4)]
    tadd_sb = [at(f"tadd{i}", [128, FS], F32, 184 * KB + 2 * KB * i) for i in range(4)]
    out_sb = [at(f"outsb{i}", [128, FS], F32, 192 * KB + 2 * KB * i) for i in range(4)]

    import contextlib
    es = contextlib.ExitStack()
    _pk = [202 * KB]

    def sb(n, s, d):
        off = _pk[0]
        t = at(n, s, d, off)
        sz = int(np.prod(s[1:])) * mybir.dt.size(d)
        _pk[0] = off + ((sz + 63) // 64) * 64
        return t
    c_id = sb("c_id", [128, 128], BF16)
    c_onesf = sb("c_onesf", [1, 128], F32)
    c_onesb = sb("c_onesb", [128, 1], BF16)
    c_onesrb = sb("c_onesrb", [1, 128], BF16)
    b_v_bf = sb("b_v_bf", [1, FS], BF16)
    c_onescf = sb("c_onescf", [128, 1], F32)
    c_bivf = at("c_bivf", [3, 64], F32, RS + 6 * KB)
    c_rc = at("c_rc", [2, 64], F32, RS + 6 * KB + 256)
    c_dc = at("c_dc", [2, 64], F32, RS + 6 * KB + 512)
    c_bwT = at("c_bwT", [3, 2], F32, RS + 6 * KB + 768)
    c_T = sb("c_T", [128, 256], BF16)
    c_A = at("c_A", [2, 64], F32, RS + 7 * KB)
    c_Abf = at("c_Abf", [2, 64], BF16, RS + 7 * KB + 256)
    b_q = at("b_q", [1, FS], F32, RS)
    b_k = at("b_k", [1, FS], F32, RS + 2 * KB)
    b_v = at("b_v", [1, FS], F32, RS + 4 * KB)
    bkT = sb("bkT", [128, 4], F32)
    bqT = sb("bqT", [128, 4], F32)
    bqTb = sb("bqTb", [128, 4], BF16)
    bqe = sb("bqe", [128, 4], F32)
    st_all = sb("st_all", [128, 84], F32)
    _st = [0]

    def _stc(n):
        a, _st[0] = _st[0], _st[0] + n
        return st_all[:, a:a + n]

    st_bn = _stc(48)
    st_mv = _stc(4)
    st_varc = _stc(2)
    st_rstd = _stc(2)
    st_nrstd = _stc(2)
    st_nmrs = _stc(2)
    st_mrs = _stc(2)
    st_xmax = _stc(2)
    st_xmin = _stc(2)
    st_a = _stc(2)
    st_b = _stc(2)
    st_amax = _stc(2)
    st_sx = _stc(2)
    st_rx = _stc(2)
    st_rr = _stc(2)
    st_bb = _stc(2)
    st_bb1 = _stc(2)
    c_2565 = sb("c_2565", [128, 1], F32)
    c_m257 = sb("c_m257", [128, 1], F32)
    c_eps = sb("c_eps", [128, 1], F32)
    sxw = sb("sxw", [128, NT], F32)
    s_wcol = sb("s_wcol", [128, 1], F32)
    r_wcol = sb("r_wcol", [128, 1], F32)
    wfsum = sb("wfsum", [128, 1], F32)
    wfpart = sb("wfpart", [128, 1], F32)
    wfparts = sb("wfparts", [128, 32], F32)
    sm_all = sb("sm_all", [1, 16], F32)
    s_sw = sm_all[0:1, 0:1]
    s_rw = sm_all[0:1, 1:2]
    qnmax = sm_all[0:1, 2:3]
    knmax = sm_all[0:1, 3:4]
    nrm_red = sm_all[0:1, 4:6]
    cc_q = sm_all[0:1, 6:7]
    cc_k = sm_all[0:1, 7:8]
    cc_c = sm_all[0:1, 8:9]
    cc_c2 = sm_all[0:1, 9:10]
    eb_exp = sb("eb_exp", [128, 1], F32)
    rec_sb = sb("rec_sb", [128, 1], F32)

    pb_f = [es.enter_context(nc.psum_tensor(f"pbf{i}", [128, 512], F32)) for i in range(6)]
    pb_t = [es.enter_context(nc.psum_tensor(f"pbt{i}", [128, 1024], BF16)) for i in range(2)]
    ps_pj = pb_f[0:2]
    ps_nm = pb_f[2]
    ps_m0 = pb_f[3]

    def pvs(lc, a, b2):
        return pb_f[2 + lc][:, a:b2]

    sem_names = [
        "BAR", "C0", "T0", "V0", "S0", "G0", "W0", "W1",
        "LDX", "LDXB", "LDH", "LDHB", "W0B", "W1B", "WF0", "WF1", "WFV", "WFG", "DMOB", "XI2B", "A2B", "XSLB", "QLDB", "VS", "GS", "SSM", "SXN",
        "VW", "SC1", "SC2", "VSS", "GX", "DMO",
        "TPS", "TEV", "PRJ", "PBK", "PEV", "PEVK", "SQN", "TNM", "NRD", "KVQ",
        "KLD", "QLD", "QF", "SR", "ER", "EF", "OF",
        "A2", "XSL", "XI2", "TXF", "FPB", "FFV", "VOUT", "OUT", "ASD", "ASDB",
        "CB1", "CB2", "CB3", "SSQ", "VRS", "GE1", "VSF", "SSF", "GSF", "VT", "NRM",
    ]
    S = {n: es.enter_context(nc.semaphore(n)) for n in sem_names}

    plan = {r: [] for r in ROLES}
    cnt = {n: 0 for n in sem_names}

    def op(role, fn, sem=None, n=1):
        if sem is not None:
            cnt[sem] += n
            plan[role].append(lambda eng, fn=fn, s=S[sem], m=n: fn(eng).then_inc(s, m))
        else:
            plan[role].append(lambda eng, fn=fn: fn(eng))

    def dma(role, out, in_, sem=None, slow=False):
        op(role, lambda e, o=out, i=in_, sl=slow: e.dma_start(
            out=o, in_=i, allow_slow_non_contiguous=sl), sem=sem, n=16)

    deferred_waits = []

    def wait(role, sem, val):
        deferred_waits.append((role, sem, val))
        if val > 0:
            plan[role].append(lambda eng, s=S[sem], v=val: eng.wait_ge(s, v))

    def wait_cur(role, sem):
        wait(role, sem, cnt[sem])

    def drain(role, sem=None, n=1):
        op(role, lambda e: e.drain(), sem=sem, n=n)

    SELF = {"vector": "VSF", "scalar": "SSF", "gpsimd": "GSF"}

    def selfwait(role):
        wait(role, SELF[role], cnt[SELF[role]])

    def chop(role, fn, sem=None):
        selfwait(role)
        op(role, fn, sem=(sem if sem is not None else SELF[role]))

    nbar = [0]

    def flush(*sems, role="sync"):
        for s_ in sems:
            wait_cur(role, s_)

    def barrier():
        nbar[0] += 1
        k = nbar[0] * len(ROLES)
        cnt["BAR"] += len(ROLES)
        for r in ROLES:
            def f(eng, v=k):
                eng.drain().then_inc(S["BAR"], 1)
                eng.wait_ge(S["BAR"], v)
            plan[r].append(f)

    for dst, src in [(c_id, d_id), (c_onesf, d_onesf), (c_onesb, d_onesb),
                     (c_onescf, d_onescf), (c_bivf, d_bivf), (c_rc, d_rc),
                     (c_dc, d_dc), (c_bwT, d_bwT), (b_q, d_bq), (b_k, d_bk),
                     (b_v, d_bv), (c_onesrb, d_onesrb)]:
        dma("sync", dst[:], src, sem="C0")
    op("vector", lambda e: e.memset(c_2565[:], SHIFT))
    op("vector", lambda e: e.memset(c_m257[:], -257.0))
    op("vector", lambda e: e.memset(c_eps[:], EPS))
    op("vector", lambda e: e.memset(qnmax, 0.0))
    op("vector", lambda e: e.memset(knmax, 0.0))
    op("vector", lambda e: e.memset(wfsum[:], 0.0))
    op("vector", lambda e: e.memset(c_T[:], 0.0))
    flush("C0")
    barrier()

    for f in range(4):
        dma("sync", bkT[:, f:f + 1], b_k[0:1, f * 128:(f + 1) * 128], sem="C0")
        dma("sync", bqT[:, f:f + 1], b_q[0:1, f * 128:(f + 1) * 128], sem="C0")
    op("tensor", lambda e: e.matmul(ps_m0[0:2, 0:64], c_bwT[:], c_bivf[:],
                                    start=True, stop=True), sem="T0")
    wait_cur("vector", "T0")
    op("vector", lambda e: e.tensor_mul(out=c_A[:], in0=ps_m0[0:2, 0:64], in1=c_rc[:]))
    drain("vector")
    op("vector", lambda e: e.tensor_add(out=c_A[:], in0=c_A[:], in1=c_dc[:]))
    drain("vector")
    op("vector", lambda e: e.tensor_copy(out=c_Abf[:], in_=c_A[:]))
    wait_cur("vector", "C0")
    op("vector", lambda e: e.tensor_copy(out=bqTb[:], in_=bqT[:]))
    drain("vector", sem="V0")
    wait_cur("sync", "V0")
    for hh in range(2):
        for dd in range(16):
            dma("sync", c_T[8 * dd:8 * dd + 8, hh * 128 + 8 * dd:hh * 128 + 8 * dd + 8],
                c_Abf[hh:hh + 1, 0:64], sem="C0")
    op("vector", lambda e: e.tensor_copy(out=b_v_bf[:], in_=b_v[:]))
    drain("vector", sem="V0")
    wait_cur("tensor", "C0")
    wait_cur("tensor", "V0")
    for j in range(4):
        if j >= 2:
            wait("tensor", "S0", cnt["S0"] - 1)
        op("tensor", lambda e, j=j: e.matmul(
            ps_pj[j % 2][:, 0:1], c_T[:, (j // 2) * 128:(j // 2) * 128 + 128],
            bqTb[:, j:j + 1], start=True, stop=True), sem="T0")
        wait_cur("scalar", "T0")
        op("scalar", lambda e, j=j: e.copy(out=bqe[:, j:j + 1], in_=ps_pj[j % 2][:, 0:1]))
        drain("scalar", sem="S0")
    flush("C0")
    barrier()

    def transp32(src_bf, dst, rcslice, tag_rc):
        for g in range(4):
            bnk = g % 2
            wait("tensor", "TEV", cnt["TEV"] - 1)
            for j in range(8):
                f = g * 8 + j
                op("tensor", lambda e, f=f, bnk=bnk, j=j, s=src_bf: e.matmul(
                    pb_t[bnk][:, j * 128:(j + 1) * 128], s[:, f * 128:(f + 1) * 128],
                    c_id[:], start=(j == 0), stop=(j == 7), is_transpose=True),
                    sem=("TPS" if j == 7 else None))
            wait_cur("vector", "TPS")
            op("vector", lambda e, g=g, bnk=bnk, d=dst, rcs=rcslice: e.tensor_copy(
                out=d[:, g * 8:(g + 1) * 8, rcs], in_=pb_t[bnk][:]), sem="TEV")

    wkv_n = [0]

    def w_prep_one(dsrc, wT, rc):
        par = wkv_n[0] % 2
        wkv_n[0] += 1
        sw = "W0" if par == 0 else "W0B"
        wait("sync", "V0", cnt["V0"] - 1)
        dma("sync", wstage[par][:], dsrc[rc * 128:(rc + 1) * 128, :], sem=sw)
        wait_cur("vector", sw)
        wait("vector", "TEV", cnt["TEV"])
        op("vector", lambda e, par=par: e.tensor_copy(out=wcast[:], in_=wstage[par][:]))
        drain("vector", sem="V0")
        wait_cur("tensor", "V0")
        transp32(wcast, wT, slice(rc * 128, (rc + 1) * 128), rc)

    wfb = [wstA, wstB]
    wf_cons = {}
    wkv_iter = iter([(d_wk, wkT, 0), (d_wk, wkT, 1), (d_wk, wkT, 2), (d_wk, wkT, 3),
                     (d_wv, wvT, 0), (d_wv, wvT, 1), (d_wv, wvT, 2), (d_wv, wvT, 3)])

    def em_wf(ch):
        b = ch % 2
        sw = "WF0" if b == 0 else "WF1"
        if ch >= 2:
            wait("scalar", *wf_cons[ch - 2])
        dma("scalar", wfb[b][:], d_wf[ch * 128:(ch + 1) * 128, :], sem=sw)
        wait("vector", sw, cnt[sw])
        op("vector", lambda e, b=b, ch=ch: e.tensor_reduce(
            out=wfparts[:, ch:ch + 1], in_=wfb[b][:], axis=AX, op=A.add,
            apply_absolute_value=True), sem="WFV")
        wf_cons[ch] = ("WFV", cnt["WFV"])

    for ch in range(32):
        em_wf(ch)
        if ch % 4 == 3:
            src, wT, rc = next(wkv_iter)
            w_prep_one(src, wT, rc)
    flush("W0", "W0B")
    wait_cur("sync", "WF0")
    wait_cur("sync", "WF1")
    barrier()

    op("vector", lambda e: e.tensor_reduce(
        out=wfsum[:], in_=wfparts[:], axis=AX, op=A.add))
    drain("vector", sem="V0")
    wait_cur("tensor", "V0")
    op("tensor", lambda e: e.matmul(ps_m0[0:1, 0:1], c_onescf[:], wfsum[:],
                                    start=True, stop=True), sem="T0")
    wait_cur("vector", "T0")
    op("vector", lambda e: e.tensor_scalar_mul(out=s_sw, in0=ps_m0[0:1, 0:1],
                                               scalar1=1.0 / (DM * DM)))
    drain("vector")
    op("vector", lambda e: e.tensor_scalar_add(out=s_rw, in0=s_sw, scalar1=1e-8))
    drain("vector")
    op("vector", lambda e: e.reciprocal(out=s_rw, in_=s_rw))
    drain("vector", sem="V0")
    wait_cur("tensor", "V0")
    op("tensor", lambda e: e.matmul(ps_m0[:, 0:1], c_onesf[:], s_sw,
                                    start=True, stop=True))
    op("tensor", lambda e: e.matmul(pb_f[4][:, 0:1], c_onesf[:], s_rw,
                                    start=True, stop=True), sem="T0")
    wait_cur("scalar", "T0")
    op("scalar", lambda e: e.copy(out=s_wcol[:], in_=ps_m0[:, 0:1]))
    op("scalar", lambda e: e.copy(out=r_wcol[:], in_=pb_f[4][:, 0:1]))
    drain("scalar", sem="S0")
    flush("W0", "W0B")
    barrier()

    jq = [0]

    def ch_S1(j, p, q, vw):
        wait("scalar", "VW", vw)
        wait("scalar", "GX", j - 3)
        op("scalar", lambda e, p=p, q=q, j=j: e.copy(
            out=cB[j % 4][:], in_=xst[p][:, q * 1024:(q + 1) * 1024]), sem="SC1")

    def ch_S2(j):
        wait("scalar", "SC1", j + 1)
        wait("scalar", "VSS", j - 3)
        op("scalar", lambda e, j=j: e.activation(
            out=cC[j % 4][:], in_=cB[j % 4][:], func=ACT.Identity,
            bias=c_m257[:]), sem="SC2")

    def ch_E(j, p, q):
        wait("gpsimd", "SC2", j + 1)
        op("gpsimd", lambda e, p=p, q=q, j=j: e.tensor_tensor(
            out=xst[p][:, q * 1024:(q + 1) * 1024],
            in0=xst[p][:, q * 1024:(q + 1) * 1024],
            in1=cC[j % 4][:], op=A.subtract), sem="GE1")

    def ch_C12(j, p, q):
        wait("vector", "GE1", j + 1)
        wait("vector", "VSS", j - 1)
        op("vector", lambda e, p=p, q=q, j=j: e.tensor_single_scalar(
            out=cF[j % 2][:], in_=xst[p][:, q * 1024:(q + 1) * 1024],
            scalar=257.0, op=A.is_ge), sem="VT")
        op("vector", lambda e, p=p, q=q, j=j: e.tensor_single_scalar(
            out=cB[j % 4][:].bitcast(BF16), in_=xst[p][:, q * 1024:(q + 1) * 1024],
            scalar=258.0, op=A.is_ge), sem="VT")

    def ch_X1(j, dst_ap):
        wait("vector", "VT", 2 * (j + 1))
        op("vector", lambda e, j=j, d=dst_ap: e.tensor_tensor(
            out=d, in0=cC[j % 4][:], in1=cF[j % 2][:], op=A.add), sem="VSS")

    def ch_X2(j, dst_ap):
        wait("gpsimd", "VSS", j + 1)
        op("gpsimd", lambda e, j=j, d=dst_ap: e.tensor_tensor(
            out=d, in0=d, in1=cB[j % 4][:].bitcast(BF16), op=A.add), sem="GX")

    tev_hist = [cnt["TEV"]]
    for rc in range(4):
        par = rc % 2
        sw = "W1" if par == 0 else "W1B"
        if rc >= 2:
            wait("sync", "VT", 2 * (4 * (rc - 2) + 4))
        dma("sync", wstage[par][:], d_wfs[rc * 128:(rc + 1) * 128, :], sem=sw)
        wait("vector", sw, cnt[sw])
        selfwait("vector")
        op("vector", lambda e, par=par: e.tensor_scalar(
            out=wstage[par][:], in0=wstage[par][:], scalar1=r_wcol[:],
            scalar2=CLIP1, op0=A.mult, op1=A.min), sem="VSF")
        selfwait("vector")
        op("vector", lambda e, par=par: e.tensor_scalar(
            out=wstage[par][:], in0=wstage[par][:], scalar1=-CLIP1,
            scalar2=SHIFT, op0=A.max, op1=A.add), sem="VW")
        vw_now = cnt["VW"]
        for q in range(4):
            j = jq[0]
            jq[0] += 1
            ch_S1(j, par, q, vw_now)
            ch_S2(j)
            ch_E(j, par, q)
            ch_C12(j, par, q)
            if q == 0:
                wait("vector", "TEV", tev_hist[-1])
            ch_X1(j, wcast[:, q * 1024:(q + 1) * 1024])
            ch_X2(j, wcast[:, q * 1024:(q + 1) * 1024])
        wait_cur("tensor", "GX")
        transp32(wcast, wiT_b, slice(rc * 128, (rc + 1) * 128), rc)
        tev_hist.append(cnt["TEV"])
    wait_cur("sync", "TEV")
    dma("sync", d_wiscr.rearrange("p (a b) -> p a b", a=KCH), wiT_b[:], sem="W1")
    flush("W1", "W1B")
    barrier()

    for rc in range(4):
        hh = rc // 2
        par = rc % 2
        sw = "W0" if par == 0 else "W0B"
        wait("sync", "V0", cnt["V0"] - 1)
        dma("sync", wstage[par][:], d_wq[rc * 128:(rc + 1) * 128, :], sem=sw)
        wait_cur("vector", sw)
        wait("vector", "PEV", cnt["PEV"])
        op("vector", lambda e, par=par: e.tensor_copy(out=wcast[:], in_=wstage[par][:]))
        drain("vector", sem="V0")
        wait_cur("tensor", "V0")
        wait_cur("scalar", "TEV")
        for g in range(8):
            bnk = g % 2
            wait("tensor", "PEV", cnt["PBK"] - 1)
            op("tensor", lambda e, g=g, bnk=bnk, hh=hh: e.matmul(
                ps_pj[bnk][:], c_T[:, hh * 128:(hh + 1) * 128],
                wcast[:, g * 512:(g + 1) * 512], start=True, stop=True), sem="PBK")
            wait_cur("scalar", "PBK")
            op("scalar", lambda e, g=g, bnk=bnk: e.copy(
                out=wcast2[:, g * 512:(g + 1) * 512], in_=ps_pj[bnk][:]))
            drain("scalar", sem="PEV")
        wait_cur("tensor", "PEV")
        transp32(wcast2, wqT, slice(rc * 128, (rc + 1) * 128), rc)
    flush("W0", "W0B")
    barrier()

    TEV0 = cnt["TEV"]
    TPS0 = cnt["TPS"]
    SSM0 = cnt["SSM"]
    VS0 = cnt["VS"]
    SSQ0 = cnt["SSQ"]
    VRS0 = cnt["VRS"]
    SXN0 = cnt["SXN"]
    VW0 = cnt["VW"]
    jb = jq[0]

    def p1():
        LDXc = {}
        pev_hist = []
        pbk = [cnt["PBK"]]

        def em_load(u):
            if not (0 <= u < NT):
                return
            sl = "LDX" if u % 2 == 0 else "LDXB"
            sh = "LDH" if u % 2 == 0 else "LDHB"
            if u >= 2:
                wait("sync", "VT", 2 * (jb + 4 * (u - 2) + 2))
            dma("sync", xst[u % 2][:, 0:2048],
                d_x[u * 128:(u + 1) * 128, 0:2048], sem=sl)
            if u >= 2:
                wait("sync", "VT", 2 * (jb + 4 * (u - 2) + 4))
            dma("sync", xst[u % 2][:, 2048:4096],
                d_x[u * 128:(u + 1) * 128, 2048:4096], sem=sh)
            LDXc[u] = (sl, cnt[sl], sh, cnt[sh])

        def em_stats(u):
            if not (0 <= u < NT):
                return
            p = u % 2
            wait("vector", LDXc[u][0], LDXc[u][1])
            wait("vector", "VS", VS0 + 3 * u)
            wait("vector", "SSM", SSM0 + max(0, u - 1))
            for i in range(4):
                op("vector", lambda e, p=p, i=i: e.bn_stats(
                    out=st_bn[:, i * 6:(i + 1) * 6],
                    in_=xst[p][:, i * 512:(i + 1) * 512]), sem="VSF")
            wait("vector", LDXc[u][2], LDXc[u][3])
            for i in range(4, 8):
                op("vector", lambda e, p=p, i=i: e.bn_stats(
                    out=st_bn[:, i * 6:(i + 1) * 6],
                    in_=xst[p][:, i * 512:(i + 1) * 512]), sem="VSF")
            op("vector", lambda e, p=p: e.reduce_max(
                out=st_xmax[:, p:p + 1], in_=xst[p][:], axis=AX), sem="VS")
            op("vector", lambda e, p=p: e.tensor_reduce(
                out=st_xmin[:, p:p + 1], in_=xst[p][:], axis=AX, op=A.min), sem="VS")
            selfwait("vector")
            op("vector", lambda e, p=p: e.bn_aggr(
                out=st_mv[:, 2 * p:2 * p + 2], in_=st_bn[:]), sem="VS")

        def em_smalls(u):
            if not (0 <= u < NT):
                return
            p = u % 2
            wait("scalar", "VS", VS0 + 3 * (u + 1))
            selfwait("scalar")
            op("scalar", lambda e, p=p: e.activation(
                out=st_varc[:, p:p + 1], in_=st_mv[:, 2 * p + 1:2 * p + 2],
                func=ACT.Relu), sem="SSF")
            selfwait("scalar")
            op("scalar", lambda e, p=p: e.activation(
                out=st_varc[:, p:p + 1], in_=st_varc[:, p:p + 1], func=ACT.Ln,
                bias=c_eps[:]), sem="SSF")
            selfwait("scalar")
            op("scalar", lambda e, p=p: e.activation(
                out=st_rstd[:, p:p + 1], in_=st_varc[:, p:p + 1], func=ACT.Exp,
                scale=-0.5), sem="SSF")
            selfwait("scalar")
            op("scalar", lambda e, p=p: e.mul(
                out=st_nrstd[:, p:p + 1], in_=st_rstd[:, p:p + 1], mul=-1.0),
                sem="SSF")
            selfwait("scalar")
            op("scalar", lambda e, p=p: e.activation(
                out=st_nmrs[:, p:p + 1], in_=st_mv[:, 2 * p:2 * p + 1],
                func=ACT.Identity, scale=st_nrstd[:, p:p + 1]), sem="SSF")
            selfwait("scalar")
            op("scalar", lambda e, p=p: e.mul(
                out=st_mrs[:, p:p + 1], in_=st_nmrs[:, p:p + 1], mul=-1.0), sem="SSF")
            op("scalar", lambda e, p=p: e.activation(
                out=st_a[:, p:p + 1], in_=st_xmax[:, p:p + 1], func=ACT.Identity,
                scale=st_rstd[:, p:p + 1], bias=st_nmrs[:, p:p + 1]), sem="SSF")
            selfwait("scalar")
            op("scalar", lambda e, p=p: e.activation(
                out=st_b[:, p:p + 1], in_=st_xmin[:, p:p + 1], func=ACT.Identity,
                scale=st_nrstd[:, p:p + 1], bias=st_mrs[:, p:p + 1]), sem="SSM")

        def em_tinies(u):
            p = u % 2
            wait("vector", "SSM", SSM0 + u + 1)
            op("vector", lambda e, p=p: e.tensor_max(
                out=st_amax[:, p:p + 1], in0=st_a[:, p:p + 1], in1=st_b[:, p:p + 1]),
                sem="VSF")
            selfwait("vector")
            op("vector", lambda e, p=p: e.tensor_scalar(
                out=st_sx[:, p:p + 1], in0=st_amax[:, p:p + 1],
                scalar1=1.0 / 127.0, scalar2=1e-8, op0=A.mult, op1=A.max), sem="VSF")
            selfwait("vector")
            op("vector", lambda e, p=p: e.reciprocal(
                out=st_rx[:, p:p + 1], in_=st_sx[:, p:p + 1]), sem="VSF")
            op("vector", lambda e, p=p, u=u: e.tensor_mul(
                out=sxw[:, u:u + 1], in0=st_sx[:, p:p + 1], in1=s_wcol[:]), sem="VSF")
            selfwait("vector")
            op("vector", lambda e, p=p: e.tensor_mul(
                out=st_rr[:, p:p + 1], in0=st_rstd[:, p:p + 1], in1=st_rx[:, p:p + 1]),
                sem="VSF")
            op("vector", lambda e, p=p: e.scalar_tensor_tensor(
                out=st_bb[:, p:p + 1], in0=st_nmrs[:, p:p + 1],
                scalar=st_rx[:, p:p + 1], in1=c_2565[:], op0=A.mult, op1=A.add),
                sem="VSF")

        def em_xn(u):
            p = u % 2
            wait("scalar", "TPS", TPS0 + 4 * u)
            selfwait("scalar")
            op("scalar", lambda e, p=p: e.activation(
                out=xn_tok[p][:], in_=xst[p][:], func=ACT.Identity,
                scale=st_rstd[:, p:p + 1], bias=st_nmrs[:, p:p + 1]), sem="SXN")

        def em_makew(u):
            p = u % 2
            wait("vector", "SXN", SXN0 + u + 1)
            selfwait("vector")
            op("vector", lambda e, p=p: e.tensor_scalar(
                out=xst[p][:], in0=xst[p][:], scalar1=st_rr[:, p:p + 1],
                scalar2=st_bb[:, p:p + 1], op0=A.mult, op1=A.add), sem="VW")

        def em_cmps_a(u):
            if not (0 <= u < NT):
                return
            p = u % 2
            js = [jb + 4 * u + q for q in range(4)]
            ch_C12(js[0], p, 0)
            ch_C12(js[1], p, 1)
            wait("vector", "DMO", 16 * u)
            ch_X1(js[0], xi_tok[0][:, 0:1024])
            ch_X1(js[1], xi_tok[0][:, 1024:2048])

        def em_cmps_b(u):
            if not (0 <= u < NT):
                return
            p = u % 2
            js = [jb + 4 * u + q for q in range(4)]
            ch_C12(js[2], p, 2)
            ch_C12(js[3], p, 3)
            ch_X1(js[2], xi_tok[0][:, 2048:3072])
            ch_X1(js[3], xi_tok[0][:, 3072:4096])

        evq = []

        def em_transp(u, g0, g1):
            p = u % 2
            m = u // 2
            wait("tensor", "SXN", SXN0 + u + 1)
            for g in range(g0, g1):
                gt = 4 * u + g
                bnk = gt % 2
                wait("tensor", "TEV", TEV0 + gt - 1)
                for jx in range(8):
                    f = g * 8 + jx
                    op("tensor", lambda e, p=p, f=f, bnk=bnk, jx=jx: e.matmul(
                        pb_t[bnk][:, jx * 128:(jx + 1) * 128],
                        xn_tok[p][:, f * 128:(f + 1) * 128], c_id[:],
                        start=(jx == 0), stop=(jx == 7), is_transpose=True),
                        sem=("TPS" if jx == 7 else None))

                def ev(g=g, gt=gt, bnk=bnk, half=u % 2, mp=m % 2, m=m):
                    wait("scalar", "TPS", TPS0 + gt + 1)
                    wait("scalar", "PRJ", max(0, m - 1))
                    op("scalar", lambda e: e.copy(
                        out=xn_fm2[mp][:, g * 8:(g + 1) * 8,
                                       half * 128:half * 128 + 128],
                        in_=pb_t[bnk][:]), sem="TEV")
                evq.append(ev)

        pev_cnt = {}

        def mk_evac(semn, fn, pre=None):
            if semn not in pev_cnt:
                pev_cnt[semn] = cnt[semn]
            pev_cnt[semn] += 1
            val = pev_cnt[semn]
            pbkv = pbk[0]
            pev_hist.append((semn, val))

            def th():
                if pre:
                    pre()
                wait("scalar", "PBK", pbkv)
                op("scalar", fn, sem=semn)
                assert cnt[semn] == val, (semn, cnt[semn], val)
            return th

        def em_proj_q(jm):
            jp = jm % 2
            wait("tensor", "TEV", TEV0 + 8 * (jm + 1))
            out = []
            for ch in range(4):
                b = pbk[0] % 6
                if len(pev_hist) >= 6:
                    wait("tensor", *pev_hist[-6])
                for k in range(KCH):
                    op("tensor", lambda e, ch=ch, k=k, b=b, jp=jp: e.matmul(
                        pb_f[b][:, 0:256], wqT[:, k, ch * 128:(ch + 1) * 128],
                        xn_fm2[jp][:, k, :], start=(k == 0), stop=(k == KCH - 1)),
                        sem=("PBK" if k == KCH - 1 else None))
                pbk[0] += 1
                pre = (lambda jm=jm: wait("scalar", "KVQ", 160 * jm)) if ch == 0 else None
                out.append(mk_evac("PEV", lambda e, ch=ch, b=b: e.activation(
                    out=q_sb[:, ch, :], in_=pb_f[b][:, 0:256], func=ACT.Identity,
                    bias=bqe[:, ch:ch + 1]), pre=pre))
            return out

        def em_proj_k(jm):
            jp = jm % 2
            out = []
            for ch in range(4):
                b = pbk[0] % 6
                if len(pev_hist) >= 6:
                    wait("tensor", *pev_hist[-6])
                for k in range(KCH):
                    op("tensor", lambda e, ch=ch, k=k, b=b, jp=jp: e.matmul(
                        pb_f[b][:, 0:256], wkT[:, k, ch * 128:(ch + 1) * 128],
                        xn_fm2[jp][:, k, :], start=(k == 0), stop=(k == KCH - 1)),
                        sem=("PBK" if k == KCH - 1 else None))
                pbk[0] += 1
                out.append(mk_evac("PEV", lambda e, ch=ch, b=b: e.activation(
                    out=k_sb[:, ch, :], in_=pb_f[b][:, 0:256], func=ACT.Identity,
                    bias=bkT[:, ch:ch + 1])))
            return out

        def em_proj_v(jm):
            jp = jm % 2
            out = []
            for tc in range(2):
                b = pbk[0] % 6
                if len(pev_hist) >= 6:
                    wait("tensor", *pev_hist[-6])
                for k in range(KCH):
                    op("tensor", lambda e, tc=tc, k=k, b=b, jp=jp: e.matmul(
                        pb_f[b][:], xn_fm2[jp][:, k, tc * 128:(tc + 1) * 128],
                        wvT[:, k, :], start=(k == 0), stop=False), sem=None)
                op("tensor", lambda e, tc=tc, b=b: e.matmul(
                    pb_f[b][:], c_onesrb[:], b_v_bf[:],
                    start=False, stop=True), sem="PBK")
                pbk[0] += 1
                out.append(mk_evac("PEVK", lambda e, tc=tc, b=b: e.copy(
                    out=v_sb[:, tc, :], in_=pb_f[b][:])))
            drain("tensor", sem="PRJ")
            PEV_AT[jm] = (pev_cnt.get("PEV", cnt["PEV"]),
                          pev_cnt.get("PEVK", cnt["PEVK"]))
            return out

        def em_kvq_stores(mm):
            wait("sync", "PEV", PEV_AT[mm][0])
            wait("sync", "PEVK", PEV_AT[mm][1])
            for ch in range(4):
                dma("sync", d_kscr[ch * 128:(ch + 1) * 128, mm * 256:(mm + 1) * 256],
                    k_sb[:, ch, :], sem="KVQ")
                dma("sync", d_qscr[ch * 128:(ch + 1) * 128, mm * 256:(mm + 1) * 256],
                    q_sb[:, ch, :], sem="KVQ")
            for tc in range(2):
                dma("sync", d_vscr[mm * 256 + tc * 128:mm * 256 + (tc + 1) * 128, :],
                    v_sb[:, tc, :], sem="KVQ")

        em_load(0)
        em_stats(0)
        em_smalls(0)

        for t in range(NT):
            p = t % 2
            m = t // 2
            if t % 2 == 0 and m >= 2:
                em_kvq_stores(m - 2)
            em_load(t + 1)
            if t >= 1:
                wait("sync", "GX", jb + 4 * (t - 1) + 4)
                dma("sync", d_xiscr[(t - 1) * 128:t * 128, :], xi_tok[0][:], sem="DMO")
            em_xn(t)
            qev = kev = vev = ()
            if t % 2 == 0 and m >= 1:
                qev = em_proj_q(m - 1)
            for f in qev:
                f()
            em_transp(t, 0, 2)
            if t % 2 == 0 and m >= 1:
                kev = em_proj_k(m - 1)
            em_transp(t, 2, 4)
            if t % 2 == 0 and m >= 1:
                vev = em_proj_v(m - 1)
            em_cmps_a(t - 1)
            em_tinies(t)
            em_makew(t)
            vw_now = cnt["VW"]
            js = [jb + 4 * t + q for q in range(4)]
            jp_ = [jb + 4 * (t - 1) + q for q in range(4)]
            ch_S1(js[0], p, 0, vw_now)
            evq.pop(0)()
            ch_S1(js[1], p, 1, vw_now)
            ch_S2(js[0])
            evq.pop(0)()
            ch_S2(js[1])
            for f in kev:
                f()
            evq.pop(0)()
            evq.pop(0)()
            if t >= 1:
                ch_X2(jp_[0], xi_tok[0][:, 0:1024])
                ch_X2(jp_[1], xi_tok[0][:, 1024:2048])
            ch_E(js[0], p, 0)
            ch_E(js[1], p, 1)
            em_cmps_b(t - 1)
            em_stats(t + 1)
            em_smalls(t + 1)
            ch_S1(js[2], p, 2, vw_now)
            ch_S1(js[3], p, 3, vw_now)
            ch_S2(js[2])
            ch_S2(js[3])
            for f in vev:
                f()
            if t >= 1:
                ch_X2(jp_[2], xi_tok[0][:, 2048:3072])
                ch_X2(jp_[3], xi_tok[0][:, 3072:4096])
            ch_E(js[2], p, 2)
            ch_E(js[3], p, 3)

        em_cmps_a(NT - 1)
        em_cmps_b(NT - 1)
        for q in range(4):
            ch_X2(jb + 4 * (NT - 1) + q, xi_tok[0][:, q * 1024:(q + 1) * 1024])
        wait("sync", "GX", jb + 4 * NT)
        dma("sync", d_xiscr[(NT - 1) * 128:NT * 128, :], xi_tok[0][:], sem="DMO")
        for f in em_proj_q(NMAC - 1):
            f()
        for f in em_proj_k(NMAC - 1):
            f()
        for f in em_proj_v(NMAC - 1):
            f()
        for mm in (NMAC - 2, NMAC - 1):
            em_kvq_stores(mm)

    PEV_AT = {}
    p1()
    flush("LDX", "LDXB", "DMO", "KVQ")
    barrier()

    nst = [xi2_tok[0], xi2_tok[1]]
    nsq = [at("nsq0", [128, DM], BF16, 114 * KB), at("nsq1", [128, DM], BF16, 122 * KB)]
    for scr, dstmax in ((d_qscr, qnmax), (d_kscr, knmax)):
        for h in range(2):
            for half in range(2):
                wait("sync", "SQN", cnt["SQN"])
                for rr_ in range(2):
                    dma("sync", nst[rr_][:],
                        scr[(2 * h + rr_) * 128:(2 * h + rr_ + 1) * 128,
                            half * DM:(half + 1) * DM], sem="KLD")
                wait_cur("scalar", "KLD")
                wait("scalar", "TNM", cnt["TNM"])
                selfwait("scalar")
                for rr_ in range(2):
                    op("scalar", lambda e, rr_=rr_: e.activation(
                        out=nsq[rr_][:], in_=nst[rr_][:], func=ACT.Square), sem="SQN")
                for cb in range(8):
                    bnk = cb % 2
                    wait("tensor", "SQN", cnt["SQN"])
                    wait("tensor", "NRM", max(0, cnt["NRM"] - 1))
                    for rr_ in range(2):
                        op("tensor", lambda e, rr_=rr_, cb=cb, bnk=bnk: e.matmul(
                            pb_f[bnk][0:1, 0:512], c_onesb[:],
                            nsq[rr_][:, cb * 512:(cb + 1) * 512],
                            start=(rr_ == 0), stop=(rr_ == 1)),
                            sem=("TNM" if rr_ == 1 else None))
                    wait("vector", "TNM", cnt["TNM"])
                    wait("vector", "NRM", cnt["NRM"])
                    op("vector", lambda e, bnk=bnk: e.reduce_max(
                        out=sm_all[0:1, 4:5], in_=pb_f[bnk][0:1, 0:512], axis=AX),
                        sem="VSF")
                    selfwait("vector")
                    op("vector", lambda e, dstmax=dstmax: e.tensor_max(
                        out=dstmax, in0=dstmax, in1=sm_all[0:1, 4:5]), sem="NRM")
    barrier()
    op("scalar", lambda e: e.activation(out=cc_q, in_=qnmax, func=ACT.Sqrt))
    op("scalar", lambda e: e.activation(out=cc_k, in_=knmax, func=ACT.Sqrt))
    drain("scalar", sem="CB1")
    wait_cur("vector", "CB1")
    op("vector", lambda e: e.tensor_mul(out=cc_c, in0=cc_q, in1=cc_k))
    drain("vector")
    op("vector", lambda e: e.tensor_scalar_mul(out=cc_c2, in0=cc_c, scalar1=-0.5))
    drain("vector", sem="CB2")
    wait_cur("tensor", "CB2")
    op("tensor", lambda e: e.matmul(ps_m0[:, 0:1], c_onesf[:], cc_c2,
                                    start=True, stop=True), sem="CB3")
    wait_cur("scalar", "CB3")
    op("scalar", lambda e: e.copy(out=eb_exp[:], in_=ps_m0[:, 0:1]))
    drain("scalar", sem="CB1")
    dma("sync", wiT[:], d_wiscr.rearrange("p (a b) -> p a b", a=KCH), sem="KLD")
    flush("KLD")
    barrier()

    v_re = d_vscr.rearrange("(mc p) f -> p mc f", p=128)

    glt = [0]
    ridx2 = [0]
    qk_i = [0]
    ffb = [0]

    def p2_head(hh):
        for ch in range(2):
            dma("sync", K_res[:, ch, :],
                d_kscr[hh * 256 + ch * 128:hh * 256 + (ch + 1) * 128, :], sem="KLD")
        dma("sync", V_res[:, :, 0:256], v_re[:, :, hh * 256:(hh + 1) * 256], sem="KLD")
        wait_cur("vector", "KLD")
        op("vector", lambda e: e.memset(V_res[:, :, 256:257], 1.0))
        drain("vector", sem="NRD")
        wait_cur("tensor", "NRD")
        wait_cur("scalar", "CB1")
        for lt in range(NLT):
            g = glt[0]
            wait("sync", "QF", g - 1)
            sq_ = "QLD" if lt % 2 == 0 else "QLDB"
            for ch in range(2):
                dma("sync", q2_sb[lt % 2][:, ch, :],
                    d_qscr[hh * 256 + ch * 128:hh * 256 + (ch + 1) * 128,
                           lt * 512:(lt + 1) * 512], sem=sq_)
            xi2c = {}
            if hh == 1:
                for lc in range(4):
                    row = lt * 512 + lc * 128
                    wait("sync", "VOUT", max(0, cnt["VOUT"] - 4 + lc + 1))
                    wait("sync", "TXF", max(0, 4 * (lt - 1) + lc + 3))
                    sx2 = "XI2" if lc % 2 == 0 else "XI2B"
                    sa2 = "A2" if lc % 2 == 0 else "A2B"
                    ss2 = "XSL" if lc % 2 == 0 else "XSLB"
                    dma("sync", xi2_tok[lc % 2][:], d_xiscr[row:row + 128, :], sem=sx2)
                    xi2c[lc] = (sx2, cnt[sx2])
                    dma("sync", a0_sb[lc][:], d_ascr[row:row + 128, :], sem=sa2)
                    dma("sync", xsl_sb[lc][:], d_xsl[row:row + 128, :], sem=ss2)

            def em_ffn(lc):
                wait("tensor", *xi2c[lc])
                for gg in range(4):
                    bnk = ridx2[0] % 2
                    if gg == 0:
                        wait("tensor", "FFV", ffb[0])
                    wait("tensor", "TEV", cnt["TEV"] - 1)
                    for j in range(8):
                        f = gg * 8 + j
                        op("tensor", lambda e, lc=lc, f=f, bnk=bnk, j=j: e.matmul(
                            pb_t[bnk][:, j * 128:(j + 1) * 128],
                            xi2_tok[lc % 2][:, f * 128:(f + 1) * 128], c_id[:],
                            start=(j == 0), stop=(j == 7), is_transpose=True),
                            sem=("TPS" if j == 7 else None))
                    wait("vector", "TPS", cnt["TPS"])
                    op("vector", lambda e, gg=gg, bnk=bnk, lc=lc: e.tensor_copy(
                        out=xi2_fm[:, gg * 8:(gg + 1) * 8, lc * 128:(lc + 1) * 128],
                        in_=pb_t[bnk][:]), sem="TEV")
                    ridx2[0] += 1
                drain("tensor", sem="TXF")
                wait("tensor", "TEV", cnt["TEV"])
                for k in range(KCH):
                    op("tensor", lambda e, lc=lc, k=k: e.matmul(
                        pb_t[0][:].bitcast(F32), xi2_fm[:, k, lc * 128:(lc + 1) * 128],
                        wiT[:, k, :], start=(k == 0), stop=(k == KCH - 1)),
                        sem=("FPB" if k == KCH - 1 else None))
                wait("vector", "FPB", cnt["FPB"])
                t_idx = lt * 4 + lc
                op("vector", lambda e, lc=lc, t_idx=t_idx: e.tensor_scalar_mul(
                    out=ffn_sb[lc][:], in0=pb_t[0][:].bitcast(F32),
                    scalar1=sxw[:, t_idx:t_idx + 1]), sem="FFV")
                ffb[0] += 1

            wait("tensor", sq_, cnt[sq_])
            sbk = [pb_f[0][:], pb_f[1][:]]
            if hh == 0:
                sbk += [pb_t[0][:].bitcast(F32), pb_t[1][:].bitcast(F32)]
            nb = len(sbk)
            LA = nb - 1

            def emit_qk(i, mc):
                b = i % nb
                wait("tensor", "ER", i - (nb - 1))
                for d in range(2):
                    op("tensor", lambda e, d=d, mc=mc, sb_=sbk[b], lt=lt: e.matmul(
                        sb_, K_res[:, d, mc * 128:(mc + 1) * 128],
                        q2_sb[lt % 2][:, d, :], start=(d == 0), stop=(d == 1)),
                        sem=("SR" if d == 1 else None))
                wait("scalar", "SR", i + 1)
                wait("scalar", "EF", i - 3)
                op("scalar", lambda e, sb_=sbk[b], e3=i % 4: e.activation(
                    out=expS[e3][:], in_=sb_, func=ACT.Exp,
                    bias=eb_exp[:]), sem="ER")

            def emit_pv(i, mc):
                e3 = i % 4
                wait("tensor", "ER", i + 1)
                for lc in range(4):
                    op("tensor", lambda e, lc=lc, e3=e3, mc=mc: e.matmul(
                        pvs(lc, 0, 257), expS[e3][:, lc * 128:(lc + 1) * 128],
                        V_res[:, mc, :], start=(mc == 0), stop=(mc == MCN - 1)),
                        sem=("EF" if lc == 3 else None))

            i0 = qk_i[0]
            qk_i[0] += MCN
            wait("tensor", "OF", 4 * g)
            for k in range(LA):
                emit_qk(i0 + k, k)
            for mc in range(LA, MCN):
                if hh == 1 and mc in (8, 24, 40, 56):
                    em_ffn(mc // 16)
                emit_qk(i0 + mc, mc)
                emit_pv(i0 + mc - LA, mc - LA)
            for k in range(MCN - LA, MCN):
                emit_pv(i0 + k, k)
            drain("tensor", sem="QF")
            for lc in range(4):
                wait("vector", "EF", 64 * (g + 1))
                wait("vector", "OF", cnt["OF"])
                op("vector", lambda e, lc=lc: e.reciprocal(
                    out=rec_sb[:], in_=pvs(lc, 256, 257)), sem="VSF")
                selfwait("vector")
                if hh == 0:
                    sad = "ASD" if lc % 2 == 0 else "ASDB"
                    wait("vector", sad, cnt[sad])
                    op("vector", lambda e, lc=lc: e.tensor_scalar_mul(
                        out=a0_sb[lc][:], in0=pvs(lc, 0, 256),
                        scalar1=rec_sb[:]), sem="OF")
                    wait("sync", "OF", cnt["OF"])
                    row = lt * 512 + lc * 128
                    dma("sync", d_ascr[row:row + 128, :], a0_sb[lc][:], sem=sad)
                else:
                    op("vector", lambda e, lc=lc: e.tensor_scalar_mul(
                        out=attn_f[lc][:, 256:512], in0=pvs(lc, 0, 256),
                        scalar1=rec_sb[:]), sem="OF")
                    wait("vector", "A2" if lc % 2 == 0 else "A2B",
                         cnt["A2" if lc % 2 == 0 else "A2B"])
                    wait("vector", "XSL" if lc % 2 == 0 else "XSLB",
                         cnt["XSL" if lc % 2 == 0 else "XSLB"])
                    op("vector", lambda e, lc=lc: e.tensor_copy(
                        out=attn_f[lc][:, 0:256], in_=a0_sb[lc][:]), sem="VSF")
                    selfwait("vector")
                    op("vector", lambda e, lc=lc: e.tensor_add(
                        out=tadd_sb[lc][:], in0=attn_f[lc][:], in1=ffn_sb[lc][:]),
                        sem="VSF")
                    selfwait("vector")
                    wait("vector", "OUT", 16 * max(0, cnt["VOUT"] - lc))
                    op("vector", lambda e, lc=lc: e.tensor_add(
                        out=out_sb[lc][:], in0=tadd_sb[lc][:], in1=xsl_sb[lc][:]),
                        sem="VOUT")
                    wait("sync", "VOUT", cnt["VOUT"])
                    row = lt * 512 + lc * 128
                    dma("sync", d_out[row:row + 128, :], out_sb[lc][:], sem="OUT")
            glt[0] += 1

    p2_head(0)
    flush("ASD", "ASDB", "QLD", "QLDB", "KLD")
    barrier()
    p2_head(1)
    wait_cur("sync", "OUT")
    wait_cur("sync", "ASD")
    if dev:
        op("vector", lambda e: e.memset(st_bn[:, 0:1], 0.0))
        drain("vector", sem="V0")
        wait_cur("sync", "V0")
        dma("sync", d_dbg[:, 0:64], sxw[:], sem="OUT", slow=True)
        dma("sync", d_dbg[:, 64:65], s_wcol[:], sem="OUT", slow=True)
        dma("sync", d_dbg[:, 65:66], r_wcol[:], sem="OUT", slow=True)
        dma("sync", d_dbg[:, 66:67], eb_exp[:], sem="OUT", slow=True)
        dma("sync", d_dbg[0:1, 67:68], qnmax, sem="OUT", slow=True)
        dma("sync", d_dbg[0:1, 68:69], knmax, sem="OUT", slow=True)
        wait_cur("sync", "OUT")

    for role, semn, val in deferred_waits:
        assert val <= cnt[semn], ("unsatisfiable wait", role, semn, val, cnt[semn])
    with nc.Block() as block:
        for role in ROLES:
            getattr(block, role)(lambda eng, role=role: [f(eng) for f in plan[role]])
    return nc


def make_in_maps(inputs, ncore=8):
    cst = host_constants()
    x = np.ascontiguousarray(np.asarray(inputs["L0_attn_x"], np.float32).reshape(L, DM))
    wq = np.asarray(inputs["L0_attn_Wq"], np.float32)
    wk = np.asarray(inputs["L0_attn_Wk"], np.float32)
    wv = np.asarray(inputs["L0_attn_Wv"], np.float32)
    bq = np.asarray(inputs["L0_attn_bq"], np.float32)
    bk = np.asarray(inputs["L0_attn_bk"], np.float32)
    bv = np.asarray(inputs["L0_attn_bv"], np.float32)
    bw = np.asarray(inputs["L0_attn_bias_weight"], np.float32)
    wf = np.ascontiguousarray(np.asarray(inputs["L0_ffn_weight"], np.float32))
    maps = []
    for c in range(ncore):
        sl = slice(FS * c, FS * (c + 1))
        m = dict(
            x=x,
            xsl=np.ascontiguousarray(x[:, sl]),
            wq=np.ascontiguousarray(wq[sl]),
            wk=np.ascontiguousarray(wk[sl]),
            wv=np.ascontiguousarray(wv[sl]),
            wf=wf,
            wfs=np.ascontiguousarray(wf[sl]),
            bq=np.ascontiguousarray(bq[sl]).reshape(1, FS),
            bk=np.ascontiguousarray(bk[sl]).reshape(1, FS),
            bv=np.ascontiguousarray(bv[sl]).reshape(1, FS),
            bwT=np.ascontiguousarray(bw[2 * c:2 * c + 2].T),
            **cst,
        )
        maps.append(m)
    return maps


def assemble(results):
    return np.concatenate([np.asarray(r["out"]) for r in results],
                          axis=1).reshape(1, L, DM)


from concourse.bass_utils import run_bass_kernel_spmd

_NC_CACHE = {}


def kernel(**inputs):
    maps = make_in_maps(inputs, ncore=8)
    if "nc" not in _NC_CACHE:
        _NC_CACHE["nc"] = build(dev=False)
    nc = _NC_CACHE["nc"]
    res = run_bass_kernel_spmd(nc, maps, core_ids=list(range(8)))
    return assemble(res.results)
